# version 05
# baseline (speedup 1.0000x reference)
import contextlib
import numpy as np
import ml_dtypes
import concourse.bass as bass
import concourse.mybir as mybir
from concourse.bass_utils import run_bass_kernel_spmd

F32 = mybir.dt.float32
BF16 = mybir.dt.bfloat16
U8 = mybir.dt.uint8
AF = mybir.ActivationFunctionType
ALU = mybir.AluOpType
AX = mybir.AxisListType

NCORES = 8
DBG_FFN = False
D = 2048
EPS = 1e-5


class SB:

    def __init__(self, nc, stack, names):
        self.nc = nc
        self.h = {}
        self.c = {}
        self.last = {}
        for n in names:
            self.h[n] = stack.enter_context(nc.semaphore(n))
            self.c[n] = 0

    def inc(self, inst, name, n=1):
        inst.then_inc(self.h[name], n)
        self.c[name] += n
        return inst

    def fence(self, nc):
        for eng in (nc.sync, nc.gpsimd, nc.vector, nc.scalar, nc.tensor):
            self.wait(eng, *self.c.keys())

    def wait(self, eng, *names):
        for nm in names:
            cnt = self.c[nm]
            key = (id(eng), nm)
            if cnt > 0 and self.last.get(key, -1) < cnt:
                eng.wait_ge(self.h[nm], cnt)
                self.last[key] = cnt


def mk_consts():
    ident = np.eye(128, dtype=ml_dtypes.bfloat16)
    masku8 = np.triu(np.ones((128, 128), np.uint8))
    lt1 = np.triu(np.ones((128, 128), np.float32))
    return {"c_ident": ident, "c_masku8": masku8, "c_lt1": lt1}


def emit_l0_attn(nc, sb, E, T, debug=False):
    OC = 512
    n_oc = T // OC
    NTT = OC // 128
    H = 16
    DK, DV = 64, 128
    KD = 1024
    QSCALE = float(np.log(DK ** (-0.5)))

    x_h, y_h = E["x"], E["y"]
    wq_h, wk_h, wv_h, wg_h = E["Wq"], E["Wk"], E["Wv"], E["Wg"]
    wgk1_h, wgk2b_h, normw_h, wo_h = E["Wgk1"], E["Wgk2b"], E["normw"], E["Wo"]
    ident_h, mask_h, lt1_h = E["c_ident"], E["c_masku8"], E["c_lt1"]
    with contextlib.ExitStack() as st:
        def sbuf(name, shape, dt):
            return st.enter_context(nc.sbuf_tensor("A0" + name, shape, dt)).ap()

        def psum(name, shape, dt=F32):
            return st.enter_context(nc.psum_tensor("A0" + name, shape, dt)).ap()

        identb = sbuf("identb", [128, 128], BF16)
        masku = sbuf("masku", [128, 128], U8)
        lt16 = sbuf("lt16", [128, 128], F32)
        normw = sbuf("normw_s", [128, 1], F32)
        onesb = sbuf("onesb", [128, 1], BF16)
        onesf = sbuf("onesf", [128, 1], F32)
        onesr = sbuf("onesr", [1, 128], F32)
        zerob = sbuf("zerob", [128, 128], BF16)
        cols = sbuf("cols", [128, 8], F32)
        kn05 = sbuf("kn05", [128, 1], F32)
        x_tok = sbuf("x_tok", [128, D], F32)
        sq_f = sbuf("sq_f", [128, D], F32)
        sq_s = sbuf("sq_s", [128, D], BF16)
        xlnT = sbuf("xlnT", [128, 16 * OC], BF16)
        qT = sbuf("qT", [128, 8 * OC], BF16)
        kT = sbuf("kT", [128, 8 * OC], BF16)
        gT = sbuf("gT", [128, 16 * OC], BF16)
        sgp = sbuf("sgp", [128, 2048], BF16)
        v_tok = sbuf("v_tok", [128, NTT * D], BF16)
        t1b = sbuf("t1b", [17, OC], BF16)
        z_tok = sbuf("z_tok", [128, KD], F32)
        sc2 = sbuf("sc2", [128, KD], F32)
        g_tok = sbuf("g_tok", [128, KD], F32)
        at_ = sbuf("at", [128, 8 * 128], F32)
        e_s = sbuf("e_s", [128, 128], F32)
        qtl = sbuf("qtl", [128, 8 * 128], BF16)
        ktl = sbuf("ktl", [128, 8 * 128], BF16)
        khT = sbuf("khT", [128, 8 * 128], BF16)
        kh_tok = sbuf("kh_tok", [128, 8 * 128], BF16)
        pbuf = sbuf("pbuf", [128, 128], BF16)
        stl = sbuf("stl", [128, 8 * 128], BF16)
        s_all = sbuf("s_all", [128, 8 * 128], F32)
        o_all = sbuf("o_all", [128, 2048], F32)
        rstd = sbuf("rstd", [1, 2048], F32)
        rb = sbuf("rb", [128, 512], F32)
        otg = sbuf("otg", [128, NTT * 2048], BF16)
        wbuf = sbuf("wbuf", [128, 16 * 512], BF16)
        xs = sbuf("xs", [128, 512], F32)
        wgk1b = sbuf("wgk1b", [128, 16 * 16], BF16)
        wgk2b = sbuf("wgk2b", [17, KD], BF16)

        p_proj = psum("p_proj", [128, 512], F32)
        p_tr = psum("p_tr", [128, 128], BF16)
        p_cum = psum("p_cum", [128, 128], F32)
        p_p = psum("p_p", [128, 128], F32)
        p_o = psum("p_o", [128, 128], F32)
        p_ds = psum("p_ds", [64, 128], F32)
        p_t1 = psum("p_t1", [16, OC], F32)
        p_row = psum("p_row", [1, 512], F32)

        sy, gp, ve, sc, te = nc.sync, nc.gpsimd, nc.vector, nc.scalar, nc.tensor

        sb.inc(sy.dma_start(out=identb, in_=ident_h), "DS", 16)
        sb.inc(sy.dma_start(out=masku, in_=mask_h), "DS", 16)
        sb.inc(sy.dma_start(out=lt16, in_=lt1_h), "DS", 16)
        sb.inc(sy.dma_start(out=normw, in_=normw_h), "DS", 16)
        sb.inc(gp.dma_start(out=wgk1b.rearrange("p (kt m) -> p kt m", m=16),
                            in_=wgk1_h.rearrange("(kt p) m -> p kt m", p=128)), "DG", 16)
        sb.inc(gp.dma_start(out=wgk2b, in_=wgk2b_h), "DG", 16)
        sb.inc(ve.memset(kn05, -0.5), "V")
        sb.inc(ve.memset(onesb, 1.0), "V")
        sb.inc(ve.memset(onesf, 1.0), "V")
        sb.inc(ve.memset(onesr, 1.0), "V")
        sb.inc(ve.memset(zerob, 0.0), "V")
        sb.inc(ve.memset(t1b, 1.0), "V")
        sb.inc(ve.memset(s_all, 0.0), "V")
        sb.wait(ve, "DS")
        sb.inc(ve.tensor_scalar_mul(lt16, lt16, 1.0 / 16.0), "V")
        for _e in (te, sc, gp, sy):
            sb.wait(_e, "DS", "DG")

        for oc in range(n_oc):
            t0 = oc * OC
            for tt in range(NTT):
                r0 = t0 + tt * 128
                sb.wait(sy, "S", "V", "T", "DS")
                sb.inc(sy.dma_start(out=x_tok, in_=x_h[r0:r0 + 128, :]), "DS", 16)
                sb.wait(sc, "DS")
                sb.inc(sc.activation(sq_f, x_tok, AF.Square), "S")
                sb.wait(ve, "DS")
                sb.inc(ve.tensor_reduce(cols[:, 1:2], x_tok, AX.X, ALU.add), "V")
                sb.wait(ve, "S")
                sb.inc(ve.tensor_reduce(cols[:, 0:1], sq_f, AX.X, ALU.add), "V")
                sb.wait(ve, "V")
                ve.tensor_scalar_mul(cols[:, 2:3], cols[:, 1:2], 1.0 / D)
                sb.inc(ve.tensor_scalar(cols[:, 3:4], cols[:, 0:1], 1.0 / D, EPS, ALU.mult, ALU.add), "V")
                sb.wait(ve, "V")
                sb.inc(ve.tensor_tensor(cols[:, 4:5], cols[:, 2:3], cols[:, 2:3], ALU.mult), "V")
                sb.wait(ve, "V")
                sb.inc(ve.tensor_tensor(cols[:, 5:6], cols[:, 3:4], cols[:, 4:5], ALU.subtract), "V")
                sb.wait(sc, "V")
                sc.activation(cols[:, 6:7], cols[:, 5:6], AF.Ln)
                sb.inc(sc.activation(cols[:, 7:8], cols[:, 6:7], AF.Exp, scale=kn05[:, 0:1]), "S")
                sb.wait(ve, "S")
                sb.inc(ve.tensor_scalar(sq_s, x_tok, cols[:, 2:3], cols[:, 7:8],
                                        ALU.subtract, ALU.mult), "V")
                if debug and oc == 0 and tt == 0:
                    sb.wait(sy, "V", "S", "T")
                    sb.inc(sy.dma_start(out=dbg["d_cols"], in_=cols), "DS", 16)
                    sb.inc(sy.dma_start(out=dbg["d_xtok"], in_=x_tok), "DS", 16)
                for j in range(16):
                    sb.wait(te, "V")
                    sb.inc(te.transpose(p_tr, sq_s[:, j * 128:(j + 1) * 128], identb), "T")
                    sb.wait(ve, "T")
                    sb.inc(ve.tensor_copy(xlnT[:, j * OC + tt * 128: j * OC + tt * 128 + 128], p_tr), "V")

            if debug and oc == 0:
                sb.wait(sy, "V", "S", "T")
                sb.inc(sy.dma_start(out=dbg["d_xlnT"], in_=xlnT), "DS", 16)
            def load_w(src_ap, m0):
                sb.wait(gp, "T")
                sb.inc(gp.dma_start(
                    out=wbuf.rearrange("p (kt m) -> p kt m", m=512),
                    in_=src_ap[:, m0:m0 + 512].rearrange("(kt p) m -> p kt m", p=128)), "DG", 16)

            def proj_fm(dst, dst_col0):
                for mt in range(4):
                    sb.wait(te, "DG", "V")
                    for kt in range(16):
                        te_i = te.matmul(p_proj[:, 0:512],
                                         wbuf[:, kt * 512 + mt * 128: kt * 512 + mt * 128 + 128],
                                         xlnT[:, kt * OC: kt * OC + 512],
                                         start=(kt == 0), stop=(kt == 15))
                    sb.inc(te_i, "T")
                    sb.wait(ve, "T")
                    sb.inc(ve.tensor_copy(
                        dst[:, dst_col0 + mt * OC: dst_col0 + mt * OC + 512],
                        p_proj[:, 0:512]), "V")

            for blk in range(2):
                load_w(wq_h, blk * 512)
                proj_fm(qT, blk * 4 * OC)
            for blk in range(2):
                load_w(wk_h, blk * 512)
                proj_fm(kT, blk * 4 * OC)
            for blk in range(4):
                load_w(wv_h, blk * 512)
                for tt in range(NTT):
                    sb.wait(te, "DG", "V")
                    for kt in range(16):
                        te_i = te.matmul(
                            p_proj[:, 0:512],
                            xlnT[:, kt * OC + tt * 128: kt * OC + tt * 128 + 128],
                            wbuf[:, kt * 512: kt * 512 + 512],
                            start=(kt == 0), stop=(kt == 15))
                    sb.inc(te_i, "T")
                    sb.wait(ve, "T")
                    sb.inc(ve.tensor_copy(
                        v_tok[:, tt * D + blk * 512: tt * D + blk * 512 + 512],
                        p_proj[:, 0:512]), "V")
            for blk in range(4):
                load_w(wg_h, blk * 512)
                proj_fm(gT, blk * 4 * OC)
            sb.wait(te, "DG", "V")
            for kt in range(16):
                te_i = te.matmul(p_t1, wgk1b[:, kt * 16:(kt + 1) * 16], xlnT[:, kt * OC: kt * OC + OC],
                                 start=(kt == 0), stop=(kt == 15))
            sb.inc(te_i, "T")
            sb.wait(ve, "T")
            sb.inc(ve.tensor_copy(t1b[0:16, :], p_t1), "V")

            if debug and oc == 0:
                sb.wait(sy, "V", "S", "T")
                for nm, buf in (("d_qT", qT), ("d_kT", kT), ("d_v", v_tok), ("d_t1", t1b)):
                    sb.inc(sy.dma_start(out=dbg[nm], in_=buf), "DS", 16)
            for i4 in range(4):
                gsl = gT[:, i4 * 2048:(i4 + 1) * 2048]
                sb.wait(sc, "V")
                sb.inc(sc.activation(sgp, gsl, AF.Sigmoid), "S")
                sb.wait(ve, "S")
                sb.inc(ve.tensor_tensor(gsl, gsl, sgp, ALU.mult), "V")

            if debug and oc == 0:
                sb.wait(sy, "V", "S", "T")
                sb.inc(sy.dma_start(out=dbg["d_gT"], in_=gT), "DS", 16)
            for cc in range(NTT):
                ccol = cc * 128
                for half in range(2):
                    sb.wait(te, "V", "DG")
                    sb.inc(te.matmul(p_proj[:, 0:512], t1b[:, ccol:ccol + 128],
                                     wgk2b[:, half * 512: half * 512 + 512],
                                     start=True, stop=True), "T")
                    sb.wait(ve, "T")
                    sb.inc(ve.tensor_copy(z_tok[:, half * 512: half * 512 + 512], p_proj[:, 0:512]), "V")
                sb.wait(sc, "V")
                sc.activation(sc2, z_tok, AF.Abs)
                sc.activation(sc2, sc2, AF.Exp, scale=-1.0)
                sb.inc(sc.activation(sc2, sc2, AF.Ln, bias=1.0), "S")
                sb.wait(ve, "S")
                ve.tensor_scalar_min(g_tok, z_tok, 0.0)
                sb.inc(ve.tensor_tensor(g_tok, g_tok, sc2, ALU.subtract), "V")
                for j in range(8):
                    sb.wait(te, "V")
                    sb.inc(te.matmul(p_cum, g_tok[:, j * 128:(j + 1) * 128], lt16,
                                     start=True, stop=True), "T")
                    sb.wait(ve, "T")
                    sb.inc(ve.tensor_copy(at_[:, j * 128:(j + 1) * 128], p_cum), "V")
                sb.wait(ve, "V")
                for j in range(8):
                    lastc = at_[:, j * 128 + 127: j * 128 + 128]
                    ve.tensor_scalar(cols[:, 0:1], lastc, -0.5, QSCALE, ALU.mult, ALU.add)
                    sb.inc(ve.tensor_scalar_mul(cols[:, 1:2], lastc, 0.5), "V")
                    sb.wait(sc, "V")
                    sb.inc(sc.activation(e_s, at_[:, j * 128:(j + 1) * 128], AF.Exp,
                                         bias=cols[:, 0:1]), "S")
                    sb.wait(ve, "S")
                    sb.inc(ve.tensor_tensor(qtl[:, j * 128:(j + 1) * 128],
                                            qT[:, j * OC + ccol: j * OC + ccol + 128], e_s, ALU.mult), "V")
                    sb.wait(sc, "V")
                    sb.inc(sc.activation(e_s, at_[:, j * 128:(j + 1) * 128], AF.Exp,
                                         scale=-1.0, bias=cols[:, 1:2]), "S")
                    sb.wait(ve, "S")
                    sb.inc(ve.tensor_tensor(ktl[:, j * 128:(j + 1) * 128],
                                            kT[:, j * OC + ccol: j * OC + ccol + 128], e_s, ALU.mult), "V")
                    sb.wait(sc, "V")
                    sb.inc(sc.activation(cols[:, 2:3], cols[:, 1:2], AF.Exp), "S")
                    sb.inc(sc.activation(cols[:, 3:4], lastc, AF.Exp), "S")
                    sb.wait(ve, "S")
                    sb.inc(ve.tensor_scalar(khT[:, j * 128:(j + 1) * 128],
                                            ktl[:, j * 128:(j + 1) * 128],
                                            cols[:, 2:3], None, ALU.mult), "V")
                    sb.inc(ve.tensor_scalar(stl[:, j * 128:(j + 1) * 128],
                                            s_all[:, j * 128:(j + 1) * 128],
                                            cols[:, 2:3], None, ALU.mult), "V")
                    sb.wait(te, "V")
                    sb.inc(te.transpose(p_tr, khT[:, j * 128:(j + 1) * 128], identb), "T")
                    sb.wait(ve, "T")
                    sb.inc(ve.tensor_copy(kh_tok[:, j * 128:(j + 1) * 128], p_tr), "V")
                    for hh in range(2):
                        h = 2 * j + hh
                        pb = hh * 64
                        sb.wait(te, "V")
                        sb.inc(te.matmul(p_p, ktl[pb:pb + 64, j * 128:(j + 1) * 128],
                                         qtl[pb:pb + 64, j * 128:(j + 1) * 128],
                                         start=True, stop=True), "T")
                        sb.wait(ve, "T")
                        sb.inc(ve.select(pbuf, masku, p_p, zerob), "V")
                        sb.wait(te, "V")
                        te.matmul(p_o, v_tok[:, cc * D + h * 128: cc * D + h * 128 + 128],
                                  pbuf, start=True, stop=False)
                        sb.inc(te.matmul(p_o, stl[pb:pb + 64, j * 128:(j + 1) * 128],
                                         qtl[pb:pb + 64, j * 128:(j + 1) * 128],
                                         start=False, stop=True), "T")
                        sb.wait(ve, "T")
                        sb.inc(ve.tensor_copy(o_all[:, h * 128:(h + 1) * 128], p_o), "V")
                        sb.wait(te, "V")
                        sb.inc(te.matmul(p_ds, kh_tok[:, h * 64: h * 64 + 64],
                                         v_tok[:, cc * D + h * 128: cc * D + h * 128 + 128],
                                         start=True, stop=True), "T")
                        sb.wait(ve, "T")
                        sb.inc(ve.scalar_tensor_tensor(
                            s_all[pb:pb + 64, j * 128:(j + 1) * 128],
                            s_all[pb:pb + 64, j * 128:(j + 1) * 128],
                            cols[pb:pb + 64, 3:4], p_ds, ALU.mult, ALU.add), "V")
                if debug and oc == 0 and cc == 0:
                    sb.wait(sy, "V", "S", "T")
                    for nm, buf in (("d_z", z_tok), ("d_g", g_tok), ("d_at", at_),
                                    ("d_qtl", qtl), ("d_ktl", ktl), ("d_khT", khT),
                                    ("d_oall", o_all), ("d_sall", s_all)):
                        sb.inc(sy.dma_start(out=dbg[nm], in_=buf), "DS", 16)
                if debug and oc == 0 and cc == 3:
                    sb.wait(sy, "V", "S", "T")
                    sb.inc(sy.dma_start(out=dbg["d_oall3"], in_=o_all), "DS", 16)
                sb.wait(sc, "V")
                sb.inc(sc.activation(sgp, o_all, AF.Square), "S")
                for nn in range(4):
                    sb.wait(te, "S", "V")
                    sb.inc(te.matmul(p_row, onesb, sgp[:, nn * 512:(nn + 1) * 512],
                                     start=True, stop=True), "T")
                    sb.wait(ve, "T")
                    sb.inc(ve.tensor_copy(rstd[:, nn * 512:(nn + 1) * 512], p_row), "V")
                sb.wait(ve, "V")
                sb.inc(ve.tensor_scalar(rstd, rstd, 1.0 / 128.0, EPS, ALU.mult, ALU.add), "V")
                sb.wait(sc, "V")
                sc.activation(rstd, rstd, AF.Ln)
                sb.inc(sc.activation(rstd, rstd, AF.Exp, scale=kn05[0:1, 0:1]), "S")
                for nn in range(4):
                    sb.wait(te, "S", "V")
                    sb.inc(te.matmul(p_proj[:, 0:512], onesr, rstd[:, nn * 512:(nn + 1) * 512],
                                     start=True, stop=True), "T")
                    sb.wait(ve, "T")
                    sb.inc(ve.tensor_copy(rb, p_proj[:, 0:512]), "V")
                    o_sl = o_all[:, nn * 512:(nn + 1) * 512]
                    ve.tensor_tensor(o_sl, o_sl, rb, ALU.mult)
                    g_ap = gT.rearrange("p (h t) -> p h t", t=OC)[
                        :, nn * 4:(nn + 1) * 4, ccol:ccol + 128]
                    og_ap = otg.rearrange("p (c h t) -> p c h t", h=16, t=128)[
                        :, cc, nn * 4:(nn + 1) * 4, :]
                    sb.inc(ve.scalar_tensor_tensor(
                        og_ap, o_sl.rearrange("p (h t) -> p h t", t=128),
                        normw[:, 0:1], g_ap, ALU.mult, ALU.mult), "V")

            if debug and oc == 0:
                sb.wait(sy, "V", "S", "T")
                sb.inc(sy.dma_start(out=dbg["d_otg"], in_=otg), "DS", 16)
            for blk in range(4):
                load_w(wo_h, blk * 512)
                for tt in range(NTT):
                    r0 = t0 + tt * 128
                    sb.wait(te, "DG", "V")
                    for kt in range(16):
                        te_i = te.matmul(
                            p_proj[:, 0:512],
                            otg[:, tt * 2048 + kt * 128: tt * 2048 + kt * 128 + 128],
                            wbuf[:, kt * 512: kt * 512 + 512],
                            start=(kt == 0), stop=(kt == 15))
                    sb.inc(te_i, "T")
                    sb.wait(sy, "V", "DS")
                    sb.inc(sy.dma_start(out=xs, in_=x_h[r0:r0 + 128, blk * 512: blk * 512 + 512]), "DS", 16)
                    sb.wait(ve, "T", "DS")
                    sb.inc(ve.tensor_tensor(xs, p_proj[:, 0:512], xs, ALU.add), "V")
                    sb.wait(sy, "V")
                    sb.inc(sy.dma_start(out=y_h[r0:r0 + 128, blk * 512: blk * 512 + 512], in_=xs), "DS", 16)


def _prep_l0_attn_inputs(x_core, ins):
    cons = mk_consts()
    return dict(
        x=np.ascontiguousarray(x_core),
        Wq=ins["L0_attn_Wq"], Wk=ins["L0_attn_Wk"], Wv=ins["L0_attn_Wv"],
        Wg=ins["L0_attn_Wg"], Wgk1=ins["L0_attn_Wgk1"],
        Wgk2b=np.ascontiguousarray(np.concatenate(
            [ins["L0_attn_Wgk2"], ins["L0_attn_bgk2"][None, :]], axis=0)),
        normw=np.ascontiguousarray(ins["L0_attn_norm_w"][:, None]),
        Wo=ins["L0_attn_Wo"], **cons)


def emit_l1_attn(nc, sb, E, T, debug=False):
    OC = 512
    n_oc = T // OC
    NTT = OC // 128
    QSCALE = float(np.log(128.0 ** (-0.5)))

    x_h, y_h = E["x"], E["y"]
    wq_h, wf_h, wi_h, wo_h, gw_h = E["Wq"], E["Wf"], E["Wi"], E["Wo"], E["gw"]
    ident_h, mask_h, lt1_h = E["c_ident"], E["c_masku8"], E["c_lt1"]
    with contextlib.ExitStack() as st:
        def sbuf(name, shape, dt):
            return st.enter_context(nc.sbuf_tensor("A1" + name, shape, dt)).ap()

        def psum(name, shape, dt=F32):
            return st.enter_context(nc.psum_tensor("A1" + name, shape, dt)).ap()

        identb = sbuf("identb", [128, 128], BF16)
        masku = sbuf("masku", [128, 128], U8)
        lt1 = sbuf("lt1", [128, 128], F32)
        gw = sbuf("gw_s", [128, 16], F32)
        onesb = sbuf("onesb", [128, 1], BF16)
        onesr = sbuf("onesr", [1, 128], F32)
        zerob = sbuf("zerob", [128, 128], BF16)
        cols = sbuf("cols", [128, 8], F32)
        kn05 = sbuf("kn05", [128, 1], F32)
        x_tok = sbuf("x_tok", [128, D], F32)
        sq_f = sbuf("sq_f", [128, D], F32)
        sq_s = sbuf("sq_s", [128, D], BF16)
        xlnT = sbuf("xlnT", [128, 16 * OC], BF16)
        qT = sbuf("qT", [128, 16 * OC], BF16)
        sgp = sbuf("sgp", [128, 2048], BF16)
        f_bf = sbuf("f_bf", [128, NTT * D], BF16)
        k_tok = sbuf("k_tok", [128, NTT * D], BF16)
        v_tok = sbuf("v_tok", [128, NTT * D], BF16)
        fz = sbuf("fz", [128, D], F32)
        sc2 = sbuf("sc2", [128, D], F32)
        at_ = sbuf("at", [128, 16 * 128], F32)
        e_s = sbuf("e_s", [128, 128], F32)
        kTc = sbuf("kTc", [128, 16 * 128], BF16)
        qtl = sbuf("qtl", [128, 16 * 128], BF16)
        ktl = sbuf("ktl", [128, 16 * 128], BF16)
        khT = sbuf("khT", [128, 16 * 128], BF16)
        kh_tok = sbuf("kh_tok", [128, 16 * 128], BF16)
        pbuf = sbuf("pbuf", [128, 128], BF16)
        stl = sbuf("stl", [128, 16 * 128], BF16)
        s_all = sbuf("s_all", [128, 16 * 128], F32)
        o_all = sbuf("o_all", [128, 2048], F32)
        rtok = sbuf("rtok", [1, 128], F32)
        rtok4 = sbuf("rtok4", [1, 128], F32)
        rb = sbuf("rb", [128, 128], F32)
        otg = sbuf("otg", [128, NTT * 2048], BF16)
        wbuf = sbuf("wbuf", [128, 16 * 512], BF16)
        xs = sbuf("xs", [128, 512], F32)

        p_proj = psum("p_proj", [128, 512], F32)
        p_tr = psum("p_tr", [128, 128], BF16)
        p_cum = psum("p_cum", [128, 128], F32)
        p_p = psum("p_p", [128, 128], F32)
        p_o = psum("p_o", [128, 128], F32)
        p_ds = psum("p_ds", [128, 128], F32)
        p_row = psum("p_row", [1, 512], F32)
        p_bc = psum("p_bc", [128, 128], F32)

        sy, gp, ve, sc, te = nc.sync, nc.gpsimd, nc.vector, nc.scalar, nc.tensor

        sb.inc(sy.dma_start(out=identb, in_=ident_h), "DS", 16)
        sb.inc(sy.dma_start(out=masku, in_=mask_h), "DS", 16)
        sb.inc(sy.dma_start(out=lt1, in_=lt1_h), "DS", 16)
        sb.inc(sy.dma_start(out=gw, in_=gw_h), "DS", 16)
        sb.inc(ve.memset(kn05, -0.5), "V")
        sb.inc(ve.memset(onesb, 1.0), "V")
        sb.inc(ve.memset(onesr, 1.0), "V")
        sb.inc(ve.memset(zerob, 0.0), "V")
        sb.inc(ve.memset(s_all, 0.0), "V")
        sb.wait(ve, "DS")
        for _e in (te, sc, gp, sy):
            sb.wait(_e, "DS")

        def load_w(src_ap, m0):
            sb.wait(gp, "T")
            sb.inc(gp.dma_start(
                out=wbuf.rearrange("p (kt m) -> p kt m", m=512),
                in_=src_ap[:, m0:m0 + 512].rearrange("(kt p) m -> p kt m", p=128)), "DG", 16)

        for oc in range(n_oc):
            t0 = oc * OC
            for tt in range(NTT):
                r0 = t0 + tt * 128
                sb.wait(sy, "S", "V", "T", "DS")
                sb.inc(sy.dma_start(out=x_tok, in_=x_h[r0:r0 + 128, :]), "DS", 16)
                sb.wait(sc, "DS")
                sb.inc(sc.activation(sq_f, x_tok, AF.Square), "S")
                sb.wait(ve, "DS")
                sb.inc(ve.tensor_reduce(cols[:, 1:2], x_tok, AX.X, ALU.add), "V")
                sb.wait(ve, "S")
                sb.inc(ve.tensor_reduce(cols[:, 0:1], sq_f, AX.X, ALU.add), "V")
                sb.wait(ve, "V")
                ve.tensor_scalar_mul(cols[:, 2:3], cols[:, 1:2], 1.0 / D)
                sb.inc(ve.tensor_scalar(cols[:, 3:4], cols[:, 0:1], 1.0 / D, EPS, ALU.mult, ALU.add), "V")
                sb.wait(ve, "V")
                sb.inc(ve.tensor_tensor(cols[:, 4:5], cols[:, 2:3], cols[:, 2:3], ALU.mult), "V")
                sb.wait(ve, "V")
                sb.inc(ve.tensor_tensor(cols[:, 5:6], cols[:, 3:4], cols[:, 4:5], ALU.subtract), "V")
                sb.wait(sc, "V")
                sc.activation(cols[:, 6:7], cols[:, 5:6], AF.Ln)
                sb.inc(sc.activation(cols[:, 7:8], cols[:, 6:7], AF.Exp, scale=kn05[:, 0:1]), "S")
                sb.wait(ve, "S")
                sb.inc(ve.tensor_scalar(sq_s, x_tok, cols[:, 2:3], cols[:, 7:8],
                                        ALU.subtract, ALU.mult), "V")
                for j in range(16):
                    sb.wait(te, "V")
                    sb.inc(te.transpose(p_tr, sq_s[:, j * 128:(j + 1) * 128], identb), "T")
                    sb.wait(ve, "T")
                    sb.inc(ve.tensor_copy(xlnT[:, j * OC + tt * 128: j * OC + tt * 128 + 128], p_tr), "V")

            for blk in range(4):
                load_w(wq_h, blk * 512)
                for mt in range(4):
                    sb.wait(te, "DG", "V")
                    for kt in range(16):
                        te_i = te.matmul(p_proj[:, 0:512],
                                         wbuf[:, kt * 512 + mt * 128: kt * 512 + mt * 128 + 128],
                                         xlnT[:, kt * OC: kt * OC + 512],
                                         start=(kt == 0), stop=(kt == 15))
                    sb.inc(te_i, "T")
                    sb.wait(ve, "T")
                    sb.inc(ve.tensor_copy(qT[:, (blk * 4 + mt) * OC: (blk * 4 + mt) * OC + 512],
                                          p_proj[:, 0:512]), "V")
            for blk in range(4):
                load_w(wf_h, blk * 512)
                for tt in range(NTT):
                    sb.wait(te, "DG", "V")
                    for kt in range(16):
                        te_i = te.matmul(
                            p_proj[:, 0:512],
                            xlnT[:, kt * OC + tt * 128: kt * OC + tt * 128 + 128],
                            wbuf[:, kt * 512: kt * 512 + 512],
                            start=(kt == 0), stop=(kt == 15))
                    sb.inc(te_i, "T")
                    sb.wait(ve, "T")
                    f_sl = f_bf[:, tt * D + blk * 512: tt * D + blk * 512 + 512]
                    sb.inc(ve.tensor_copy(f_sl, p_proj[:, 0:512]), "V")
                    sb.wait(sc, "V")
                    sb.inc(sc.activation(sgp[:, 0:512], f_sl, AF.Sigmoid), "S")
                    sb.wait(ve, "S")
                    sb.inc(ve.tensor_scalar(k_tok[:, tt * D + blk * 512: tt * D + blk * 512 + 512],
                                            sgp[:, 0:512], -1.0, 1.0, ALU.mult, ALU.add), "V")
            for blk in range(4):
                load_w(wi_h, blk * 512)
                for tt in range(NTT):
                    sb.wait(te, "DG", "V")
                    for kt in range(16):
                        te_i = te.matmul(
                            p_proj[:, 0:512],
                            xlnT[:, kt * OC + tt * 128: kt * OC + tt * 128 + 128],
                            wbuf[:, kt * 512: kt * 512 + 512],
                            start=(kt == 0), stop=(kt == 15))
                    sb.inc(te_i, "T")
                    sb.wait(ve, "T")
                    sb.inc(ve.tensor_copy(v_tok[:, tt * D + blk * 512: tt * D + blk * 512 + 512],
                                          p_proj[:, 0:512]), "V")
            for i4 in range(4):
                qsl = qT[:, i4 * 2048:(i4 + 1) * 2048]
                sb.wait(sc, "V")
                sb.inc(sc.activation(sgp, qsl, AF.Sigmoid), "S")
                sb.wait(ve, "S")
                sb.inc(ve.tensor_tensor(qsl, qsl, sgp, ALU.mult), "V")

            for cc in range(NTT):
                f_c = f_bf[:, cc * D:(cc + 1) * D]
                sb.wait(sc, "V")
                sc.activation(sc2, f_c, AF.Abs)
                sc.activation(sc2, sc2, AF.Exp, scale=-1.0)
                sb.inc(sc.activation(sc2, sc2, AF.Ln, bias=1.0), "S")
                sb.wait(ve, "S")
                ve.tensor_scalar_min(fz, f_c, 0.0)
                sb.inc(ve.tensor_tensor(fz, fz, sc2, ALU.subtract), "V")
                if debug and oc == 0 and cc == 0:
                    sb.wait(sy, "V", "S", "T")
                    sb.inc(sy.dma_start(out=dbg["d_gl"], in_=fz), "DS", 16)
                for j in range(16):
                    sb.wait(te, "V")
                    sb.inc(te.matmul(p_cum, fz[:, j * 128:(j + 1) * 128], lt1,
                                     start=True, stop=True), "T")
                    sb.wait(ve, "T")
                    sb.inc(ve.tensor_copy(at_[:, j * 128:(j + 1) * 128], p_cum), "V")
                for j in range(16):
                    sb.wait(te, "V")
                    sb.inc(te.transpose(p_tr, k_tok[:, cc * D + j * 128: cc * D + j * 128 + 128],
                                        identb), "T")
                    sb.wait(ve, "T")
                    sb.inc(ve.tensor_copy(kTc[:, j * 128:(j + 1) * 128], p_tr), "V")
                sb.wait(ve, "V")
                for j in range(16):
                    h = j
                    lastc = at_[:, j * 128 + 127: j * 128 + 128]
                    ve.tensor_scalar(cols[:, 0:1], lastc, -0.5, QSCALE, ALU.mult, ALU.add)
                    sb.inc(ve.tensor_scalar_mul(cols[:, 1:2], lastc, 0.5), "V")
                    sb.wait(sc, "V")
                    sb.inc(sc.activation(e_s, at_[:, j * 128:(j + 1) * 128], AF.Exp,
                                         bias=cols[:, 0:1]), "S")
                    sb.wait(ve, "S")
                    sb.inc(ve.tensor_tensor(qtl[:, j * 128:(j + 1) * 128],
                                            qT[:, j * OC + cc * 128: j * OC + cc * 128 + 128],
                                            e_s, ALU.mult), "V")
                    sb.wait(sc, "V")
                    sb.inc(sc.activation(e_s, at_[:, j * 128:(j + 1) * 128], AF.Exp,
                                         scale=-1.0, bias=cols[:, 1:2]), "S")
                    sb.wait(ve, "S")
                    sb.inc(ve.tensor_tensor(ktl[:, j * 128:(j + 1) * 128],
                                            kTc[:, j * 128:(j + 1) * 128], e_s, ALU.mult), "V")
                    sb.wait(sc, "V")
                    sb.inc(sc.activation(cols[:, 2:3], cols[:, 1:2], AF.Exp), "S")
                    sb.inc(sc.activation(cols[:, 3:4], lastc, AF.Exp), "S")
                    sb.wait(ve, "S")
                    sb.inc(ve.tensor_scalar(khT[:, j * 128:(j + 1) * 128],
                                            ktl[:, j * 128:(j + 1) * 128],
                                            cols[:, 2:3], None, ALU.mult), "V")
                    sb.inc(ve.tensor_scalar(stl[:, j * 128:(j + 1) * 128],
                                            s_all[:, j * 128:(j + 1) * 128],
                                            cols[:, 2:3], None, ALU.mult), "V")
                    sb.wait(te, "V")
                    sb.inc(te.transpose(p_tr, khT[:, j * 128:(j + 1) * 128], identb), "T")
                    sb.wait(ve, "T")
                    sb.inc(ve.tensor_copy(kh_tok[:, j * 128:(j + 1) * 128], p_tr), "V")
                    sb.wait(te, "V")
                    sb.inc(te.matmul(p_p, ktl[:, j * 128:(j + 1) * 128],
                                     qtl[:, j * 128:(j + 1) * 128],
                                     start=True, stop=True), "T")
                    sb.wait(ve, "T")
                    sb.inc(ve.select(pbuf, masku, p_p, zerob), "V")
                    sb.wait(te, "V")
                    te.matmul(p_o, v_tok[:, cc * D + h * 128: cc * D + h * 128 + 128],
                              pbuf, start=True, stop=False)
                    sb.inc(te.matmul(p_o, stl[:, j * 128:(j + 1) * 128],
                                     qtl[:, j * 128:(j + 1) * 128],
                                     start=False, stop=True), "T")
                    sb.wait(ve, "T")
                    sb.inc(ve.tensor_copy(o_all[:, h * 128:(h + 1) * 128], p_o), "V")
                    sb.wait(te, "V")
                    sb.inc(te.matmul(p_ds, kh_tok[:, j * 128:(j + 1) * 128],
                                     v_tok[:, cc * D + h * 128: cc * D + h * 128 + 128],
                                     start=True, stop=True), "T")
                    sb.wait(ve, "T")
                    sb.inc(ve.scalar_tensor_tensor(
                        s_all[:, j * 128:(j + 1) * 128],
                        s_all[:, j * 128:(j + 1) * 128],
                        cols[:, 3:4], p_ds, ALU.mult, ALU.add), "V")
                if debug and oc == 0 and cc == 0:
                    sb.wait(sy, "V", "S", "T")
                    sb.inc(sy.dma_start(out=dbg["d_at"], in_=at_), "DS", 16)
                    sb.inc(sy.dma_start(out=dbg["d_oall"], in_=o_all), "DS", 16)
                    sb.inc(sy.dma_start(out=dbg["d_sall"], in_=s_all), "DS", 16)
                sb.wait(sc, "V")
                sb.inc(sc.activation(sgp, o_all, AF.Square), "S")
                sb.inc(ve.memset(rtok, 0.0), "V")
                for nn in range(4):
                    sb.wait(te, "S", "V")
                    sb.inc(te.matmul(p_row, onesb, sgp[:, nn * 512:(nn + 1) * 512],
                                     start=True, stop=True), "T")
                    sb.wait(ve, "T")
                    sb.inc(ve.tensor_reduce(
                        rtok4, p_row.rearrange("o (h t) -> o t h", h=4), AX.X, ALU.add), "V")
                    sb.wait(ve, "V")
                    sb.inc(ve.tensor_tensor(rtok, rtok, rtok4, ALU.add), "V")
                    sb.wait(ve, "V")
                sb.inc(ve.tensor_scalar(rtok, rtok, 1.0 / D, EPS, ALU.mult, ALU.add), "V")
                sb.wait(sc, "V")
                sc.activation(rtok, rtok, AF.Ln)
                sb.inc(sc.activation(rtok, rtok, AF.Exp, scale=kn05[0:1, 0:1]), "S")
                sb.wait(te, "S", "V")
                sb.inc(te.matmul(p_bc, onesr, rtok, start=True, stop=True), "T")
                sb.wait(ve, "T")
                sb.inc(ve.tensor_copy(rb, p_bc), "V")
                for h in range(16):
                    sb.inc(ve.scalar_tensor_tensor(
                        otg[:, cc * D + h * 128: cc * D + h * 128 + 128],
                        o_all[:, h * 128:(h + 1) * 128],
                        gw[:, h:h + 1], rb, ALU.mult, ALU.mult), "V")

            if debug and oc == 0:
                sb.wait(sy, "V", "S", "T")
                for nm, buf in (("d_qT", qT), ("d_fb", f_bf), ("d_k", k_tok),
                                ("d_v", v_tok), ("d_otg", otg)):
                    sb.inc(sy.dma_start(out=dbg[nm], in_=buf), "DS", 16)

            for blk in range(4):
                load_w(wo_h, blk * 512)
                for tt in range(NTT):
                    r0 = t0 + tt * 128
                    sb.wait(te, "DG", "V")
                    for kt in range(16):
                        te_i = te.matmul(
                            p_proj[:, 0:512],
                            otg[:, tt * 2048 + kt * 128: tt * 2048 + kt * 128 + 128],
                            wbuf[:, kt * 512: kt * 512 + 512],
                            start=(kt == 0), stop=(kt == 15))
                    sb.inc(te_i, "T")
                    sb.wait(sy, "V", "DS")
                    sb.inc(sy.dma_start(out=xs, in_=x_h[r0:r0 + 128, blk * 512: blk * 512 + 512]), "DS", 16)
                    sb.wait(ve, "T", "DS")
                    sb.inc(ve.tensor_tensor(xs, p_proj[:, 0:512], xs, ALU.add), "V")
                    sb.wait(sy, "V")
                    sb.inc(sy.dma_start(out=y_h[r0:r0 + 128, blk * 512: blk * 512 + 512], in_=xs), "DS", 16)


def _prep_l1_attn_inputs(x_core, ins):
    cons = mk_consts()
    return dict(
        x=np.ascontiguousarray(x_core),
        Wq=ins["L1_attn_Wq"], Wf=ins["L1_attn_Wf"], Wi=ins["L1_attn_Wi"],
        Wo=ins["L1_attn_Wo"],
        gw=np.ascontiguousarray(ins["L1_attn_g_weight"].reshape(16, 128).T),
        **cons)


def emit_l0_ffn(nc, sb, E, T, debug=False):
    INTER = 8192

    x_h, y_h = E["x"], E["y"]
    wup_h, wgt_h, wdn_h = E["W_up"], E["W_gate"], E["W_down"]
    suug_h, sudp_h = E["su_ug"], E["su_dp"]
    wsup_h, wsgt_h, wsdp_h = E["ws_up"], E["ws_gt"], E["ws_dp"]
    ident_h, h128_h, h64_h = E["c_ident"], E["c_h128"], E["c_h64"]
    wupc = nc.dram_tensor("wupc", [INTER, D], BF16, kind="Internal").ap()
    wgtc = nc.dram_tensor("wgtc", [INTER, D], BF16, kind="Internal").ap()
    wdnc = nc.dram_tensor("wdnc", [D, INTER], BF16, kind="Internal").ap()
    wupT = nc.dram_tensor("wupT", [D, INTER], BF16,
                          kind="ExternalOutput" if DBG_FFN else "Internal").ap()
    wgtT = nc.dram_tensor("wgtT", [D, INTER], BF16, kind="Internal").ap()
    wh1 = nc.dram_tensor("wh1", [64, 128, D], BF16, kind="Internal").ap()
    dbg_lhs = (nc.dram_tensor("dbg_lhs", [128, 16 * 512], BF16, kind="ExternalOutput").ap()
               if DBG_FFN else None)
    wdns = nc.dram_tensor("wdns", [INTER, D], BF16,
                          kind="ExternalOutput" if DBG_FFN else "Internal").ap()
    x2s = nc.dram_tensor("x2s", [INTER, T], BF16,
                         kind="ExternalOutput" if DBG_FFN else "Internal").ap()
    with contextlib.ExitStack() as st:
        def sbuf(name, shape, dt):
            return st.enter_context(nc.sbuf_tensor("F0" + name, shape, dt)).ap()

        identb = sbuf("identb", [128, 128], BF16)
        h128b = sbuf("h128b", [128, 128], BF16)
        h64b = sbuf("h64b", [64, 64], BF16)
        su_ug = sbuf("su_ug_s", [128, 16], F32)
        su_dp = sbuf("su_dp_s", [64, 128], F32)
        ws_dp = sbuf("ws_dp_s", [128, 16], F32)
        ws_upg = sbuf("ws_upg", [128, 128], F32)
        cols = sbuf("cols", [128, 8], F32)
        kn05 = sbuf("kn05", [128, 1], F32)

        sy, gp, ve, sc, te = nc.sync, nc.gpsimd, nc.vector, nc.scalar, nc.tensor

        sb.inc(sy.dma_start(out=identb, in_=ident_h), "DS", 16)
        sb.inc(sy.dma_start(out=h128b, in_=h128_h), "DS", 16)
        sb.inc(sy.dma_start(out=h64b, in_=h64_h), "DS", 16)
        sb.inc(sy.dma_start(out=su_ug, in_=suug_h), "DS", 16)
        sb.inc(sy.dma_start(out=su_dp, in_=sudp_h), "DS", 16)
        sb.inc(sy.dma_start(out=ws_dp, in_=wsdp_h), "DS", 16)
        sb.inc(sy.dma_start(out=ws_upg[:, 0:64], in_=wsup_h), "DS", 16)
        sb.inc(sy.dma_start(out=ws_upg[:, 64:128], in_=wsgt_h), "DS", 16)
        sb.inc(ve.memset(kn05, -0.5), "V")
        sb.wait(ve, "DS")
        sb.inc(ve.tensor_scalar_mul(su_ug, su_ug, float(2048.0 ** -0.5)), "V")
        sb.inc(ve.tensor_scalar_mul(ws_dp, ws_dp, float(8192.0 ** -0.5)), "V")
        sb.wait(ve, "V")
        for _e in (te, sc, gp, sy):
            sb.wait(_e, "DS")

        with nc.sbuf_tensor("F0cast_s", [128, INTER], BF16) as cast_g:
            cast_s = cast_g.ap()
            for m, src, dstd, nt in (("up", wup_h, wupc, 64), ("gt", wgt_h, wgtc, 64),
                                     ("dn", wdn_h, wdnc, 16)):
                wdt = D if m != "dn" else INTER
                for t_i in range(nt):
                    sb.wait(gp, "S", "DG")
                    sb.inc(gp.dma_start(out=cast_s[:, 0:wdt],
                                        in_=src[t_i * 128:(t_i + 1) * 128, :]), "DG", 16)
                    sb.wait(sc, "DG")
                    if m == "up":
                        s_ap = ws_upg[:, t_i: t_i + 1]
                    elif m == "gt":
                        s_ap = ws_upg[:, 64 + t_i: 65 + t_i]
                    else:
                        s_ap = ws_dp[:, t_i: t_i + 1]
                    sb.inc(sc.activation(cast_s[:, 0:wdt], cast_s[:, 0:wdt], AF.Copy,
                                         scale=s_ap), "S")
                    sb.wait(gp, "S")
                    sb.inc(gp.dma_start(out=dstd[t_i * 128:(t_i + 1) * 128, :],
                                        in_=cast_s[:, 0:wdt]), "DG", 16)
            sb.wait(sy, "DG")
            for srcd, dstd in ((wupc, wupT), (wgtc, wgtT)):
                for ht in range(16):
                    sb.inc(sy.dma_start_transpose(
                        cast_s[:, 0:INTER], srcd[:, ht * 128:(ht + 1) * 128]), "DT", 16)
                    sb.wait(sy, "DT")
                    sb.inc(sy.dma_start(out=dstd[ht * 128:(ht + 1) * 128, :],
                                        in_=cast_s[:, 0:INTER]), "DT", 16)
                    sb.wait(sy, "DT")

        sb.fence(nc)
        with (nc.sbuf_tensor("F0f_in", [128, D], BF16) as f_in_g,
              nc.sbuf_tensor("F0f_h1", [128, D], BF16) as f_h1_g,
              nc.psum_tensor("F0p_w", [128, 512], F32) as p_w_g,
              nc.psum_tensor("F0p_f", [64, 512], F32) as p_f_g):
            f_in, f_h1 = f_in_g.ap(), f_h1_g.ap()
            p_w, p_f = p_w_g.ap(), p_f_g.ap()
            for it in range(64):
                sb.wait(sy, "DT", "V", "DG", "T")
                sb.inc(sy.dma_start_transpose(f_in, wdnc[:, it * 128:(it + 1) * 128]), "DT", 16)
                sb.wait(te, "DT", "V")
                for nn2 in range(4):
                    sb.wait(te, "V")
                    sb.inc(te.matmul(p_w, h128b, f_in[:, nn2 * 512:(nn2 + 1) * 512],
                                     start=True, stop=True), "T")
                    sb.wait(ve, "T", "DS")
                    sb.inc(ve.tensor_copy(f_h1[:, nn2 * 512:(nn2 + 1) * 512], p_w), "V")
                sb.wait(sy, "V")
                sb.inc(sy.dma_start(out=wh1[it], in_=f_h1), "DS", 16)
            with (nc.sbuf_tensor("F0f_rd", [64, 8 * D], BF16) as f_rd_g,
                  nc.sbuf_tensor("F0f_o", [64, 8 * D], BF16) as f_o_g):
                f_rd, f_o = f_rd_g.ap(), f_o_g.ap()
                for bg in range(16):
                    sb.wait(sy, "V", "DS")
                    sb.inc(sy.dma_start(
                        out=f_rd.rearrange("p (b d) -> p b d", b=8),
                        in_=wh1[:, bg * 8:(bg + 1) * 8, :]), "DS", 16)
                    for bs in range(8):
                        b = bg * 8 + bs
                        for dc in range(4):
                            sb.wait(te, "DS", "V")
                            sb.inc(te.matmul(p_f, h64b,
                                             f_rd[0:64, bs * D + dc * 512: bs * D + dc * 512 + 512],
                                             start=True, stop=True), "T")
                            sb.wait(ve, "T", "DS")
                            sb.inc(ve.tensor_scalar(
                                f_o[0:64, bs * D + dc * 512: bs * D + dc * 512 + 512],
                                p_f, su_dp[:, b:b + 1], None, ALU.mult), "V")
                    sb.wait(sy, "V")
                    sb.inc(sy.dma_start(
                        out=wdns.rearrange("(a b) d -> a b d", b=128)[:, bg * 8:(bg + 1) * 8, :],
                        in_=f_o.rearrange("p (b d) -> p b d", b=8)[0:64]), "DS", 16)
            if debug:
                sb.wait(sy, "DS")
                sb.inc(sy.dma_start(out=dbg["d_wdns"], in_=wdns[0:128, :]), "DS", 16)

        sb.fence(nc)
        with (nc.sbuf_tensor("F0x1", [128, 16 * T], BF16) as x1_g,
              nc.psum_tensor("F0p_a", [128, 512], F32) as p_a_g,
              nc.psum_tensor("F0p_tr", [128, 128], BF16) as p_tr_g):
            x1, p_a, p_tr = x1_g.ap(), p_a_g.ap(), p_tr_g.ap()
            with (nc.sbuf_tensor("F0x_tok", [128, D], F32) as xtok_g,
                  nc.sbuf_tensor("F0sq_f", [128, D], BF16) as sqf_g,
                  nc.sbuf_tensor("F0sq_s", [128, D], BF16) as sqs_g,
                  nc.sbuf_tensor("F0xsuT", [128, 16 * 512], BF16) as xsuT_g,
                  nc.sbuf_tensor("F0bb", [128, 16 * 256], F32) as bb_g,
                  nc.sbuf_tensor("F0btmp", [128, 8 * 256], F32) as btmp_g):
                x_tok, sq_f, sq_s, xsuT, bb, btmp = (
                    xtok_g.ap(), sqf_g.ap(), sqs_g.ap(), xsuT_g.ap(), bb_g.ap(), btmp_g.ap())
                for oc in range(T // 512):
                    for tt in range(4):
                        r0 = oc * 512 + tt * 128
                        sb.wait(sy, "S", "V", "T", "DS")
                        sb.inc(sy.dma_start(out=x_tok, in_=x_h[r0:r0 + 128, :]), "DS", 16)
                        sb.wait(sc, "DS")
                        sb.inc(sc.activation(sq_f, x_tok, AF.Square), "S")
                        sb.wait(ve, "DS")
                        sb.inc(ve.tensor_reduce(cols[:, 1:2], x_tok, AX.X, ALU.add), "V")
                        sb.wait(ve, "S")
                        sb.inc(ve.tensor_reduce(cols[:, 0:1], sq_f, AX.X, ALU.add), "V")
                        sb.wait(ve, "V")
                        ve.tensor_scalar_mul(cols[:, 2:3], cols[:, 1:2], 1.0 / D)
                        sb.inc(ve.tensor_scalar(cols[:, 3:4], cols[:, 0:1], 1.0 / D, EPS,
                                                ALU.mult, ALU.add), "V")
                        sb.wait(ve, "V")
                        sb.inc(ve.tensor_tensor(cols[:, 4:5], cols[:, 2:3], cols[:, 2:3], ALU.mult), "V")
                        sb.wait(ve, "V")
                        sb.inc(ve.tensor_tensor(cols[:, 5:6], cols[:, 3:4], cols[:, 4:5], ALU.subtract), "V")
                        sb.wait(sc, "V")
                        sc.activation(cols[:, 6:7], cols[:, 5:6], AF.Ln)
                        sb.inc(sc.activation(cols[:, 7:8], cols[:, 6:7], AF.Exp, scale=kn05[:, 0:1]), "S")
                        sb.wait(ve, "S")
                        sb.inc(ve.tensor_scalar(sq_s, x_tok, cols[:, 2:3], cols[:, 7:8],
                                                ALU.subtract, ALU.mult), "V")
                        for j in range(16):
                            sb.wait(te, "V")
                            sb.inc(te.transpose(p_tr, sq_s[:, j * 128:(j + 1) * 128], identb), "T")
                            sb.wait(ve, "T")
                            sb.inc(ve.tensor_scalar(
                                xsuT[:, j * 512 + tt * 128: j * 512 + tt * 128 + 128],
                                p_tr, su_ug[:, j:j + 1], None, ALU.mult), "V")
                    for hh2 in range(2):
                        c0 = hh2 * 256
                        for j in range(16):
                            sb.wait(te, "V")
                            sb.inc(te.matmul(p_a[:, 0:256], h128b,
                                             xsuT[:, j * 512 + c0: j * 512 + c0 + 256],
                                             start=True, stop=True), "T")
                            sb.wait(ve, "T")
                            sb.inc(ve.tensor_copy(bb[:, j * 256:(j + 1) * 256], p_a[:, 0:256]), "V")
                        for s4 in range(4):
                            stp = 1 << s4
                            a_ap = bb.rearrange("p (pr two w) -> p pr two w", two=2, w=stp * 256)[:, :, 0, :]
                            b_ap = bb.rearrange("p (pr two w) -> p pr two w", two=2, w=stp * 256)[:, :, 1, :]
                            t_ap = btmp.rearrange("p (pr w) -> p pr w", w=stp * 256)
                            sb.wait(ve, "V")
                            sb.inc(ve.tensor_tensor(t_ap, a_ap, b_ap, ALU.subtract), "V")
                            sb.wait(ve, "V")
                            if s4 == 3:
                                x1v = x1.rearrange("p (j t) -> p j t", t=T)
                                sb.inc(ve.tensor_tensor(
                                    x1v[:, 0:8, oc * 512 + c0: oc * 512 + c0 + 256], a_ap, b_ap, ALU.add), "V")
                                sb.inc(ve.tensor_copy(
                                    x1v[:, 8:16, oc * 512 + c0: oc * 512 + c0 + 256], t_ap), "V")
                            else:
                                sb.inc(ve.tensor_tensor(a_ap, a_ap, b_ap, ALU.add), "V")
                                sb.wait(ve, "V")
                                sb.inc(ve.tensor_copy(b_ap, t_ap), "V")
                if debug:
                    sb.wait(sy, "V")
                    sb.inc(sy.dma_start(out=dbg["d_x1"], in_=x1[:, 0:16 * 512]), "DS", 16)

            sb.fence(nc)
            with (nc.sbuf_tensor("F0lhs_up", [128, 16 * 512], BF16) as lup_g,
                  nc.sbuf_tensor("F0lhs_gt", [128, 16 * 512], BF16) as lgt_g,
                  nc.sbuf_tensor("F0sgb", [128, 512], BF16) as sgb_g,
                  nc.sbuf_tensor("F0m1b", [128, 512], F32) as m1b_g,
                  nc.sbuf_tensor("F0x2b", [128, 512], BF16) as x2b_g,
                  nc.psum_tensor("F0p_up", [128, 512], F32) as pup_g,
                  nc.psum_tensor("F0p_gt", [128, 512], F32) as pgt_g):
                lhs_up, lhs_gt, sgb, m1b, x2b = (lup_g.ap(), lgt_g.ap(), sgb_g.ap(),
                                                 m1b_g.ap(), x2b_g.ap())
                p_up, p_gt = pup_g.ap(), pgt_g.ap()
                for b16 in range(16):
                    sb.wait(gp, "T", "DT")
                    sb.inc(gp.dma_start(
                        out=lhs_up.rearrange("p (kt m) -> p kt m", m=512),
                        in_=wupT[:, b16 * 512:(b16 + 1) * 512].rearrange("(kt p) m -> p kt m", p=128)), "DG", 16)
                    sb.inc(gp.dma_start(
                        out=lhs_gt.rearrange("p (kt m) -> p kt m", m=512),
                        in_=wgtT[:, b16 * 512:(b16 + 1) * 512].rearrange("(kt p) m -> p kt m", p=128)), "DG", 16)
                    if DBG_FFN and b16 == 0:
                        sb.wait(sy, "DG")
                        sb.inc(sy.dma_start(out=dbg_lhs, in_=lhs_up), "DS", 16)
                    for mt in range(4):
                        for tc in range(T // 512):
                            sb.wait(te, "DG", "V", "S")
                            for kt in range(16):
                                te_i = te.matmul(
                                    p_up,
                                    lhs_up[:, kt * 512 + mt * 128: kt * 512 + mt * 128 + 128],
                                    x1[:, kt * T + tc * 512: kt * T + tc * 512 + 512],
                                    start=(kt == 0), stop=(kt == 15))
                            sb.inc(te_i, "T")
                            for kt in range(16):
                                te_i = te.matmul(
                                    p_gt,
                                    lhs_gt[:, kt * 512 + mt * 128: kt * 512 + mt * 128 + 128],
                                    x1[:, kt * T + tc * 512: kt * T + tc * 512 + 512],
                                    start=(kt == 0), stop=(kt == 15))
                            sb.inc(te_i, "T")
                            sb.wait(sc, "T")
                            sb.inc(sc.activation(sgb, p_gt, AF.Sigmoid), "S")
                            sb.wait(ve, "S")
                            sb.inc(ve.tensor_tensor(m1b, p_gt, sgb, ALU.mult), "V")
                            sb.wait(ve, "V", "DS")
                            sb.inc(ve.tensor_tensor(x2b, m1b, p_up, ALU.mult), "V")
                            sb.wait(sy, "V")
                            sb.inc(sy.dma_start(
                                out=x2s[b16 * 512 + mt * 128: b16 * 512 + mt * 128 + 128,
                                        tc * 512:(tc + 1) * 512],
                                in_=x2b), "DS", 16)
                            if debug and b16 == 0 and mt == 0 and tc == 0:
                                sb.wait(sy, "V", "T")
                                sb.inc(sy.dma_start(out=dbg["d_x2"], in_=x2b), "DS", 16)
                                sb.inc(sy.dma_start(out=dbg["d_up"], in_=m1b), "DS", 16)

        sb.fence(nc)
        with (nc.sbuf_tensor("F0x2sl", [128, 64 * 512], BF16) as x2sl_g,
              nc.sbuf_tensor("F0wdnb", [128, 2 * 1024], BF16) as wdnb_g,
              nc.sbuf_tensor("F0xs", [128, 512], F32) as xs_g,
              contextlib.ExitStack() as st3):
            x2sl, wdnb, xs = x2sl_g.ap(), wdnb_g.ap(), xs_g.ap()
            p_d = [st3.enter_context(nc.psum_tensor(f"F0p_d{i}", [128, 512], F32)).ap()
                   for i in range(8)]
            for tg in range(T // 512):
                sb.wait(sy, "T", "DS")
                sb.inc(sy.dma_start(
                    out=x2sl.rearrange("p (kt t) -> p kt t", t=512),
                    in_=x2s[:, tg * 512:(tg + 1) * 512].rearrange("(kt p) t -> p kt t", p=128)), "DS", 16)
                for half in range(2):
                    for kt in range(64):
                        sb.wait(gp, "T", "DS")
                        sb.inc(gp.dma_start(
                            out=wdnb[:, (kt % 2) * 1024: (kt % 2) * 1024 + 1024],
                            in_=wdns[kt * 128:(kt + 1) * 128,
                                     half * 1024: half * 1024 + 1024]), "DG", 16)
                        sb.wait(te, "DG", "DS", "V")
                        for sub in range(4):
                            for dc2 in range(2):
                                te_i = te.matmul(
                                    p_d[sub * 2 + dc2],
                                    x2sl[:, kt * 512 + sub * 128: kt * 512 + sub * 128 + 128],
                                    wdnb[:, (kt % 2) * 1024 + dc2 * 512: (kt % 2) * 1024 + dc2 * 512 + 512],
                                    start=(kt == 0), stop=(kt == 63))
                        sb.inc(te_i, "T")
                    for sub in range(4):
                        r0 = tg * 512 + sub * 128
                        for dc2 in range(2):
                            col = half * 1024 + dc2 * 512
                            sb.wait(sy, "V", "DS")
                            sb.inc(sy.dma_start(out=xs, in_=x_h[r0:r0 + 128, col:col + 512]), "DS", 16)
                            sb.wait(ve, "T", "DS")
                            sb.inc(ve.tensor_tensor(xs, p_d[sub * 2 + dc2], xs, ALU.add), "V")
                            sb.wait(sy, "V")
                            sb.inc(sy.dma_start(out=y_h[r0:r0 + 128, col:col + 512], in_=xs), "DS", 16)


def _prep_l0_ffn_inputs(x_core, ins):
    cons = mk_consts()
    h128 = np.asarray(algebra_butterfly_eye(128), dtype=ml_dtypes.bfloat16)
    h64 = np.asarray(algebra_butterfly_eye(64), dtype=ml_dtypes.bfloat16)
    return dict(
        x=np.ascontiguousarray(x_core),
        W_up=ins["L0_ffn_W_up"], W_gate=ins["L0_ffn_W_gate"], W_down=ins["L0_ffn_W_down"],
        su_ug=np.ascontiguousarray(ins["L0_ffn_SU_ug"].reshape(16, 128).T),
        su_dp=np.ascontiguousarray(ins["L0_ffn_SU_dp"].reshape(64, 128)),
        ws_up=np.ascontiguousarray(ins["L0_ffn_Wscale_ug"][:8192].reshape(64, 128).T),
        ws_gt=np.ascontiguousarray(ins["L0_ffn_Wscale_ug"][8192:].reshape(64, 128).T),
        ws_dp=np.ascontiguousarray(ins["L0_ffn_Wscale_dp"].reshape(16, 128).T),
        c_ident=cons["c_ident"], c_h128=h128, c_h64=h64)


def algebra_butterfly_eye(n):
    x = np.eye(n, dtype=np.float32)
    h = 1
    while h < n:
        x = x.reshape(-1, n // (2 * h), 2, h)
        a, b = x[:, :, 0, :].copy(), x[:, :, 1, :].copy()
        x = np.stack([a + b, a - b], axis=2).reshape(-1, n)
        h *= 2
    return x


def emit_l1_ffn(nc, sb, E, T, debug=False):
    OC = 512
    n_oc = T // OC

    x_h, y8_h = E["x"], E["y8"]
    w1_h, w3_h, w2_h, wr_h, msk_h = E["w1"], E["w3"], E["w2"], E["wr"], E["opmask"]
    ident_h = E["c_ident"]
    w1c = nc.dram_tensor("w1c", [D, D], BF16, kind="Internal").ap()
    w3c = nc.dram_tensor("w3c", [D, D], BF16, kind="Internal").ap()
    w2c = nc.dram_tensor("w2c", [D, D], BF16, kind="Internal").ap()
    w1T = nc.dram_tensor("w1T", [D, D], BF16, kind="Internal").ap()
    w3T = nc.dram_tensor("w3T", [D, D], BF16, kind="Internal").ap()
    w2T = nc.dram_tensor("w2T", [D, D], BF16, kind="Internal").ap()
    cc_in = nc.dram_tensor("cc_in", [8, D], F32, kind="Internal")
    cc_out = nc.dram_tensor("cc_out", [8, D], F32, kind="Internal")
    with contextlib.ExitStack() as st:
        def sbuf(name, shape, dt):
            return st.enter_context(nc.sbuf_tensor("F1" + name, shape, dt)).ap()

        def psum(name, shape, dt=F32):
            return st.enter_context(nc.psum_tensor("F1" + name, shape, dt)).ap()

        identb = sbuf("identb", [128, 128], BF16)
        wrT = sbuf("wrT", [128, 16 * 8], BF16)
        opmask = sbuf("opmask_s", [8, 1], F32)
        cols = sbuf("cols", [128, 8], F32)
        kn05 = sbuf("kn05", [128, 1], F32)
        x_tok = sbuf("x_tok", [128, D], F32)
        sq_f = sbuf("sq_f", [128, D], F32)
        sq_s = sbuf("sq_s", [128, D], BF16)
        xlnT = sbuf("xlnT", [128, 16 * OC], BF16)
        h1T = sbuf("h1T", [128, 16 * OC], BF16)
        h2T = sbuf("h2T", [128, 16 * OC], BF16)
        h3b = sbuf("h3b", [128, 4 * D], BF16)
        wbuf = sbuf("wbuf", [128, 16 * 512], BF16)
        hf = sbuf("hf", [128, 512], F32)
        sgb = sbuf("sgb", [128, 512], BF16)
        lg = sbuf("lg", [128, 8], F32)
        lcols = sbuf("lcols", [128, 8], F32)
        mb = sbuf("mb", [128, 4 * 8], BF16)
        op_sb = sbuf("op_sb", [8, D], F32)
        x8 = sbuf("x8", [8, D], F32)

        p_proj = psum("p_proj", [128, 512], F32)
        p_tr = psum("p_tr", [128, 128], BF16)
        p_lg = psum("p_lg", [128, 8], F32)
        p_op = [psum(f"p_op{i}", [8, 512], F32) for i in range(4)]

        sy, gp, ve, sc, te = nc.sync, nc.gpsimd, nc.vector, nc.scalar, nc.tensor

        sb.inc(sy.dma_start(out=identb, in_=ident_h), "DS", 16)
        sb.inc(sy.dma_start(out=opmask, in_=msk_h), "DS", 16)
        sb.inc(ve.memset(kn05, -0.5), "V")
        for _e in (te, sc, gp, ve):
            sb.wait(_e, "DS")

        with nc.sbuf_tensor("F1cast_s", [128, D], BF16) as cast_g:
            cast_s = cast_g.ap()
            for src, dstc in ((w1_h, w1c), (w3_h, w3c), (w2_h, w2c)):
                for t_i in range(16):
                    sb.wait(gp, "DG")
                    sb.inc(gp.dma_start(out=cast_s,
                                        in_=src[t_i * 128:(t_i + 1) * 128, :]), "DG", 16)
                    sb.wait(gp, "DG")
                    sb.inc(gp.dma_start(out=dstc[t_i * 128:(t_i + 1) * 128, :],
                                        in_=cast_s), "DG", 16)
            sb.wait(sy, "DG")
            for srcc, dstT in ((w1c, w1T), (w3c, w3T), (w2c, w2T)):
                for ht in range(16):
                    sb.inc(sy.dma_start_transpose(
                        cast_s, srcc[:, ht * 128:(ht + 1) * 128]), "DT", 16)
                    sb.wait(sy, "DT")
                    sb.inc(sy.dma_start(out=dstT[ht * 128:(ht + 1) * 128, :],
                                        in_=cast_s), "DT", 16)
                    sb.wait(sy, "DT")
            with nc.sbuf_tensor("F1wr8", [8, D], BF16) as wr8_g:
                wr8 = wr8_g.ap()
                sb.wait(gp, "DT")
                sb.inc(gp.dma_start(out=wr8, in_=wr_h), "DG", 16)
                for kt in range(16):
                    sb.wait(te, "DG", "V")
                    sb.inc(te.transpose(p_tr[0:128, 0:8], wr8[:, kt * 128:(kt + 1) * 128],
                                        identb[0:8, 0:8]), "T")
                    sb.wait(ve, "T")
                    sb.inc(ve.tensor_copy(wrT[:, kt * 8:(kt + 1) * 8], p_tr[0:128, 0:8]), "V")

        sb.fence(nc)
        for oc in range(n_oc):
            t0 = oc * OC
            for tt in range(4):
                r0 = t0 + tt * 128
                sb.wait(sy, "S", "V", "T", "DS")
                sb.inc(sy.dma_start(out=x_tok, in_=x_h[r0:r0 + 128, :]), "DS", 16)
                sb.wait(sc, "DS")
                sb.inc(sc.activation(sq_f, x_tok, AF.Square), "S")
                sb.wait(ve, "DS")
                sb.inc(ve.tensor_reduce(cols[:, 1:2], x_tok, AX.X, ALU.add), "V")
                sb.wait(ve, "S")
                sb.inc(ve.tensor_reduce(cols[:, 0:1], sq_f, AX.X, ALU.add), "V")
                sb.wait(ve, "V")
                ve.tensor_scalar_mul(cols[:, 2:3], cols[:, 1:2], 1.0 / D)
                sb.inc(ve.tensor_scalar(cols[:, 3:4], cols[:, 0:1], 1.0 / D, EPS, ALU.mult, ALU.add), "V")
                sb.wait(ve, "V")
                sb.inc(ve.tensor_tensor(cols[:, 4:5], cols[:, 2:3], cols[:, 2:3], ALU.mult), "V")
                sb.wait(ve, "V")
                sb.inc(ve.tensor_tensor(cols[:, 5:6], cols[:, 3:4], cols[:, 4:5], ALU.subtract), "V")
                sb.wait(sc, "V")
                sc.activation(cols[:, 6:7], cols[:, 5:6], AF.Ln)
                sb.inc(sc.activation(cols[:, 7:8], cols[:, 6:7], AF.Exp, scale=kn05[:, 0:1]), "S")
                sb.wait(ve, "S")
                sb.inc(ve.tensor_scalar(sq_s, x_tok, cols[:, 2:3], cols[:, 7:8],
                                        ALU.subtract, ALU.mult), "V")
                for j in range(16):
                    sb.wait(te, "V")
                    sb.inc(te.transpose(p_tr, sq_s[:, j * 128:(j + 1) * 128], identb), "T")
                    sb.wait(ve, "T")
                    sb.inc(ve.tensor_copy(xlnT[:, j * OC + tt * 128: j * OC + tt * 128 + 128], p_tr), "V")

            for tt in range(4):
                sb.wait(te, "V")
                for kt in range(16):
                    te_i = te.matmul(p_lg, xlnT[:, kt * OC + tt * 128: kt * OC + tt * 128 + 128],
                                     wrT[:, kt * 8:(kt + 1) * 8],
                                     start=(kt == 0), stop=(kt == 15))
                sb.inc(te_i, "T")
                sb.wait(ve, "T")
                sb.inc(ve.tensor_copy(lg, p_lg), "V")
                sb.wait(ve, "V")
                sb.inc(ve.tensor_reduce(lcols[:, 0:1], lg, AX.X, ALU.max), "V")
                sb.wait(ve, "V")
                sb.inc(ve.tensor_scalar(lcols[:, 2:3], lcols[:, 0:1], 1.0, None, ALU.mult), "V")
                sb.wait(ve, "V")
                sb.inc(ve.tensor_scalar(sq_f[:, 0:8], lg, lcols[:, 0:1], None, ALU.is_ge), "V")
                sb.wait(ve, "V")
                sb.inc(ve.scalar_tensor_tensor(sq_f[:, 8:16], sq_f[:, 0:8], -1.0e30, lg,
                                               ALU.mult, ALU.add), "V")
                sb.wait(ve, "V")
                sb.inc(ve.tensor_reduce(lcols[:, 1:2], sq_f[:, 8:16], AX.X, ALU.max), "V")
                sb.wait(ve, "V")
                sb.inc(ve.tensor_scalar(sq_f[:, 16:24], sq_f[:, 8:16], lcols[:, 1:2], None, ALU.is_ge), "V")
                sb.inc(ve.tensor_tensor(lcols[:, 3:4], lcols[:, 0:1], lcols[:, 1:2], ALU.subtract), "V")
                sb.wait(sc, "V")
                sb.inc(sc.activation(lcols[:, 4:5], lcols[:, 3:4], AF.Sigmoid), "S")
                sb.wait(ve, "S")
                sb.inc(ve.tensor_scalar(lcols[:, 5:6], lcols[:, 4:5], -1.0, 1.0, ALU.mult, ALU.add), "V")
                sb.wait(ve, "V")
                sb.inc(ve.tensor_scalar(sq_f[:, 24:32], sq_f[:, 0:8], lcols[:, 4:5], None, ALU.mult), "V")
                sb.wait(ve, "V")
                sb.inc(ve.scalar_tensor_tensor(mb[:, tt * 8:(tt + 1) * 8], sq_f[:, 16:24],
                                               lcols[:, 5:6], sq_f[:, 24:32], ALU.mult, ALU.add), "V")
                if debug and oc == 0 and tt == 0:
                    sb.wait(sy, "V", "S", "T")
                    sb.inc(sy.dma_start(out=dbg["d_lg"], in_=lg), "DS", 16)
                    sb.inc(sy.dma_start(out=dbg["d_M"], in_=mb[:, 0:8]), "DS", 16)

            def hproj(srcT, dstT_buf, wsrc):
                for blk in range(4):
                    sb.wait(gp, "T", "DT")
                    sb.inc(gp.dma_start(
                        out=wbuf.rearrange("p (kt m) -> p kt m", m=512),
                        in_=wsrc[:, blk * 512:(blk + 1) * 512].rearrange("(kt p) m -> p kt m", p=128)), "DG", 16)
                    for mt in range(4):
                        sb.wait(te, "DG", "V", "S")
                        for kt in range(16):
                            te_i = te.matmul(p_proj[:, 0:512],
                                             wbuf[:, kt * 512 + mt * 128: kt * 512 + mt * 128 + 128],
                                             srcT[:, kt * OC: kt * OC + 512],
                                             start=(kt == 0), stop=(kt == 15))
                        sb.inc(te_i, "T")
                        sb.wait(sc, "T")
                        sb.inc(sc.activation(sgb, p_proj[:, 0:512], AF.Sigmoid), "S")
                        sb.wait(ve, "S", "T")
                        sb.inc(ve.tensor_tensor(
                            dstT_buf[:, (blk * 4 + mt) * OC: (blk * 4 + mt) * OC + 512],
                            p_proj[:, 0:512], sgb, ALU.mult), "V")

            hproj(xlnT, h1T, w1T)
            hproj(h1T, h2T, w3T)
            for dc in range(4):
                sb.wait(gp, "T", "DT")
                sb.inc(gp.dma_start(
                    out=wbuf.rearrange("p (kt m) -> p kt m", m=512),
                    in_=w2T[:, dc * 512:(dc + 1) * 512].rearrange("(kt p) m -> p kt m", p=128)), "DG", 16)
                for tt in range(4):
                    sb.wait(te, "DG", "V")
                    for kt in range(16):
                        te_i = te.matmul(p_proj[:, 0:512],
                                         h2T[:, kt * OC + tt * 128: kt * OC + tt * 128 + 128],
                                         wbuf[:, kt * 512: kt * 512 + 512],
                                         start=(kt == 0), stop=(kt == 15))
                    sb.inc(te_i, "T")
                    sb.wait(ve, "T")
                    sb.inc(ve.tensor_copy(h3b[:, tt * D + dc * 512: tt * D + dc * 512 + 512],
                                          p_proj[:, 0:512]), "V")
            for tt in range(4):
                sb.wait(te, "V")
                for dc in range(4):
                    te_i = te.matmul(p_op[dc], mb[:, tt * 8:(tt + 1) * 8],
                                     h3b[:, tt * D + dc * 512: tt * D + dc * 512 + 512],
                                     start=(oc == 0 and tt == 0),
                                     stop=(oc == n_oc - 1 and tt == 3))
                sb.inc(te_i, "T")
            if debug and oc == 0:
                sb.wait(sy, "V", "S", "T")
                sb.inc(sy.dma_start(out=dbg["d_h1"], in_=h1T[:, 0:512]), "DS", 16)
                sb.inc(sy.dma_start(out=dbg["d_h3"], in_=h3b[:, 0:2048]), "DS", 16)

        sb.wait(ve, "T")
        for dc in range(4):
            sb.inc(ve.tensor_copy(op_sb[:, dc * 512:(dc + 1) * 512], p_op[dc]), "V")
        sb.wait(gp, "V")
        sb.inc(gp.dma_start(out=cc_in.ap(), in_=op_sb), "DG", 16)
        sb.wait(gp, "DG")
        gp.collective_compute("AllReduce", ALU.add, replica_groups=[list(range(8))],
                              ins=[cc_in.ap().opt()], outs=[cc_out.ap().opt()]).then_inc(sb.h["CC"], 1)
        sb.c["CC"] += 1
        sb.wait(gp, "CC")
        sb.inc(gp.dma_start(out=op_sb, in_=cc_out.ap()), "DG", 16)
        sb.wait(sy, "DS")
        sb.inc(sy.dma_start(out=x8, in_=x_h[0:8, :]), "DS", 16)
        sb.wait(ve, "DG", "DS")
        sb.inc(ve.scalar_tensor_tensor(x8, op_sb, opmask[:, 0:1], x8, ALU.mult, ALU.add), "V")
        if debug:
            sb.wait(sy, "V")
            sb.inc(sy.dma_start(out=dbg["d_op"], in_=op_sb), "DS", 16)
        sb.wait(sy, "V")
        sb.inc(sy.dma_start(out=y8_h, in_=x8), "DS", 16)


def _prep_l1_ffn_inputs(x_core, ins, core_id):
    cons = mk_consts()
    return dict(
        x=np.ascontiguousarray(x_core),
        w1=ins["L1_ffn_w1"], w3=ins["L1_ffn_w3"], w2=ins["L1_ffn_w2"],
        wr=ins["L1_ffn_wr"],
        opmask=np.full((8, 1), 1.0 if core_id == 0 else 0.0, np.float32),
        c_ident=cons["c_ident"])


_BUILD_CACHE = {}


def build_fused(T=4096, dbg_mid=False):
    nc = bass.Bass("TRN2", target_bir_lowering=False)

    def inp(name, shape, dt=F32):
        return nc.dram_tensor(name, shape, dt, kind="ExternalInput").ap()

    x0 = inp("L0_attn_x", [T, D])
    out = nc.dram_tensor("out", [T, D], F32, kind="ExternalOutput").ap()
    y8 = nc.dram_tensor("y8", [8, D], F32, kind="ExternalOutput").ap()
    mk = "ExternalOutput" if dbg_mid else "Internal"
    x1s = nc.dram_tensor("x1s", [T, D], F32, kind=mk).ap()
    x2s_r = nc.dram_tensor("x2s_r", [T, D], F32, kind=mk).ap()

    cons = dict(c_ident=inp("c_ident", [128, 128], BF16),
                c_masku8=inp("c_masku8", [128, 128], U8),
                c_lt1=inp("c_lt1", [128, 128], F32))

    with contextlib.ExitStack() as st:
        sb = SB(nc, st, ["T", "V", "S", "DS", "DG", "DT", "CC"])
        emit_l0_attn(nc, sb, dict(
            x=x0, y=x1s, Wq=inp("L0_attn_Wq", [D, 1024]), Wk=inp("L0_attn_Wk", [D, 1024]),
            Wv=inp("L0_attn_Wv", [D, D]), Wg=inp("L0_attn_Wg", [D, D]),
            Wgk1=inp("L0_attn_Wgk1", [D, 16]), Wgk2b=inp("p_wgk2b", [17, 1024]),
            normw=inp("p_normw", [128, 1]), Wo=inp("L0_attn_Wo", [D, D]), **cons), T)
        sb.fence(nc)
        emit_l0_ffn(nc, sb, dict(
            x=x1s, y=x2s_r, W_up=inp("L0_ffn_W_up", [8192, D]),
            W_gate=inp("L0_ffn_W_gate", [8192, D]), W_down=inp("L0_ffn_W_down", [D, 8192]),
            su_ug=inp("p_suug", [128, 16]), su_dp=inp("p_sudp", [64, 128]),
            ws_up=inp("p_wsup", [128, 64]), ws_gt=inp("p_wsgt", [128, 64]),
            ws_dp=inp("p_wsdp", [128, 16]), c_ident=cons["c_ident"],
            c_h128=inp("c_h128", [128, 128], BF16), c_h64=inp("c_h64", [64, 64], BF16)), T)
        sb.fence(nc)
        emit_l1_attn(nc, sb, dict(
            x=x2s_r, y=out, Wq=inp("L1_attn_Wq", [D, D]), Wf=inp("L1_attn_Wf", [D, D]),
            Wi=inp("L1_attn_Wi", [D, D]), Wo=inp("L1_attn_Wo", [D, D]),
            gw=inp("p_gw", [128, 16]), **cons), T)
        sb.fence(nc)
        emit_l1_ffn(nc, sb, dict(
            x=out, y8=y8, w1=inp("L1_ffn_w1", [D, D]), w3=inp("L1_ffn_w3", [D, D]),
            w2=inp("L1_ffn_w2", [D, D]), wr=inp("L1_ffn_wr", [8, D]),
            opmask=inp("p_opmask", [8, 1]), c_ident=cons["c_ident"]), T)
        nc.sync.wait_ge(sb.h["DS"], sb.c["DS"])
        if sb.c["DG"]:
            nc.sync.wait_ge(sb.h["DG"], sb.c["DG"])
        if sb.c["DT"]:
            nc.sync.wait_ge(sb.h["DT"], sb.c["DT"])
    return nc


def _prep_fused_inputs(x_core, ins, core_id):
    cons = mk_consts()
    h128 = np.asarray(algebra_butterfly_eye(128), dtype=ml_dtypes.bfloat16)
    h64 = np.asarray(algebra_butterfly_eye(64), dtype=ml_dtypes.bfloat16)
    d = {k: ins[k] for k in (
        "L0_attn_Wq", "L0_attn_Wk", "L0_attn_Wv", "L0_attn_Wg", "L0_attn_Wgk1",
        "L0_attn_Wo", "L0_ffn_W_up", "L0_ffn_W_gate", "L0_ffn_W_down",
        "L1_attn_Wq", "L1_attn_Wf", "L1_attn_Wi", "L1_attn_Wo",
        "L1_ffn_w1", "L1_ffn_w3", "L1_ffn_w2", "L1_ffn_wr")}
    d.update(
        L0_attn_x=np.ascontiguousarray(x_core),
        p_wgk2b=np.ascontiguousarray(np.concatenate(
            [ins["L0_attn_Wgk2"], ins["L0_attn_bgk2"][None, :]], axis=0)),
        p_normw=np.ascontiguousarray(ins["L0_attn_norm_w"][:, None]),
        p_suug=np.ascontiguousarray(ins["L0_ffn_SU_ug"].reshape(16, 128).T),
        p_sudp=np.ascontiguousarray(ins["L0_ffn_SU_dp"].reshape(64, 128)),
        p_wsup=np.ascontiguousarray(ins["L0_ffn_Wscale_ug"][:8192].reshape(64, 128).T),
        p_wsgt=np.ascontiguousarray(ins["L0_ffn_Wscale_ug"][8192:].reshape(64, 128).T),
        p_wsdp=np.ascontiguousarray(ins["L0_ffn_Wscale_dp"].reshape(16, 128).T),
        p_gw=np.ascontiguousarray(ins["L1_attn_g_weight"].reshape(16, 128).T),
        p_opmask=np.full((8, 1), 1.0 if core_id == 0 else 0.0, np.float32),
        c_h128=h128, c_h64=h64, **cons)
    return d


def kernel(**inputs):
    x = np.ascontiguousarray(np.asarray(inputs["L0_attn_x"], np.float32))
    B, T, _ = x.shape
    assert B == NCORES
    key = ("fused", T)
    if key not in _BUILD_CACHE:
        _BUILD_CACHE[key] = build_fused(T=T)
    nc = _BUILD_CACHE[key]
    in_maps = [_prep_fused_inputs(x[c], inputs, c) for c in range(NCORES)]
    r = run_bass_kernel_spmd(nc, in_maps, core_ids=list(range(NCORES)))
    out = np.stack([np.asarray(r.results[c]["out"], np.float32) for c in range(NCORES)], axis=0)
    out[0, 0:8, :] = np.asarray(r.results[0]["y8"], np.float32)
    return out


# version 06
# speedup vs baseline: 1.3885x; 1.3885x over previous
import contextlib
import numpy as np
import ml_dtypes
import concourse.bass as bass
import concourse.mybir as mybir
from concourse.bass_utils import run_bass_kernel_spmd

F32 = mybir.dt.float32
BF16 = mybir.dt.bfloat16
U8 = mybir.dt.uint8
AF = mybir.ActivationFunctionType
ALU = mybir.AluOpType
AX = mybir.AxisListType

NCORES = 8
DBG_FFN = False
D = 2048
EPS = 1e-5


class SB:

    def __init__(self, nc, stack, names):
        self.nc = nc
        self.h = {}
        self.c = {}
        self.last = {}
        for n in names:
            self.h[n] = stack.enter_context(nc.semaphore(n))
            self.c[n] = 0

    def inc(self, inst, name, n=1):
        inst.then_inc(self.h[name], n)
        self.c[name] += n
        return inst

    def fence(self, nc):
        for eng in (nc.sync, nc.gpsimd, nc.vector, nc.scalar, nc.tensor):
            self.wait(eng, *self.c.keys())

    def wait_val(self, eng, name, val):
        key = (id(eng), name)
        if val > 0 and self.last.get(key, -1) < val:
            eng.wait_ge(self.h[name], val)
            self.last[key] = val

    def wait(self, eng, *names):
        for nm in names:
            cnt = self.c[nm]
            key = (id(eng), nm)
            if cnt > 0 and self.last.get(key, -1) < cnt:
                eng.wait_ge(self.h[nm], cnt)
                self.last[key] = cnt


def mk_consts():
    ident = np.eye(128, dtype=ml_dtypes.bfloat16)
    masku8 = np.triu(np.ones((128, 128), np.uint8))
    lt1 = np.triu(np.ones((128, 128), np.float32))
    return {"c_ident": ident, "c_masku8": masku8, "c_lt1": lt1}


def emit_l0_attn(nc, sb, E, T, debug=False):
    OC = 512
    n_oc = T // OC
    NTT = OC // 128
    H = 16
    DK, DV = 64, 128
    KD = 1024
    QSCALE = float(np.log(DK ** (-0.5)))

    x_h, y_h = E["x"], E["y"]
    wq_h, wk_h, wv_h, wg_h = E["Wq"], E["Wk"], E["Wv"], E["Wg"]
    wgk1_h, wgk2b_h, normw_h, wo_h = E["Wgk1"], E["Wgk2b"], E["normw"], E["Wo"]
    ident_h, mask_h, lt1_h = E["c_ident"], E["c_masku8"], E["c_lt1"]
    with contextlib.ExitStack() as st:
        def sbuf(name, shape, dt):
            return st.enter_context(nc.sbuf_tensor("A0" + name, shape, dt)).ap()

        def psum(name, shape, dt=F32):
            return st.enter_context(nc.psum_tensor("A0" + name, shape, dt)).ap()

        identb = sbuf("identb", [128, 128], BF16)
        masku = sbuf("masku", [128, 128], U8)
        lt16 = sbuf("lt16", [128, 128], F32)
        normw = sbuf("normw_s", [128, 1], F32)
        onesb = sbuf("onesb", [128, 1], BF16)
        onesf = sbuf("onesf", [128, 1], F32)
        onesr = sbuf("onesr", [1, 128], F32)
        zerob = sbuf("zerob", [128, 128], BF16)
        cols = sbuf("cols", [128, 8], F32)
        kn05 = sbuf("kn05", [128, 1], F32)
        x_tok = sbuf("x_tok", [128, D], F32)
        sq_f = sbuf("sq_f", [128, D], F32)
        sq_s = sbuf("sq_s", [128, D], BF16)
        xlnT = sbuf("xlnT", [128, 16 * OC], BF16)
        qT = sbuf("qT", [128, 8 * OC], BF16)
        kT = sbuf("kT", [128, 8 * OC], BF16)
        gT = sbuf("gT", [128, 16 * OC], BF16)
        sgp = sbuf("sgp", [128, 2048], BF16)
        v_tok = sbuf("v_tok", [128, NTT * D], BF16)
        t1b = sbuf("t1b", [17, OC], BF16)
        z_tok = sbuf("z_tok", [128, KD], F32)
        sc2 = sbuf("sc2", [128, KD], F32)
        g_tok = sbuf("g_tok", [128, KD], F32)
        at_ = sbuf("at", [128, 8 * 128], F32)
        e_s = sbuf("e_s", [128, 128], F32)
        qtl = sbuf("qtl", [128, 8 * 128], BF16)
        ktl = sbuf("ktl", [128, 8 * 128], BF16)
        khT = sbuf("khT", [128, 8 * 128], BF16)
        kh_tok = sbuf("kh_tok", [128, 8 * 128], BF16)
        pbuf = sbuf("pbuf", [128, 128], BF16)
        stl = sbuf("stl", [128, 8 * 128], BF16)
        s_all = sbuf("s_all", [128, 8 * 128], F32)
        o_all = sbuf("o_all", [128, 2048], F32)
        rstd = sbuf("rstd", [1, 2048], F32)
        rb = sbuf("rb", [128, 512], F32)
        otg = sbuf("otg", [128, NTT * 2048], BF16)
        wbuf = sbuf("wbuf", [128, 2 * 16 * 512], BF16)
        xs = sbuf("xs", [128, 512], F32)
        wgk1b = sbuf("wgk1b", [128, 16 * 16], BF16)
        wgk2b = sbuf("wgk2b", [17, KD], BF16)

        p_projA = psum("p_projA", [128, 512], F32)
        p_projB = psum("p_projB", [128, 512], F32)
        p_proj = p_projA
        p_tr = psum("p_tr", [128, 128], BF16)
        p_tr2 = psum("p_tr2", [128, 128], BF16)
        p_cum = psum("p_cum", [128, 128], F32)
        p_p = psum("p_p", [128, 128], F32)
        p_o = psum("p_o", [128, 128], F32)
        p_ds = psum("p_ds", [64, 128], F32)

        sy, gp, ve, sc, te = nc.sync, nc.gpsimd, nc.vector, nc.scalar, nc.tensor

        sb.inc(sy.dma_start(out=identb, in_=ident_h), "DS", 16)
        sb.inc(sy.dma_start(out=masku, in_=mask_h), "DS", 16)
        sb.inc(sy.dma_start(out=lt16, in_=lt1_h), "DS", 16)
        sb.inc(sy.dma_start(out=normw, in_=normw_h), "DS", 16)
        sb.inc(gp.dma_start(out=wgk1b.rearrange("p (kt m) -> p kt m", m=16),
                            in_=wgk1_h.rearrange("(kt p) m -> p kt m", p=128)), "DG", 16)
        sb.inc(gp.dma_start(out=wgk2b, in_=wgk2b_h), "DG", 16)
        sb.inc(ve.memset(kn05, -0.5), "V")
        sb.inc(ve.memset(onesb, 1.0), "V")
        sb.inc(ve.memset(onesf, 1.0), "V")
        sb.inc(ve.memset(onesr, 1.0), "V")
        sb.inc(ve.memset(zerob, 0.0), "V")
        sb.inc(ve.memset(t1b, 1.0), "V")
        sb.inc(ve.memset(s_all, 0.0), "V")
        sb.wait(ve, "DS")
        sb.inc(ve.tensor_scalar_mul(lt16, lt16, 1.0 / 16.0), "V")
        for _e in (te, sc, gp, sy):
            sb.wait(_e, "DS", "DG")
        PST = {"tr": [0, 0], "wu": [0, 0], "wr": [0, 0], "pe": [0, 0], "pb": 0}

        for oc in range(n_oc):
            t0 = oc * OC
            for tt in range(NTT):
                r0 = t0 + tt * 128
                sb.wait(sy, "S", "V", "T", "DS")
                sb.inc(sy.dma_start(out=x_tok, in_=x_h[r0:r0 + 128, :]), "DS", 16)
                sb.wait(sc, "DS")
                sb.inc(sc.activation(sq_f, x_tok, AF.Square), "S")
                sb.wait(ve, "DS")
                sb.inc(ve.tensor_reduce(cols[:, 1:2], x_tok, AX.X, ALU.add), "V")
                sb.wait(ve, "S")
                sb.inc(ve.tensor_reduce(cols[:, 0:1], sq_f, AX.X, ALU.add), "V")
                sb.wait(ve, "V")
                ve.tensor_scalar_mul(cols[:, 2:3], cols[:, 1:2], 1.0 / D)
                sb.inc(ve.tensor_scalar(cols[:, 3:4], cols[:, 0:1], 1.0 / D, EPS, ALU.mult, ALU.add), "V")
                sb.wait(ve, "V")
                sb.inc(ve.tensor_tensor(cols[:, 4:5], cols[:, 2:3], cols[:, 2:3], ALU.mult), "V")
                sb.wait(ve, "V")
                sb.inc(ve.tensor_tensor(cols[:, 5:6], cols[:, 3:4], cols[:, 4:5], ALU.subtract), "V")
                sb.wait(sc, "V")
                sc.activation(cols[:, 6:7], cols[:, 5:6], AF.Ln)
                sb.inc(sc.activation(cols[:, 7:8], cols[:, 6:7], AF.Exp, scale=kn05[:, 0:1]), "S")
                sb.wait(ve, "S")
                sb.inc(ve.tensor_scalar(sq_s, x_tok, cols[:, 2:3], cols[:, 7:8],
                                        ALU.subtract, ALU.mult), "V")
                if debug and oc == 0 and tt == 0:
                    sb.wait(sy, "V", "S", "T")
                    sb.inc(sy.dma_start(out=dbg["d_cols"], in_=cols), "DS", 16)
                    sb.inc(sy.dma_start(out=dbg["d_xtok"], in_=x_tok), "DS", 16)
                sb.wait(te, "V")
                for j in range(16):
                    pt = p_tr if j % 2 == 0 else p_tr2
                    sb.wait_val(te, "V", PST["tr"][j % 2])
                    sb.inc(te.transpose(pt, sq_s[:, j * 128:(j + 1) * 128], identb), "T")
                    sb.wait(ve, "T")
                    sb.inc(ve.tensor_copy(xlnT[:, j * OC + tt * 128: j * OC + tt * 128 + 128], pt), "V")
                    PST["tr"][j % 2] = sb.c["V"]

            if debug and oc == 0:
                sb.wait(sy, "V", "S", "T")
                sb.inc(sy.dma_start(out=dbg["d_xlnT"], in_=xlnT), "DS", 16)
            def load_w(src_ap, m0, slot):
                sb.wait_val(gp, "T", PST["wu"][slot])
                sb.inc(gp.dma_start(
                    out=wbuf.rearrange("p (s kt m) -> p s kt m", s=2, m=512)[:, slot],
                    in_=src_ap[:, m0:m0 + 512].rearrange("(kt p) m -> p kt m", p=128)), "DG", 16)
                PST["wr"][slot] = sb.c["DG"]

            def run_jobs(jobs):
                load_w(*jobs[0][:2], 0)
                for idx, (wsrc, m0, kind, dst, dcol0) in enumerate(jobs):
                    slot = idx % 2
                    if idx + 1 < len(jobs):
                        load_w(*jobs[idx + 1][:2], (idx + 1) % 2)
                    wb = wbuf.rearrange("p (s kt m) -> p s kt m", s=2, m=512)[:, slot]
                    for mt in range(4):
                        bank = PST["pb"]
                        PST["pb"] ^= 1
                        pp = p_projA if bank == 0 else p_projB
                        sb.wait_val(te, "DG", PST["wr"][slot])
                        sb.wait_val(te, "V", PST["pe"][bank])
                        for kt in range(16):
                            if kind == "fm":
                                te_i = te.matmul(pp[:, 0:512],
                                                 wb[:, kt, mt * 128: mt * 128 + 128],
                                                 xlnT[:, kt * OC: kt * OC + 512],
                                                 start=(kt == 0), stop=(kt == 15))
                            else:
                                te_i = te.matmul(pp[:, 0:512],
                                                 xlnT[:, kt * OC + mt * 128: kt * OC + mt * 128 + 128],
                                                 wb[:, kt, 0:512],
                                                 start=(kt == 0), stop=(kt == 15))
                        sb.inc(te_i, "T")
                        PST["wu"][slot] = sb.c["T"]
                        sb.wait(ve, "T")
                        if kind == "fm":
                            sb.inc(ve.tensor_copy(
                                dst[:, dcol0 + mt * OC: dcol0 + mt * OC + 512], pp[:, 0:512]), "V")
                        else:
                            sb.inc(ve.tensor_copy(
                                dst[:, mt * D + dcol0: mt * D + dcol0 + 512], pp[:, 0:512]), "V")
                        PST["pe"][bank] = sb.c["V"]

            sb.wait(te, "V")
            run_jobs([(wq_h, 0, "fm", qT, 0), (wq_h, 512, "fm", qT, 4 * OC),
                      (wk_h, 0, "fm", kT, 0), (wk_h, 512, "fm", kT, 4 * OC),
                      (wv_h, 0, "tok", v_tok, 0), (wv_h, 512, "tok", v_tok, 512),
                      (wv_h, 1024, "tok", v_tok, 1024), (wv_h, 1536, "tok", v_tok, 1536),
                      (wg_h, 0, "fm", gT, 0), (wg_h, 512, "fm", gT, 4 * OC),
                      (wg_h, 1024, "fm", gT, 8 * OC), (wg_h, 1536, "fm", gT, 12 * OC)])
            sb.wait(te, "DG", "V")
            for kt in range(16):
                te_i = te.matmul(p_projA[0:16, 0:OC], wgk1b[:, kt * 16:(kt + 1) * 16],
                                 xlnT[:, kt * OC: kt * OC + OC],
                                 start=(kt == 0), stop=(kt == 15))
            sb.inc(te_i, "T")
            sb.wait(ve, "T")
            sb.inc(ve.tensor_copy(t1b[0:16, :], p_projA[0:16, 0:OC]), "V")

            if debug and oc == 0:
                sb.wait(sy, "V", "S", "T")
                for nm, buf in (("d_qT", qT), ("d_kT", kT), ("d_v", v_tok), ("d_t1", t1b)):
                    sb.inc(sy.dma_start(out=dbg[nm], in_=buf), "DS", 16)
            for i4 in range(4):
                gsl = gT[:, i4 * 2048:(i4 + 1) * 2048]
                sb.wait(sc, "V")
                sb.inc(sc.activation(sgp, gsl, AF.Sigmoid), "S")
                sb.wait(ve, "S")
                sb.inc(ve.tensor_tensor(gsl, gsl, sgp, ALU.mult), "V")

            if debug and oc == 0:
                sb.wait(sy, "V", "S", "T")
                sb.inc(sy.dma_start(out=dbg["d_gT"], in_=gT), "DS", 16)
            for cc in range(NTT):
                ccol = cc * 128
                for half in range(2):
                    sb.wait(te, "V", "DG")
                    sb.inc(te.matmul(p_proj[:, 0:512], t1b[:, ccol:ccol + 128],
                                     wgk2b[:, half * 512: half * 512 + 512],
                                     start=True, stop=True), "T")
                    sb.wait(ve, "T")
                    sb.inc(ve.tensor_copy(z_tok[:, half * 512: half * 512 + 512], p_proj[:, 0:512]), "V")
                sb.wait(sc, "V")
                sc.activation(sc2, z_tok, AF.Abs)
                sc.activation(sc2, sc2, AF.Exp, scale=-1.0)
                sb.inc(sc.activation(sc2, sc2, AF.Ln, bias=1.0), "S")
                sb.wait(ve, "S")
                ve.tensor_scalar_min(g_tok, z_tok, 0.0)
                sb.inc(ve.tensor_tensor(g_tok, g_tok, sc2, ALU.subtract), "V")
                for j in range(8):
                    sb.wait(te, "V")
                    sb.inc(te.matmul(p_cum, g_tok[:, j * 128:(j + 1) * 128], lt16,
                                     start=True, stop=True), "T")
                    sb.wait(ve, "T")
                    sb.inc(ve.tensor_copy(at_[:, j * 128:(j + 1) * 128], p_cum), "V")
                sb.wait(ve, "V")
                for j in range(8):
                    lastc = at_[:, j * 128 + 127: j * 128 + 128]
                    ve.tensor_scalar(cols[:, 0:1], lastc, -0.5, QSCALE, ALU.mult, ALU.add)
                    sb.inc(ve.tensor_scalar_mul(cols[:, 1:2], lastc, 0.5), "V")
                    sb.wait(sc, "V")
                    sb.inc(sc.activation(e_s, at_[:, j * 128:(j + 1) * 128], AF.Exp,
                                         bias=cols[:, 0:1]), "S")
                    sb.wait(ve, "S")
                    sb.inc(ve.tensor_tensor(qtl[:, j * 128:(j + 1) * 128],
                                            qT[:, j * OC + ccol: j * OC + ccol + 128], e_s, ALU.mult), "V")
                    sb.wait(sc, "V")
                    sb.inc(sc.activation(e_s, at_[:, j * 128:(j + 1) * 128], AF.Exp,
                                         scale=-1.0, bias=cols[:, 1:2]), "S")
                    sb.wait(ve, "S")
                    sb.inc(ve.tensor_tensor(ktl[:, j * 128:(j + 1) * 128],
                                            kT[:, j * OC + ccol: j * OC + ccol + 128], e_s, ALU.mult), "V")
                    sb.wait(sc, "V")
                    sb.inc(sc.activation(cols[:, 2:3], cols[:, 1:2], AF.Exp), "S")
                    sb.inc(sc.activation(cols[:, 3:4], lastc, AF.Exp), "S")
                    sb.wait(ve, "S")
                    sb.inc(ve.tensor_scalar(khT[:, j * 128:(j + 1) * 128],
                                            ktl[:, j * 128:(j + 1) * 128],
                                            cols[:, 2:3], None, ALU.mult), "V")
                    sb.inc(ve.tensor_scalar(stl[:, j * 128:(j + 1) * 128],
                                            s_all[:, j * 128:(j + 1) * 128],
                                            cols[:, 2:3], None, ALU.mult), "V")
                    sb.wait(te, "V")
                    sb.inc(te.transpose(p_tr, khT[:, j * 128:(j + 1) * 128], identb), "T")
                    sb.wait(ve, "T")
                    sb.inc(ve.tensor_copy(kh_tok[:, j * 128:(j + 1) * 128], p_tr), "V")
                    for hh in range(2):
                        h = 2 * j + hh
                        pb = hh * 64
                        sb.wait(te, "V")
                        sb.inc(te.matmul(p_p, ktl[pb:pb + 64, j * 128:(j + 1) * 128],
                                         qtl[pb:pb + 64, j * 128:(j + 1) * 128],
                                         start=True, stop=True), "T")
                        sb.wait(ve, "T")
                        sb.inc(ve.select(pbuf, masku, p_p, zerob), "V")
                        sb.wait(te, "V")
                        te.matmul(p_o, v_tok[:, cc * D + h * 128: cc * D + h * 128 + 128],
                                  pbuf, start=True, stop=False)
                        sb.inc(te.matmul(p_o, stl[pb:pb + 64, j * 128:(j + 1) * 128],
                                         qtl[pb:pb + 64, j * 128:(j + 1) * 128],
                                         start=False, stop=True), "T")
                        sb.wait(ve, "T")
                        sb.inc(ve.tensor_copy(o_all[:, h * 128:(h + 1) * 128], p_o), "V")
                        sb.wait(te, "V")
                        sb.inc(te.matmul(p_ds, kh_tok[:, h * 64: h * 64 + 64],
                                         v_tok[:, cc * D + h * 128: cc * D + h * 128 + 128],
                                         start=True, stop=True), "T")
                        sb.wait(ve, "T")
                        sb.inc(ve.scalar_tensor_tensor(
                            s_all[pb:pb + 64, j * 128:(j + 1) * 128],
                            s_all[pb:pb + 64, j * 128:(j + 1) * 128],
                            cols[pb:pb + 64, 3:4], p_ds, ALU.mult, ALU.add), "V")
                if debug and oc == 0 and cc == 0:
                    sb.wait(sy, "V", "S", "T")
                    for nm, buf in (("d_z", z_tok), ("d_g", g_tok), ("d_at", at_),
                                    ("d_qtl", qtl), ("d_ktl", ktl), ("d_khT", khT),
                                    ("d_oall", o_all), ("d_sall", s_all)):
                        sb.inc(sy.dma_start(out=dbg[nm], in_=buf), "DS", 16)
                if debug and oc == 0 and cc == 3:
                    sb.wait(sy, "V", "S", "T")
                    sb.inc(sy.dma_start(out=dbg["d_oall3"], in_=o_all), "DS", 16)
                sb.wait(sc, "V")
                sb.inc(sc.activation(sgp, o_all, AF.Square), "S")
                for nn in range(4):
                    sb.wait(te, "S", "V")
                    sb.inc(te.matmul(p_projB[0:1, 0:512], onesb, sgp[:, nn * 512:(nn + 1) * 512],
                                     start=True, stop=True), "T")
                    sb.wait(ve, "T")
                    sb.inc(ve.tensor_copy(rstd[:, nn * 512:(nn + 1) * 512], p_projB[0:1, 0:512]), "V")
                sb.wait(ve, "V")
                sb.inc(ve.tensor_scalar(rstd, rstd, 1.0 / 128.0, EPS, ALU.mult, ALU.add), "V")
                sb.wait(sc, "V")
                sc.activation(rstd, rstd, AF.Ln)
                sb.inc(sc.activation(rstd, rstd, AF.Exp, scale=kn05[0:1, 0:1]), "S")
                for nn in range(4):
                    sb.wait(te, "S", "V")
                    sb.inc(te.matmul(p_proj[:, 0:512], onesr, rstd[:, nn * 512:(nn + 1) * 512],
                                     start=True, stop=True), "T")
                    sb.wait(ve, "T")
                    sb.inc(ve.tensor_copy(rb, p_proj[:, 0:512]), "V")
                    o_sl = o_all[:, nn * 512:(nn + 1) * 512]
                    ve.tensor_tensor(o_sl, o_sl, rb, ALU.mult)
                    g_ap = gT.rearrange("p (h t) -> p h t", t=OC)[
                        :, nn * 4:(nn + 1) * 4, ccol:ccol + 128]
                    og_ap = otg.rearrange("p (c h t) -> p c h t", h=16, t=128)[
                        :, cc, nn * 4:(nn + 1) * 4, :]
                    sb.inc(ve.scalar_tensor_tensor(
                        og_ap, o_sl.rearrange("p (h t) -> p h t", t=128),
                        normw[:, 0:1], g_ap, ALU.mult, ALU.mult), "V")

            if debug and oc == 0:
                sb.wait(sy, "V", "S", "T")
                sb.inc(sy.dma_start(out=dbg["d_otg"], in_=otg), "DS", 16)
            sb.wait(te, "V")
            PST["pe"] = [sb.c["V"], sb.c["V"]]
            load_w(wo_h, 0, 0)
            for blk in range(4):
                slot = blk % 2
                if blk + 1 < 4:
                    load_w(wo_h, (blk + 1) * 512, (blk + 1) % 2)
                wb = wbuf.rearrange("p (s kt m) -> p s kt m", s=2, m=512)[:, slot]
                for tt in range(NTT):
                    r0 = t0 + tt * 128
                    bank = PST["pb"]
                    PST["pb"] ^= 1
                    pp = p_projA if bank == 0 else p_projB
                    sb.wait_val(te, "DG", PST["wr"][slot])
                    sb.wait_val(te, "V", PST["pe"][bank])
                    for kt in range(16):
                        te_i = te.matmul(
                            pp[:, 0:512],
                            otg[:, tt * 2048 + kt * 128: tt * 2048 + kt * 128 + 128],
                            wb[:, kt, 0:512],
                            start=(kt == 0), stop=(kt == 15))
                    sb.inc(te_i, "T")
                    PST["wu"][slot] = sb.c["T"]
                    sb.wait(sy, "V", "DS")
                    sb.inc(sy.dma_start(out=xs, in_=x_h[r0:r0 + 128, blk * 512: blk * 512 + 512]), "DS", 16)
                    sb.wait(ve, "T", "DS")
                    sb.inc(ve.tensor_tensor(xs, pp[:, 0:512], xs, ALU.add), "V")
                    PST["pe"][bank] = sb.c["V"]
                    sb.wait(sy, "V")
                    sb.inc(sy.dma_start(out=y_h[r0:r0 + 128, blk * 512: blk * 512 + 512], in_=xs), "DS", 16)


def _prep_l0_attn_inputs(x_core, ins):
    cons = mk_consts()
    return dict(
        x=np.ascontiguousarray(x_core),
        Wq=ins["L0_attn_Wq"], Wk=ins["L0_attn_Wk"], Wv=ins["L0_attn_Wv"],
        Wg=ins["L0_attn_Wg"], Wgk1=ins["L0_attn_Wgk1"],
        Wgk2b=np.ascontiguousarray(np.concatenate(
            [ins["L0_attn_Wgk2"], ins["L0_attn_bgk2"][None, :]], axis=0)),
        normw=np.ascontiguousarray(ins["L0_attn_norm_w"][:, None]),
        Wo=ins["L0_attn_Wo"], **cons)


def emit_l1_attn(nc, sb, E, T, debug=False):
    OC = 512
    n_oc = T // OC
    NTT = OC // 128
    QSCALE = float(np.log(128.0 ** (-0.5)))

    x_h, y_h = E["x"], E["y"]
    wq_h, wf_h, wi_h, wo_h, gw_h = E["Wq"], E["Wf"], E["Wi"], E["Wo"], E["gw"]
    ident_h, mask_h, lt1_h = E["c_ident"], E["c_masku8"], E["c_lt1"]
    with contextlib.ExitStack() as st:
        def sbuf(name, shape, dt):
            return st.enter_context(nc.sbuf_tensor("A1" + name, shape, dt)).ap()

        def psum(name, shape, dt=F32):
            return st.enter_context(nc.psum_tensor("A1" + name, shape, dt)).ap()

        identb = sbuf("identb", [128, 128], BF16)
        masku = sbuf("masku", [128, 128], U8)
        lt1 = sbuf("lt1", [128, 128], F32)
        gw = sbuf("gw_s", [128, 16], F32)
        onesb = sbuf("onesb", [128, 1], BF16)
        onesr = sbuf("onesr", [1, 128], F32)
        zerob = sbuf("zerob", [128, 128], BF16)
        cols = sbuf("cols", [128, 8], F32)
        kn05 = sbuf("kn05", [128, 1], F32)
        x_tok = sbuf("x_tok", [128, D], F32)
        sq_f = sbuf("sq_f", [128, D], BF16)
        sq_s = sbuf("sq_s", [128, D], BF16)
        xlnT = sbuf("xlnT", [128, 16 * OC], BF16)
        qT = sbuf("qT", [128, 16 * OC], BF16)
        sgp = sbuf("sgp", [128, 2048], BF16)
        f_bf = sbuf("f_bf", [128, NTT * D], BF16)
        k_c = sbuf("k_c", [128, D], BF16)
        v_tok = sbuf("v_tok", [128, NTT * D], BF16)
        fz = sbuf("fz", [128, D], F32)
        sc2 = sbuf("sc2", [128, D], F32)
        at_ = sbuf("at", [128, 16 * 128], F32)
        e_s = sbuf("e_s", [128, 128], F32)
        kTc = sbuf("kTc", [128, 16 * 128], BF16)
        qtl = sbuf("qtl", [128, 16 * 128], BF16)
        ktl = sbuf("ktl", [128, 16 * 128], BF16)
        khT = sbuf("khT", [128, 16 * 128], BF16)
        kh_tok = sbuf("kh_tok", [128, 16 * 128], BF16)
        pbuf = sbuf("pbuf", [128, 128], BF16)
        stl = sbuf("stl", [128, 16 * 128], BF16)
        s_all = sbuf("s_all", [128, 16 * 128], F32)
        o_all = sbuf("o_all", [128, 2048], F32)
        rtok = sbuf("rtok", [1, 128], F32)
        rtok4 = sbuf("rtok4", [1, 128], F32)
        rb = sbuf("rb", [128, 128], F32)
        otg = sbuf("otg", [128, NTT * 2048], BF16)
        wbuf = sbuf("wbuf", [128, 2 * 16 * 512], BF16)
        xs = sbuf("xs", [128, 512], F32)

        p_projA = psum("p_projA", [128, 512], F32)
        p_projB = psum("p_projB", [128, 512], F32)
        p_proj = p_projA
        p_tr = psum("p_tr", [128, 128], BF16)
        p_tr2 = psum("p_tr2", [128, 128], BF16)
        p_cum = psum("p_cum", [128, 128], F32)
        p_p = psum("p_p", [128, 128], F32)
        p_o = psum("p_o", [128, 128], F32)
        p_ds = psum("p_ds", [128, 128], F32)

        sy, gp, ve, sc, te = nc.sync, nc.gpsimd, nc.vector, nc.scalar, nc.tensor

        sb.inc(sy.dma_start(out=identb, in_=ident_h), "DS", 16)
        sb.inc(sy.dma_start(out=masku, in_=mask_h), "DS", 16)
        sb.inc(sy.dma_start(out=lt1, in_=lt1_h), "DS", 16)
        sb.inc(sy.dma_start(out=gw, in_=gw_h), "DS", 16)
        sb.inc(ve.memset(kn05, -0.5), "V")
        sb.inc(ve.memset(onesb, 1.0), "V")
        sb.inc(ve.memset(onesr, 1.0), "V")
        sb.inc(ve.memset(zerob, 0.0), "V")
        sb.inc(ve.memset(s_all, 0.0), "V")
        sb.wait(ve, "DS")
        for _e in (te, sc, gp, sy):
            sb.wait(_e, "DS")

        PST = {"tr": [0, 0], "wu": [0, 0], "wr": [0, 0], "pe": [0, 0], "pb": 0}

        def load_w(src_ap, m0, slot):
            sb.wait_val(gp, "T", PST["wu"][slot])
            sb.inc(gp.dma_start(
                out=wbuf.rearrange("p (s kt m) -> p s kt m", s=2, m=512)[:, slot],
                in_=src_ap[:, m0:m0 + 512].rearrange("(kt p) m -> p kt m", p=128)), "DG", 16)
            PST["wr"][slot] = sb.c["DG"]

        for oc in range(n_oc):
            t0 = oc * OC
            for tt in range(NTT):
                r0 = t0 + tt * 128
                sb.wait(sy, "S", "V", "T", "DS")
                sb.inc(sy.dma_start(out=x_tok, in_=x_h[r0:r0 + 128, :]), "DS", 16)
                sb.wait(sc, "DS")
                sb.inc(sc.activation(sq_f, x_tok, AF.Square), "S")
                sb.wait(ve, "DS")
                sb.inc(ve.tensor_reduce(cols[:, 1:2], x_tok, AX.X, ALU.add), "V")
                sb.wait(ve, "S")
                sb.inc(ve.tensor_reduce(cols[:, 0:1], sq_f, AX.X, ALU.add), "V")
                sb.wait(ve, "V")
                ve.tensor_scalar_mul(cols[:, 2:3], cols[:, 1:2], 1.0 / D)
                sb.inc(ve.tensor_scalar(cols[:, 3:4], cols[:, 0:1], 1.0 / D, EPS, ALU.mult, ALU.add), "V")
                sb.wait(ve, "V")
                sb.inc(ve.tensor_tensor(cols[:, 4:5], cols[:, 2:3], cols[:, 2:3], ALU.mult), "V")
                sb.wait(ve, "V")
                sb.inc(ve.tensor_tensor(cols[:, 5:6], cols[:, 3:4], cols[:, 4:5], ALU.subtract), "V")
                sb.wait(sc, "V")
                sc.activation(cols[:, 6:7], cols[:, 5:6], AF.Ln)
                sb.inc(sc.activation(cols[:, 7:8], cols[:, 6:7], AF.Exp, scale=kn05[:, 0:1]), "S")
                sb.wait(ve, "S")
                sb.inc(ve.tensor_scalar(sq_s, x_tok, cols[:, 2:3], cols[:, 7:8],
                                        ALU.subtract, ALU.mult), "V")
                sb.wait(te, "V")
                for j in range(16):
                    pt = p_tr if j % 2 == 0 else p_tr2
                    sb.wait_val(te, "V", PST["tr"][j % 2])
                    sb.inc(te.transpose(pt, sq_s[:, j * 128:(j + 1) * 128], identb), "T")
                    sb.wait(ve, "T")
                    sb.inc(ve.tensor_copy(xlnT[:, j * OC + tt * 128: j * OC + tt * 128 + 128], pt), "V")
                    PST["tr"][j % 2] = sb.c["V"]

            def run_jobs(jobs):
                load_w(*jobs[0][:2], 0)
                for idx, (wsrc, m0, kind, blk) in enumerate(jobs):
                    slot = idx % 2
                    if idx + 1 < len(jobs):
                        load_w(*jobs[idx + 1][:2], (idx + 1) % 2)
                    wb = wbuf.rearrange("p (s kt m) -> p s kt m", s=2, m=512)[:, slot]
                    for mt in range(4):
                        bank = PST["pb"]
                        PST["pb"] ^= 1
                        pp = p_projA if bank == 0 else p_projB
                        sb.wait_val(te, "DG", PST["wr"][slot])
                        sb.wait_val(te, "V", PST["pe"][bank])
                        for kt in range(16):
                            if kind == "q":
                                te_i = te.matmul(pp[:, 0:512],
                                                 wb[:, kt, mt * 128: mt * 128 + 128],
                                                 xlnT[:, kt * OC: kt * OC + 512],
                                                 start=(kt == 0), stop=(kt == 15))
                            else:
                                te_i = te.matmul(pp[:, 0:512],
                                                 xlnT[:, kt * OC + mt * 128: kt * OC + mt * 128 + 128],
                                                 wb[:, kt, 0:512],
                                                 start=(kt == 0), stop=(kt == 15))
                        sb.inc(te_i, "T")
                        PST["wu"][slot] = sb.c["T"]
                        sb.wait(ve, "T")
                        if kind == "q":
                            sb.inc(ve.tensor_copy(
                                qT[:, (blk * 4 + mt) * OC: (blk * 4 + mt) * OC + 512],
                                pp[:, 0:512]), "V")
                        elif kind == "f":
                            f_sl = f_bf[:, mt * D + blk * 512: mt * D + blk * 512 + 512]
                            sb.inc(ve.tensor_copy(f_sl, pp[:, 0:512]), "V")
                        else:
                            sb.inc(ve.tensor_copy(
                                v_tok[:, mt * D + blk * 512: mt * D + blk * 512 + 512],
                                pp[:, 0:512]), "V")
                        PST["pe"][bank] = sb.c["V"]

            sb.wait(te, "V")
            run_jobs([(wq_h, b * 512, "q", b) for b in range(4)]
                     + [(wf_h, b * 512, "f", b) for b in range(4)]
                     + [(wi_h, b * 512, "i", b) for b in range(4)])
            for i4 in range(4):
                qsl = qT[:, i4 * 2048:(i4 + 1) * 2048]
                sb.wait(sc, "V")
                sb.inc(sc.activation(sgp, qsl, AF.Sigmoid), "S")
                sb.wait(ve, "S")
                sb.inc(ve.tensor_tensor(qsl, qsl, sgp, ALU.mult), "V")

            for cc in range(NTT):
                f_c = f_bf[:, cc * D:(cc + 1) * D]
                sb.wait(sc, "V")
                sc.activation(sc2, f_c, AF.Abs)
                sc.activation(sc2, sc2, AF.Exp, scale=-1.0)
                sb.inc(sc.activation(sc2, sc2, AF.Ln, bias=1.0), "S")
                sb.wait(ve, "S")
                ve.tensor_scalar_min(fz, f_c, 0.0)
                sb.inc(ve.tensor_tensor(fz, fz, sc2, ALU.subtract), "V")
                sb.inc(ve.scalar_tensor_tensor(sc2, f_c, 0.0, sc2, ALU.max, ALU.add), "V")
                sb.wait(sc, "V")
                sb.inc(sc.activation(k_c, sc2, AF.Exp, scale=-1.0), "S")
                if debug and oc == 0 and cc == 0:
                    sb.wait(sy, "V", "S", "T")
                    sb.inc(sy.dma_start(out=dbg["d_gl"], in_=fz), "DS", 16)
                for j in range(16):
                    sb.wait(te, "V")
                    sb.inc(te.matmul(p_cum, fz[:, j * 128:(j + 1) * 128], lt1,
                                     start=True, stop=True), "T")
                    sb.wait(ve, "T")
                    sb.inc(ve.tensor_copy(at_[:, j * 128:(j + 1) * 128], p_cum), "V")
                sb.wait(te, "S")
                for j in range(16):
                    pt = p_tr if j % 2 == 0 else p_tr2
                    sb.wait_val(te, "V", PST["tr"][j % 2])
                    sb.inc(te.transpose(pt, k_c[:, j * 128:(j + 1) * 128], identb), "T")
                    sb.wait(ve, "T")
                    sb.inc(ve.tensor_copy(kTc[:, j * 128:(j + 1) * 128], pt), "V")
                    PST["tr"][j % 2] = sb.c["V"]
                sb.wait(ve, "V")
                for j in range(16):
                    h = j
                    lastc = at_[:, j * 128 + 127: j * 128 + 128]
                    ve.tensor_scalar(cols[:, 0:1], lastc, -0.5, QSCALE, ALU.mult, ALU.add)
                    sb.inc(ve.tensor_scalar_mul(cols[:, 1:2], lastc, 0.5), "V")
                    sb.wait(sc, "V")
                    sb.inc(sc.activation(e_s, at_[:, j * 128:(j + 1) * 128], AF.Exp,
                                         bias=cols[:, 0:1]), "S")
                    sb.wait(ve, "S")
                    sb.inc(ve.tensor_tensor(qtl[:, j * 128:(j + 1) * 128],
                                            qT[:, j * OC + cc * 128: j * OC + cc * 128 + 128],
                                            e_s, ALU.mult), "V")
                    sb.wait(sc, "V")
                    sb.inc(sc.activation(e_s, at_[:, j * 128:(j + 1) * 128], AF.Exp,
                                         scale=-1.0, bias=cols[:, 1:2]), "S")
                    sb.wait(ve, "S")
                    sb.inc(ve.tensor_tensor(ktl[:, j * 128:(j + 1) * 128],
                                            kTc[:, j * 128:(j + 1) * 128], e_s, ALU.mult), "V")
                    sb.wait(sc, "V")
                    sb.inc(sc.activation(cols[:, 2:3], cols[:, 1:2], AF.Exp), "S")
                    sb.inc(sc.activation(cols[:, 3:4], lastc, AF.Exp), "S")
                    sb.wait(ve, "S")
                    sb.inc(ve.tensor_scalar(khT[:, j * 128:(j + 1) * 128],
                                            ktl[:, j * 128:(j + 1) * 128],
                                            cols[:, 2:3], None, ALU.mult), "V")
                    sb.inc(ve.tensor_scalar(stl[:, j * 128:(j + 1) * 128],
                                            s_all[:, j * 128:(j + 1) * 128],
                                            cols[:, 2:3], None, ALU.mult), "V")
                    sb.wait(te, "V")
                    sb.inc(te.transpose(p_tr, khT[:, j * 128:(j + 1) * 128], identb), "T")
                    sb.wait(ve, "T")
                    sb.inc(ve.tensor_copy(kh_tok[:, j * 128:(j + 1) * 128], p_tr), "V")
                    sb.wait(te, "V")
                    sb.inc(te.matmul(p_p, ktl[:, j * 128:(j + 1) * 128],
                                     qtl[:, j * 128:(j + 1) * 128],
                                     start=True, stop=True), "T")
                    sb.wait(ve, "T")
                    sb.inc(ve.select(pbuf, masku, p_p, zerob), "V")
                    sb.wait(te, "V")
                    te.matmul(p_o, v_tok[:, cc * D + h * 128: cc * D + h * 128 + 128],
                              pbuf, start=True, stop=False)
                    sb.inc(te.matmul(p_o, stl[:, j * 128:(j + 1) * 128],
                                     qtl[:, j * 128:(j + 1) * 128],
                                     start=False, stop=True), "T")
                    sb.wait(ve, "T")
                    sb.inc(ve.tensor_copy(o_all[:, h * 128:(h + 1) * 128], p_o), "V")
                    sb.wait(te, "V")
                    sb.inc(te.matmul(p_ds, kh_tok[:, j * 128:(j + 1) * 128],
                                     v_tok[:, cc * D + h * 128: cc * D + h * 128 + 128],
                                     start=True, stop=True), "T")
                    sb.wait(ve, "T")
                    sb.inc(ve.scalar_tensor_tensor(
                        s_all[:, j * 128:(j + 1) * 128],
                        s_all[:, j * 128:(j + 1) * 128],
                        cols[:, 3:4], p_ds, ALU.mult, ALU.add), "V")
                if debug and oc == 0 and cc == 0:
                    sb.wait(sy, "V", "S", "T")
                    sb.inc(sy.dma_start(out=dbg["d_at"], in_=at_), "DS", 16)
                    sb.inc(sy.dma_start(out=dbg["d_oall"], in_=o_all), "DS", 16)
                    sb.inc(sy.dma_start(out=dbg["d_sall"], in_=s_all), "DS", 16)
                sb.wait(sc, "V")
                sb.inc(sc.activation(sgp, o_all, AF.Square), "S")
                sb.inc(ve.memset(rtok, 0.0), "V")
                for nn in range(4):
                    sb.wait(te, "S", "V")
                    sb.inc(te.matmul(p_projB[0:1, 0:512], onesb, sgp[:, nn * 512:(nn + 1) * 512],
                                     start=True, stop=True), "T")
                    sb.wait(ve, "T")
                    sb.inc(ve.tensor_reduce(
                        rtok4, p_projB[0:1, 0:512].rearrange("o (h t) -> o t h", h=4), AX.X, ALU.add), "V")
                    sb.wait(ve, "V")
                    sb.inc(ve.tensor_tensor(rtok, rtok, rtok4, ALU.add), "V")
                    sb.wait(ve, "V")
                sb.inc(ve.tensor_scalar(rtok, rtok, 1.0 / D, EPS, ALU.mult, ALU.add), "V")
                sb.wait(sc, "V")
                sc.activation(rtok, rtok, AF.Ln)
                sb.inc(sc.activation(rtok, rtok, AF.Exp, scale=kn05[0:1, 0:1]), "S")
                sb.wait(te, "S", "V")
                sb.inc(te.matmul(p_projA[0:128, 0:128], onesr, rtok, start=True, stop=True), "T")
                sb.wait(ve, "T")
                sb.inc(ve.tensor_copy(rb, p_projA[0:128, 0:128]), "V")
                for h in range(16):
                    sb.inc(ve.scalar_tensor_tensor(
                        otg[:, cc * D + h * 128: cc * D + h * 128 + 128],
                        o_all[:, h * 128:(h + 1) * 128],
                        gw[:, h:h + 1], rb, ALU.mult, ALU.mult), "V")

            if debug and oc == 0:
                sb.wait(sy, "V", "S", "T")
                for nm, buf in (("d_qT", qT), ("d_fb", f_bf), ("d_k", k_tok),
                                ("d_v", v_tok), ("d_otg", otg)):
                    sb.inc(sy.dma_start(out=dbg[nm], in_=buf), "DS", 16)

            sb.wait(te, "V")
            PST["pe"] = [sb.c["V"], sb.c["V"]]
            load_w(wo_h, 0, 0)
            for blk in range(4):
                slot = blk % 2
                if blk + 1 < 4:
                    load_w(wo_h, (blk + 1) * 512, (blk + 1) % 2)
                wb = wbuf.rearrange("p (s kt m) -> p s kt m", s=2, m=512)[:, slot]
                for tt in range(NTT):
                    r0 = t0 + tt * 128
                    bank = PST["pb"]
                    PST["pb"] ^= 1
                    pp = p_projA if bank == 0 else p_projB
                    sb.wait_val(te, "DG", PST["wr"][slot])
                    sb.wait_val(te, "V", PST["pe"][bank])
                    for kt in range(16):
                        te_i = te.matmul(
                            pp[:, 0:512],
                            otg[:, tt * 2048 + kt * 128: tt * 2048 + kt * 128 + 128],
                            wb[:, kt, 0:512],
                            start=(kt == 0), stop=(kt == 15))
                    sb.inc(te_i, "T")
                    PST["wu"][slot] = sb.c["T"]
                    sb.wait(sy, "V", "DS")
                    sb.inc(sy.dma_start(out=xs, in_=x_h[r0:r0 + 128, blk * 512: blk * 512 + 512]), "DS", 16)
                    sb.wait(ve, "T", "DS")
                    sb.inc(ve.tensor_tensor(xs, pp[:, 0:512], xs, ALU.add), "V")
                    PST["pe"][bank] = sb.c["V"]
                    sb.wait(sy, "V")
                    sb.inc(sy.dma_start(out=y_h[r0:r0 + 128, blk * 512: blk * 512 + 512], in_=xs), "DS", 16)


def _prep_l1_attn_inputs(x_core, ins):
    cons = mk_consts()
    return dict(
        x=np.ascontiguousarray(x_core),
        Wq=ins["L1_attn_Wq"], Wf=ins["L1_attn_Wf"], Wi=ins["L1_attn_Wi"],
        Wo=ins["L1_attn_Wo"],
        gw=np.ascontiguousarray(ins["L1_attn_g_weight"].reshape(16, 128).T),
        **cons)


def emit_l0_ffn(nc, sb, E, T, debug=False):
    INTER = 8192

    x_h, y_h = E["x"], E["y"]
    wup_h, wgt_h, wdn_h = E["W_up"], E["W_gate"], E["W_down"]
    suug_h, sudp_h = E["su_ug"], E["su_dp"]
    wsup_h, wsgt_h, wsdp_h = E["ws_up"], E["ws_gt"], E["ws_dp"]
    ident_h, h128_h, h64_h = E["c_ident"], E["c_h128"], E["c_h64"]
    wupc = nc.dram_tensor("wupc", [INTER, D], BF16, kind="Internal").ap()
    wgtc = nc.dram_tensor("wgtc", [INTER, D], BF16, kind="Internal").ap()
    wdnc = nc.dram_tensor("wdnc", [D, INTER], BF16, kind="Internal").ap()
    wupT = nc.dram_tensor("wupT", [D, INTER], BF16,
                          kind="ExternalOutput" if DBG_FFN else "Internal").ap()
    wgtT = nc.dram_tensor("wgtT", [D, INTER], BF16, kind="Internal").ap()
    wh1 = nc.dram_tensor("wh1", [64, 128, D], BF16, kind="Internal").ap()
    dbg_lhs = (nc.dram_tensor("dbg_lhs", [128, 16 * 512], BF16, kind="ExternalOutput").ap()
               if DBG_FFN else None)
    wdns = nc.dram_tensor("wdns", [INTER, D], BF16,
                          kind="ExternalOutput" if DBG_FFN else "Internal").ap()
    x2s = nc.dram_tensor("x2s", [INTER, T], BF16,
                         kind="ExternalOutput" if DBG_FFN else "Internal").ap()
    with contextlib.ExitStack() as st:
        def sbuf(name, shape, dt):
            return st.enter_context(nc.sbuf_tensor("F0" + name, shape, dt)).ap()

        identb = sbuf("identb", [128, 128], BF16)
        h128b = sbuf("h128b", [128, 128], BF16)
        h64b = sbuf("h64b", [64, 64], BF16)
        su_ug = sbuf("su_ug_s", [128, 16], F32)
        su_dp = sbuf("su_dp_s", [64, 128], F32)
        ws_dp = sbuf("ws_dp_s", [128, 16], F32)
        ws_upg = sbuf("ws_upg", [128, 128], F32)
        cols = sbuf("cols", [128, 8], F32)
        kn05 = sbuf("kn05", [128, 1], F32)

        sy, gp, ve, sc, te = nc.sync, nc.gpsimd, nc.vector, nc.scalar, nc.tensor

        sb.inc(sy.dma_start(out=identb, in_=ident_h), "DS", 16)
        sb.inc(sy.dma_start(out=h128b, in_=h128_h), "DS", 16)
        sb.inc(sy.dma_start(out=h64b, in_=h64_h), "DS", 16)
        sb.inc(sy.dma_start(out=su_ug, in_=suug_h), "DS", 16)
        sb.inc(sy.dma_start(out=su_dp, in_=sudp_h), "DS", 16)
        sb.inc(sy.dma_start(out=ws_dp, in_=wsdp_h), "DS", 16)
        sb.inc(sy.dma_start(out=ws_upg[:, 0:64], in_=wsup_h), "DS", 16)
        sb.inc(sy.dma_start(out=ws_upg[:, 64:128], in_=wsgt_h), "DS", 16)
        sb.inc(ve.memset(kn05, -0.5), "V")
        sb.wait(ve, "DS")
        sb.inc(ve.tensor_scalar_mul(su_ug, su_ug, float(2048.0 ** -0.5)), "V")
        sb.inc(ve.tensor_scalar_mul(ws_dp, ws_dp, float(8192.0 ** -0.5)), "V")
        sb.wait(ve, "V")
        for _e in (te, sc, gp, sy):
            sb.wait(_e, "DS")

        with nc.sbuf_tensor("F0cast_s", [128, INTER], BF16) as cast_g:
            cast_s = cast_g.ap()
            for m, src, dstd, nt in (("up", wup_h, wupc, 64), ("gt", wgt_h, wgtc, 64),
                                     ("dn", wdn_h, wdnc, 16)):
                wdt = D if m != "dn" else INTER
                for t_i in range(nt):
                    sb.wait(gp, "S", "DG")
                    sb.inc(gp.dma_start(out=cast_s[:, 0:wdt],
                                        in_=src[t_i * 128:(t_i + 1) * 128, :]), "DG", 16)
                    sb.wait(sc, "DG")
                    if m == "up":
                        s_ap = ws_upg[:, t_i: t_i + 1]
                    elif m == "gt":
                        s_ap = ws_upg[:, 64 + t_i: 65 + t_i]
                    else:
                        s_ap = ws_dp[:, t_i: t_i + 1]
                    sb.inc(sc.activation(cast_s[:, 0:wdt], cast_s[:, 0:wdt], AF.Copy,
                                         scale=s_ap), "S")
                    sb.wait(gp, "S")
                    sb.inc(gp.dma_start(out=dstd[t_i * 128:(t_i + 1) * 128, :],
                                        in_=cast_s[:, 0:wdt]), "DG", 16)
            sb.wait(sy, "DG")
            for srcd, dstd in ((wupc, wupT), (wgtc, wgtT)):
                for ht in range(16):
                    sb.inc(sy.dma_start_transpose(
                        cast_s[:, 0:INTER], srcd[:, ht * 128:(ht + 1) * 128]), "DT", 16)
                    sb.wait(sy, "DT")
                    sb.inc(sy.dma_start(out=dstd[ht * 128:(ht + 1) * 128, :],
                                        in_=cast_s[:, 0:INTER]), "DT", 16)
                    sb.wait(sy, "DT")

        sb.fence(nc)
        with (nc.sbuf_tensor("F0f_in", [128, D], BF16) as f_in_g,
              nc.sbuf_tensor("F0f_h1", [128, D], BF16) as f_h1_g,
              nc.psum_tensor("F0p_w", [128, 512], F32) as p_w_g,
              nc.psum_tensor("F0p_f", [64, 512], F32) as p_f_g):
            f_in, f_h1 = f_in_g.ap(), f_h1_g.ap()
            p_w, p_f = p_w_g.ap(), p_f_g.ap()
            for it in range(64):
                sb.wait(sy, "DT", "V", "DG", "T")
                sb.inc(sy.dma_start_transpose(f_in, wdnc[:, it * 128:(it + 1) * 128]), "DT", 16)
                sb.wait(te, "DT", "V")
                for nn2 in range(4):
                    sb.wait(te, "V")
                    sb.inc(te.matmul(p_w, h128b, f_in[:, nn2 * 512:(nn2 + 1) * 512],
                                     start=True, stop=True), "T")
                    sb.wait(ve, "T", "DS")
                    sb.inc(ve.tensor_copy(f_h1[:, nn2 * 512:(nn2 + 1) * 512], p_w), "V")
                sb.wait(sy, "V")
                sb.inc(sy.dma_start(out=wh1[it], in_=f_h1), "DS", 16)
            with (nc.sbuf_tensor("F0f_rd", [64, 8 * D], BF16) as f_rd_g,
                  nc.sbuf_tensor("F0f_o", [64, 8 * D], BF16) as f_o_g):
                f_rd, f_o = f_rd_g.ap(), f_o_g.ap()
                for bg in range(16):
                    sb.wait(sy, "V", "DS")
                    sb.inc(sy.dma_start(
                        out=f_rd.rearrange("p (b d) -> p b d", b=8),
                        in_=wh1[:, bg * 8:(bg + 1) * 8, :]), "DS", 16)
                    for bs in range(8):
                        b = bg * 8 + bs
                        for dc in range(4):
                            sb.wait(te, "DS", "V")
                            sb.inc(te.matmul(p_f, h64b,
                                             f_rd[0:64, bs * D + dc * 512: bs * D + dc * 512 + 512],
                                             start=True, stop=True), "T")
                            sb.wait(ve, "T", "DS")
                            sb.inc(ve.tensor_scalar(
                                f_o[0:64, bs * D + dc * 512: bs * D + dc * 512 + 512],
                                p_f, su_dp[:, b:b + 1], None, ALU.mult), "V")
                    sb.wait(sy, "V")
                    sb.inc(sy.dma_start(
                        out=wdns.rearrange("(a b) d -> a b d", b=128)[:, bg * 8:(bg + 1) * 8, :],
                        in_=f_o.rearrange("p (b d) -> p b d", b=8)[0:64]), "DS", 16)
            if debug:
                sb.wait(sy, "DS")
                sb.inc(sy.dma_start(out=dbg["d_wdns"], in_=wdns[0:128, :]), "DS", 16)

        sb.fence(nc)
        with (nc.sbuf_tensor("F0x1", [128, 16 * T], BF16) as x1_g,
              nc.psum_tensor("F0p_a", [128, 512], F32) as p_a_g,
              nc.psum_tensor("F0p_tr", [128, 128], BF16) as p_tr_g):
            x1, p_a, p_tr = x1_g.ap(), p_a_g.ap(), p_tr_g.ap()
            with (nc.sbuf_tensor("F0x_tok", [128, D], F32) as xtok_g,
                  nc.sbuf_tensor("F0sq_f", [128, D], BF16) as sqf_g,
                  nc.sbuf_tensor("F0sq_s", [128, D], BF16) as sqs_g,
                  nc.sbuf_tensor("F0xsuT", [128, 16 * 512], BF16) as xsuT_g,
                  nc.sbuf_tensor("F0bb", [128, 16 * 256], F32) as bb_g,
                  nc.sbuf_tensor("F0btmp", [128, 8 * 256], F32) as btmp_g):
                x_tok, sq_f, sq_s, xsuT, bb, btmp = (
                    xtok_g.ap(), sqf_g.ap(), sqs_g.ap(), xsuT_g.ap(), bb_g.ap(), btmp_g.ap())
                for oc in range(T // 512):
                    for tt in range(4):
                        r0 = oc * 512 + tt * 128
                        sb.wait(sy, "S", "V", "T", "DS")
                        sb.inc(sy.dma_start(out=x_tok, in_=x_h[r0:r0 + 128, :]), "DS", 16)
                        sb.wait(sc, "DS")
                        sb.inc(sc.activation(sq_f, x_tok, AF.Square), "S")
                        sb.wait(ve, "DS")
                        sb.inc(ve.tensor_reduce(cols[:, 1:2], x_tok, AX.X, ALU.add), "V")
                        sb.wait(ve, "S")
                        sb.inc(ve.tensor_reduce(cols[:, 0:1], sq_f, AX.X, ALU.add), "V")
                        sb.wait(ve, "V")
                        ve.tensor_scalar_mul(cols[:, 2:3], cols[:, 1:2], 1.0 / D)
                        sb.inc(ve.tensor_scalar(cols[:, 3:4], cols[:, 0:1], 1.0 / D, EPS,
                                                ALU.mult, ALU.add), "V")
                        sb.wait(ve, "V")
                        sb.inc(ve.tensor_tensor(cols[:, 4:5], cols[:, 2:3], cols[:, 2:3], ALU.mult), "V")
                        sb.wait(ve, "V")
                        sb.inc(ve.tensor_tensor(cols[:, 5:6], cols[:, 3:4], cols[:, 4:5], ALU.subtract), "V")
                        sb.wait(sc, "V")
                        sc.activation(cols[:, 6:7], cols[:, 5:6], AF.Ln)
                        sb.inc(sc.activation(cols[:, 7:8], cols[:, 6:7], AF.Exp, scale=kn05[:, 0:1]), "S")
                        sb.wait(ve, "S")
                        sb.inc(ve.tensor_scalar(sq_s, x_tok, cols[:, 2:3], cols[:, 7:8],
                                                ALU.subtract, ALU.mult), "V")
                        for j in range(16):
                            sb.wait(te, "V")
                            sb.inc(te.transpose(p_tr, sq_s[:, j * 128:(j + 1) * 128], identb), "T")
                            sb.wait(ve, "T")
                            sb.inc(ve.tensor_scalar(
                                xsuT[:, j * 512 + tt * 128: j * 512 + tt * 128 + 128],
                                p_tr, su_ug[:, j:j + 1], None, ALU.mult), "V")
                    for hh2 in range(2):
                        c0 = hh2 * 256
                        for j in range(16):
                            sb.wait(te, "V")
                            sb.inc(te.matmul(p_a[:, 0:256], h128b,
                                             xsuT[:, j * 512 + c0: j * 512 + c0 + 256],
                                             start=True, stop=True), "T")
                            sb.wait(ve, "T")
                            sb.inc(ve.tensor_copy(bb[:, j * 256:(j + 1) * 256], p_a[:, 0:256]), "V")
                        for s4 in range(4):
                            stp = 1 << s4
                            a_ap = bb.rearrange("p (pr two w) -> p pr two w", two=2, w=stp * 256)[:, :, 0, :]
                            b_ap = bb.rearrange("p (pr two w) -> p pr two w", two=2, w=stp * 256)[:, :, 1, :]
                            t_ap = btmp.rearrange("p (pr w) -> p pr w", w=stp * 256)
                            sb.wait(ve, "V")
                            sb.inc(ve.tensor_tensor(t_ap, a_ap, b_ap, ALU.subtract), "V")
                            sb.wait(ve, "V")
                            if s4 == 3:
                                x1v = x1.rearrange("p (j t) -> p j t", t=T)
                                sb.inc(ve.tensor_tensor(
                                    x1v[:, 0:8, oc * 512 + c0: oc * 512 + c0 + 256], a_ap, b_ap, ALU.add), "V")
                                sb.inc(ve.tensor_copy(
                                    x1v[:, 8:16, oc * 512 + c0: oc * 512 + c0 + 256], t_ap), "V")
                            else:
                                sb.inc(ve.tensor_tensor(a_ap, a_ap, b_ap, ALU.add), "V")
                                sb.wait(ve, "V")
                                sb.inc(ve.tensor_copy(b_ap, t_ap), "V")
                if debug:
                    sb.wait(sy, "V")
                    sb.inc(sy.dma_start(out=dbg["d_x1"], in_=x1[:, 0:16 * 512]), "DS", 16)

            sb.fence(nc)
            with (nc.sbuf_tensor("F0lhs_up", [128, 2 * 16 * 512], BF16) as lup_g,
                  nc.sbuf_tensor("F0lhs_gt", [128, 2 * 16 * 512], BF16) as lgt_g,
                  nc.sbuf_tensor("F0sgb", [128, 2 * 512], BF16) as sgb_g,
                  nc.sbuf_tensor("F0m1b", [128, 2 * 512], F32) as m1b_g,
                  nc.sbuf_tensor("F0x2b", [128, 2 * 512], BF16) as x2b_g,
                  nc.psum_tensor("F0p_up0", [128, 512], F32) as pup0_g,
                  nc.psum_tensor("F0p_up1", [128, 512], F32) as pup1_g,
                  nc.psum_tensor("F0p_gt0", [128, 512], F32) as pgt0_g,
                  nc.psum_tensor("F0p_gt1", [128, 512], F32) as pgt1_g):
                lhs_upb, lhs_gtb, sgb2, m1b2, x2b2 = (lup_g.ap(), lgt_g.ap(), sgb_g.ap(),
                                                      m1b_g.ap(), x2b_g.ap())
                p_ups = [pup0_g.ap(), pup1_g.ap()]
                p_gts = [pgt0_g.ap(), pgt1_g.ap()]
                W2 = {"wu": [0, 0], "wr": [0, 0], "pe": [0, 0], "pb": 0}

                def load_ug(b16, slot):
                    sb.wait_val(gp, "T", W2["wu"][slot])
                    sb.wait(gp, "DT")
                    sb.inc(gp.dma_start(
                        out=lhs_upb.rearrange("p (s kt m) -> p s kt m", s=2, m=512)[:, slot],
                        in_=wupT[:, b16 * 512:(b16 + 1) * 512].rearrange("(kt p) m -> p kt m", p=128)), "DG", 16)
                    sb.inc(gp.dma_start(
                        out=lhs_gtb.rearrange("p (s kt m) -> p s kt m", s=2, m=512)[:, slot],
                        in_=wgtT[:, b16 * 512:(b16 + 1) * 512].rearrange("(kt p) m -> p kt m", p=128)), "DG", 16)
                    W2["wr"][slot] = sb.c["DG"]

                sb.wait(te, "V")
                load_ug(0, 0)
                for b16 in range(16):
                    slot = b16 % 2
                    if b16 + 1 < 16:
                        load_ug(b16 + 1, (b16 + 1) % 2)
                    lu = lhs_upb.rearrange("p (s kt m) -> p s kt m", s=2, m=512)[:, slot]
                    lg2 = lhs_gtb.rearrange("p (s kt m) -> p s kt m", s=2, m=512)[:, slot]
                    for mt in range(4):
                        for tc in range(T // 512):
                            bank = W2["pb"]
                            W2["pb"] ^= 1
                            p_up = p_ups[bank]
                            p_gt = p_gts[bank]
                            sgb = sgb2[:, bank * 512:(bank + 1) * 512]
                            m1b = m1b2[:, bank * 512:(bank + 1) * 512]
                            x2b = x2b2[:, bank * 512:(bank + 1) * 512]
                            sb.wait_val(te, "DG", W2["wr"][slot])
                            sb.wait_val(te, "V", W2["pe"][bank])
                            for kt in range(16):
                                te_i = te.matmul(
                                    p_up,
                                    lu[:, kt, mt * 128: mt * 128 + 128],
                                    x1[:, kt * T + tc * 512: kt * T + tc * 512 + 512],
                                    start=(kt == 0), stop=(kt == 15))
                            sb.inc(te_i, "T")
                            for kt in range(16):
                                te_i = te.matmul(
                                    p_gt,
                                    lg2[:, kt, mt * 128: mt * 128 + 128],
                                    x1[:, kt * T + tc * 512: kt * T + tc * 512 + 512],
                                    start=(kt == 0), stop=(kt == 15))
                            sb.inc(te_i, "T")
                            W2["wu"][slot] = sb.c["T"]
                            sb.wait(sc, "T")
                            sb.inc(sc.activation(sgb, p_gt, AF.Sigmoid), "S")
                            sb.wait(ve, "S")
                            sb.inc(ve.tensor_tensor(m1b, p_gt, sgb, ALU.mult), "V")
                            sb.wait(ve, "V", "DS")
                            sb.inc(ve.tensor_tensor(x2b, m1b, p_up, ALU.mult), "V")
                            W2["pe"][bank] = sb.c["V"]
                            sb.wait(sy, "V")
                            sb.inc(sy.dma_start(
                                out=x2s[b16 * 512 + mt * 128: b16 * 512 + mt * 128 + 128,
                                        tc * 512:(tc + 1) * 512],
                                in_=x2b), "DS", 16)
                            if debug and b16 == 0 and mt == 0 and tc == 0:
                                sb.wait(sy, "V", "T")
                                sb.inc(sy.dma_start(out=dbg["d_x2"], in_=x2b), "DS", 16)
                                sb.inc(sy.dma_start(out=dbg["d_up"], in_=m1b), "DS", 16)

        sb.fence(nc)
        with (nc.sbuf_tensor("F0x2sl", [128, 2 * 64 * 512], BF16) as x2sl_g,
              nc.sbuf_tensor("F0wdnb", [128, 2 * 1024], BF16) as wdnb_g,
              nc.sbuf_tensor("F0xsa", [128, 4 * 1024], F32) as xsa_g,
              contextlib.ExitStack() as st3):
            x2sl2, wdnb, xsa = x2sl_g.ap(), wdnb_g.ap(), xsa_g.ap()
            p_d = [st3.enter_context(nc.psum_tensor(f"F0p_d{i}", [128, 512], F32)).ap()
                   for i in range(8)]
            W3 = {"xu": [0, 0], "xr": [0, 0], "wu": [0, 0], "wr": [0, 0],
                  "ev": 0, "xsu": 0}

            def load_x2(tg, slot):
                sb.wait_val(sy, "T", W3["xu"][slot])
                sb.wait(sy, "DS")
                sb.inc(sy.dma_start(
                    out=x2sl2.rearrange("p (s kt t) -> p s kt t", s=2, t=512)[:, slot],
                    in_=x2s[:, tg * 512:(tg + 1) * 512].rearrange("(kt p) t -> p kt t", p=128)), "DS", 16)
                W3["xr"][slot] = sb.c["DS"]

            def load_wdn(kt, half, slot):
                sb.wait_val(gp, "T", W3["wu"][slot])
                sb.inc(gp.dma_start(
                    out=wdnb[:, slot * 1024: slot * 1024 + 1024],
                    in_=wdns[kt * 128:(kt + 1) * 128,
                             half * 1024: half * 1024 + 1024]), "DG", 16)
                W3["wr"][slot] = sb.c["DG"]

            load_x2(0, 0)
            for tg in range(T // 512):
                xslot = tg % 2
                if tg + 1 < T // 512:
                    load_x2(tg + 1, (tg + 1) % 2)
                x2sl = x2sl2.rearrange("p (s kt t) -> p s kt t", s=2, t=512)[:, xslot]
                sb.wait_val(te, "DS", W3["xr"][xslot])
                for half in range(2):
                    sb.wait_val(sy, "DS", W3["xsu"])
                    sb.inc(sy.dma_start(
                        out=xsa.rearrange("p (sub c) -> p sub c", c=1024),
                        in_=x_h[tg * 512:(tg + 1) * 512, half * 1024: half * 1024 + 1024]
                        .rearrange("(sub p) c -> p sub c", p=128)), "DS", 16)
                    xsa_ready = sb.c["DS"]
                    load_wdn(0, half, 0)
                    for kt in range(64):
                        wslot = kt % 2
                        if kt + 1 < 64:
                            load_wdn(kt + 1, half, (kt + 1) % 2)
                        sb.wait_val(te, "DG", W3["wr"][wslot])
                        if kt == 0:
                            sb.wait_val(te, "V", W3["ev"])
                        for sub in range(4):
                            for dc2 in range(2):
                                te_i = te.matmul(
                                    p_d[sub * 2 + dc2],
                                    x2sl[:, kt, sub * 128: sub * 128 + 128],
                                    wdnb[:, wslot * 1024 + dc2 * 512: wslot * 1024 + dc2 * 512 + 512],
                                    start=(kt == 0), stop=(kt == 63))
                        sb.inc(te_i, "T")
                        W3["wu"][wslot] = sb.c["T"]
                    W3["xu"][xslot] = sb.c["T"]
                    sb.wait(ve, "T")
                    sb.wait_val(ve, "DS", xsa_ready)
                    for sub in range(4):
                        for dc2 in range(2):
                            xsl = xsa.rearrange("p (sub c) -> p sub c", c=1024)[
                                :, sub, dc2 * 512: dc2 * 512 + 512]
                            ve_i = ve.tensor_tensor(xsl, p_d[sub * 2 + dc2], xsl, ALU.add)
                    sb.inc(ve_i, "V")
                    W3["ev"] = sb.c["V"]
                    sb.wait(sy, "V")
                    sb.inc(sy.dma_start(
                        out=y_h[tg * 512:(tg + 1) * 512, half * 1024: half * 1024 + 1024]
                        .rearrange("(sub p) c -> p sub c", p=128),
                        in_=xsa.rearrange("p (sub c) -> p sub c", c=1024)), "DS", 16)
                    W3["xsu"] = sb.c["DS"]


def _prep_l0_ffn_inputs(x_core, ins):
    cons = mk_consts()
    h128 = np.asarray(algebra_butterfly_eye(128), dtype=ml_dtypes.bfloat16)
    h64 = np.asarray(algebra_butterfly_eye(64), dtype=ml_dtypes.bfloat16)
    return dict(
        x=np.ascontiguousarray(x_core),
        W_up=ins["L0_ffn_W_up"], W_gate=ins["L0_ffn_W_gate"], W_down=ins["L0_ffn_W_down"],
        su_ug=np.ascontiguousarray(ins["L0_ffn_SU_ug"].reshape(16, 128).T),
        su_dp=np.ascontiguousarray(ins["L0_ffn_SU_dp"].reshape(64, 128)),
        ws_up=np.ascontiguousarray(ins["L0_ffn_Wscale_ug"][:8192].reshape(64, 128).T),
        ws_gt=np.ascontiguousarray(ins["L0_ffn_Wscale_ug"][8192:].reshape(64, 128).T),
        ws_dp=np.ascontiguousarray(ins["L0_ffn_Wscale_dp"].reshape(16, 128).T),
        c_ident=cons["c_ident"], c_h128=h128, c_h64=h64)


def algebra_butterfly_eye(n):
    x = np.eye(n, dtype=np.float32)
    h = 1
    while h < n:
        x = x.reshape(-1, n // (2 * h), 2, h)
        a, b = x[:, :, 0, :].copy(), x[:, :, 1, :].copy()
        x = np.stack([a + b, a - b], axis=2).reshape(-1, n)
        h *= 2
    return x


def emit_l1_ffn(nc, sb, E, T, debug=False):
    OC = 512
    n_oc = T // OC

    x_h, y8_h = E["x"], E["y8"]
    w1_h, w3_h, w2_h, wr_h, msk_h = E["w1"], E["w3"], E["w2"], E["wr"], E["opmask"]
    ident_h = E["c_ident"]
    w1c = nc.dram_tensor("w1c", [D, D], BF16, kind="Internal").ap()
    w3c = nc.dram_tensor("w3c", [D, D], BF16, kind="Internal").ap()
    w2c = nc.dram_tensor("w2c", [D, D], BF16, kind="Internal").ap()
    w1T = nc.dram_tensor("w1T", [D, D], BF16, kind="Internal").ap()
    w3T = nc.dram_tensor("w3T", [D, D], BF16, kind="Internal").ap()
    w2T = nc.dram_tensor("w2T", [D, D], BF16, kind="Internal").ap()
    cc_in = nc.dram_tensor("cc_in", [8, D], F32, kind="Internal")
    cc_out = nc.dram_tensor("cc_out", [8, D], F32, kind="Internal")
    with contextlib.ExitStack() as st:
        def sbuf(name, shape, dt):
            return st.enter_context(nc.sbuf_tensor("F1" + name, shape, dt)).ap()

        def psum(name, shape, dt=F32):
            return st.enter_context(nc.psum_tensor("F1" + name, shape, dt)).ap()

        identb = sbuf("identb", [128, 128], BF16)
        wrT = sbuf("wrT", [128, 16 * 8], BF16)
        opmask = sbuf("opmask_s", [8, 1], F32)
        cols = sbuf("cols", [128, 8], F32)
        kn05 = sbuf("kn05", [128, 1], F32)
        x_tok = sbuf("x_tok", [128, D], F32)
        sq_f = sbuf("sq_f", [128, D], F32)
        sq_s = sbuf("sq_s", [128, D], BF16)
        xlnT = sbuf("xlnT", [128, 16 * OC], BF16)
        h1T = sbuf("h1T", [128, 16 * OC], BF16)
        h2T = sbuf("h2T", [128, 16 * OC], BF16)
        h3b = sbuf("h3b", [128, 4 * D], BF16)
        wbuf = sbuf("wbuf", [128, 2 * 16 * 512], BF16)
        sgb = sbuf("sgb", [128, 2 * 512], BF16)
        lg = sbuf("lg", [128, 8], F32)
        lcols = sbuf("lcols", [128, 8], F32)
        mb = sbuf("mb", [128, 4 * 8], BF16)
        op_sb = sbuf("op_sb", [8, D], F32)
        x8 = sbuf("x8", [8, D], F32)

        p_projA = psum("p_projA", [128, 512], F32)
        p_projB = psum("p_projB", [128, 512], F32)
        p_proj = p_projA
        p_tr = psum("p_tr", [128, 128], BF16)
        p_lg = psum("p_lg", [128, 8], F32)
        p_op = [psum(f"p_op{i}", [8, 512], F32) for i in range(4)]

        sy, gp, ve, sc, te = nc.sync, nc.gpsimd, nc.vector, nc.scalar, nc.tensor

        sb.inc(sy.dma_start(out=identb, in_=ident_h), "DS", 16)
        sb.inc(sy.dma_start(out=opmask, in_=msk_h), "DS", 16)
        sb.inc(ve.memset(kn05, -0.5), "V")
        for _e in (te, sc, gp, ve):
            sb.wait(_e, "DS")

        with nc.sbuf_tensor("F1cast_s", [128, D], BF16) as cast_g:
            cast_s = cast_g.ap()
            for src, dstc in ((w1_h, w1c), (w3_h, w3c), (w2_h, w2c)):
                for t_i in range(16):
                    sb.wait(gp, "DG")
                    sb.inc(gp.dma_start(out=cast_s,
                                        in_=src[t_i * 128:(t_i + 1) * 128, :]), "DG", 16)
                    sb.wait(gp, "DG")
                    sb.inc(gp.dma_start(out=dstc[t_i * 128:(t_i + 1) * 128, :],
                                        in_=cast_s), "DG", 16)
            sb.wait(sy, "DG")
            for srcc, dstT in ((w1c, w1T), (w3c, w3T), (w2c, w2T)):
                for ht in range(16):
                    sb.inc(sy.dma_start_transpose(
                        cast_s, srcc[:, ht * 128:(ht + 1) * 128]), "DT", 16)
                    sb.wait(sy, "DT")
                    sb.inc(sy.dma_start(out=dstT[ht * 128:(ht + 1) * 128, :],
                                        in_=cast_s), "DT", 16)
                    sb.wait(sy, "DT")
            with nc.sbuf_tensor("F1wr8", [8, D], BF16) as wr8_g:
                wr8 = wr8_g.ap()
                sb.wait(gp, "DT")
                sb.inc(gp.dma_start(out=wr8, in_=wr_h), "DG", 16)
                for kt in range(16):
                    sb.wait(te, "DG", "V")
                    sb.inc(te.transpose(p_tr[0:128, 0:8], wr8[:, kt * 128:(kt + 1) * 128],
                                        identb[0:8, 0:8]), "T")
                    sb.wait(ve, "T")
                    sb.inc(ve.tensor_copy(wrT[:, kt * 8:(kt + 1) * 8], p_tr[0:128, 0:8]), "V")

        sb.fence(nc)
        PSF = {"wu": [0, 0], "wr": [0, 0], "pe": [0, 0], "pb": 0}
        for oc in range(n_oc):
            t0 = oc * OC
            for tt in range(4):
                r0 = t0 + tt * 128
                sb.wait(sy, "S", "V", "T", "DS")
                sb.inc(sy.dma_start(out=x_tok, in_=x_h[r0:r0 + 128, :]), "DS", 16)
                sb.wait(sc, "DS")
                sb.inc(sc.activation(sq_f, x_tok, AF.Square), "S")
                sb.wait(ve, "DS")
                sb.inc(ve.tensor_reduce(cols[:, 1:2], x_tok, AX.X, ALU.add), "V")
                sb.wait(ve, "S")
                sb.inc(ve.tensor_reduce(cols[:, 0:1], sq_f, AX.X, ALU.add), "V")
                sb.wait(ve, "V")
                ve.tensor_scalar_mul(cols[:, 2:3], cols[:, 1:2], 1.0 / D)
                sb.inc(ve.tensor_scalar(cols[:, 3:4], cols[:, 0:1], 1.0 / D, EPS, ALU.mult, ALU.add), "V")
                sb.wait(ve, "V")
                sb.inc(ve.tensor_tensor(cols[:, 4:5], cols[:, 2:3], cols[:, 2:3], ALU.mult), "V")
                sb.wait(ve, "V")
                sb.inc(ve.tensor_tensor(cols[:, 5:6], cols[:, 3:4], cols[:, 4:5], ALU.subtract), "V")
                sb.wait(sc, "V")
                sc.activation(cols[:, 6:7], cols[:, 5:6], AF.Ln)
                sb.inc(sc.activation(cols[:, 7:8], cols[:, 6:7], AF.Exp, scale=kn05[:, 0:1]), "S")
                sb.wait(ve, "S")
                sb.inc(ve.tensor_scalar(sq_s, x_tok, cols[:, 2:3], cols[:, 7:8],
                                        ALU.subtract, ALU.mult), "V")
                for j in range(16):
                    sb.wait(te, "V")
                    sb.inc(te.transpose(p_tr, sq_s[:, j * 128:(j + 1) * 128], identb), "T")
                    sb.wait(ve, "T")
                    sb.inc(ve.tensor_copy(xlnT[:, j * OC + tt * 128: j * OC + tt * 128 + 128], p_tr), "V")

            for tt in range(4):
                sb.wait(te, "V")
                for kt in range(16):
                    te_i = te.matmul(p_lg, xlnT[:, kt * OC + tt * 128: kt * OC + tt * 128 + 128],
                                     wrT[:, kt * 8:(kt + 1) * 8],
                                     start=(kt == 0), stop=(kt == 15))
                sb.inc(te_i, "T")
                sb.wait(ve, "T")
                sb.inc(ve.tensor_copy(lg, p_lg), "V")
                sb.wait(ve, "V")
                sb.inc(ve.tensor_reduce(lcols[:, 0:1], lg, AX.X, ALU.max), "V")
                sb.wait(ve, "V")
                sb.inc(ve.tensor_scalar(lcols[:, 2:3], lcols[:, 0:1], 1.0, None, ALU.mult), "V")
                sb.wait(ve, "V")
                sb.inc(ve.tensor_scalar(sq_f[:, 0:8], lg, lcols[:, 0:1], None, ALU.is_ge), "V")
                sb.wait(ve, "V")
                sb.inc(ve.scalar_tensor_tensor(sq_f[:, 8:16], sq_f[:, 0:8], -1.0e30, lg,
                                               ALU.mult, ALU.add), "V")
                sb.wait(ve, "V")
                sb.inc(ve.tensor_reduce(lcols[:, 1:2], sq_f[:, 8:16], AX.X, ALU.max), "V")
                sb.wait(ve, "V")
                sb.inc(ve.tensor_scalar(sq_f[:, 16:24], sq_f[:, 8:16], lcols[:, 1:2], None, ALU.is_ge), "V")
                sb.inc(ve.tensor_tensor(lcols[:, 3:4], lcols[:, 0:1], lcols[:, 1:2], ALU.subtract), "V")
                sb.wait(sc, "V")
                sb.inc(sc.activation(lcols[:, 4:5], lcols[:, 3:4], AF.Sigmoid), "S")
                sb.wait(ve, "S")
                sb.inc(ve.tensor_scalar(lcols[:, 5:6], lcols[:, 4:5], -1.0, 1.0, ALU.mult, ALU.add), "V")
                sb.wait(ve, "V")
                sb.inc(ve.tensor_scalar(sq_f[:, 24:32], sq_f[:, 0:8], lcols[:, 4:5], None, ALU.mult), "V")
                sb.wait(ve, "V")
                sb.inc(ve.scalar_tensor_tensor(mb[:, tt * 8:(tt + 1) * 8], sq_f[:, 16:24],
                                               lcols[:, 5:6], sq_f[:, 24:32], ALU.mult, ALU.add), "V")
                if debug and oc == 0 and tt == 0:
                    sb.wait(sy, "V", "S", "T")
                    sb.inc(sy.dma_start(out=dbg["d_lg"], in_=lg), "DS", 16)
                    sb.inc(sy.dma_start(out=dbg["d_M"], in_=mb[:, 0:8]), "DS", 16)

            def load_wl(wsrc, blk, slot):
                sb.wait_val(gp, "T", PSF["wu"][slot])
                sb.wait(gp, "DT")
                sb.inc(gp.dma_start(
                    out=wbuf.rearrange("p (s kt m) -> p s kt m", s=2, m=512)[:, slot],
                    in_=wsrc[:, blk * 512:(blk + 1) * 512].rearrange("(kt p) m -> p kt m", p=128)), "DG", 16)
                PSF["wr"][slot] = sb.c["DG"]

            def hproj(srcT, dstT_buf, wsrc, sig):
                sb.wait(te, "V")
                load_wl(wsrc, 0, 0)
                for blk in range(4):
                    slot = blk % 2
                    if blk + 1 < 4:
                        load_wl(wsrc, blk + 1, (blk + 1) % 2)
                    wb = wbuf.rearrange("p (s kt m) -> p s kt m", s=2, m=512)[:, slot]
                    for mt in range(4):
                        bank = PSF["pb"]
                        PSF["pb"] ^= 1
                        pp = p_projA if bank == 0 else p_projB
                        sb.wait_val(te, "DG", PSF["wr"][slot])
                        sb.wait_val(te, "V", PSF["pe"][bank])
                        for kt in range(16):
                            if sig:
                                te_i = te.matmul(pp[:, 0:512],
                                                 wb[:, kt, mt * 128: mt * 128 + 128],
                                                 srcT[:, kt * OC: kt * OC + 512],
                                                 start=(kt == 0), stop=(kt == 15))
                            else:
                                te_i = te.matmul(pp[:, 0:512],
                                                 srcT[:, kt * OC + mt * 128: kt * OC + mt * 128 + 128],
                                                 wb[:, kt, 0:512],
                                                 start=(kt == 0), stop=(kt == 15))
                        sb.inc(te_i, "T")
                        PSF["wu"][slot] = sb.c["T"]
                        if sig:
                            sb.wait(sc, "T")
                            sgbk = sgb[:, bank * 512: bank * 512 + 512]
                            sb.inc(sc.activation(sgbk, pp[:, 0:512], AF.Sigmoid), "S")
                            sb.wait(ve, "S")
                            sb.inc(ve.tensor_tensor(
                                dstT_buf[:, (blk * 4 + mt) * OC: (blk * 4 + mt) * OC + 512],
                                pp[:, 0:512], sgbk, ALU.mult), "V")
                        else:
                            sb.wait(ve, "T")
                            sb.inc(ve.tensor_copy(
                                dstT_buf[:, mt * D + blk * 512: mt * D + blk * 512 + 512],
                                pp[:, 0:512]), "V")
                        PSF["pe"][bank] = sb.c["V"]

            hproj(xlnT, h1T, w1T, True)
            hproj(h1T, h2T, w3T, True)
            hproj(h2T, h3b, w2T, False)
            for tt in range(4):
                sb.wait(te, "V")
                for dc in range(4):
                    te_i = te.matmul(p_op[dc], mb[:, tt * 8:(tt + 1) * 8],
                                     h3b[:, tt * D + dc * 512: tt * D + dc * 512 + 512],
                                     start=(oc == 0 and tt == 0),
                                     stop=(oc == n_oc - 1 and tt == 3))
                sb.inc(te_i, "T")
            if debug and oc == 0:
                sb.wait(sy, "V", "S", "T")
                sb.inc(sy.dma_start(out=dbg["d_h1"], in_=h1T[:, 0:512]), "DS", 16)
                sb.inc(sy.dma_start(out=dbg["d_h3"], in_=h3b[:, 0:2048]), "DS", 16)

        sb.wait(ve, "T")
        for dc in range(4):
            sb.inc(ve.tensor_copy(op_sb[:, dc * 512:(dc + 1) * 512], p_op[dc]), "V")
        sb.wait(gp, "V")
        sb.inc(gp.dma_start(out=cc_in.ap(), in_=op_sb), "DG", 16)
        sb.wait(gp, "DG")
        gp.collective_compute("AllReduce", ALU.add, replica_groups=[list(range(8))],
                              ins=[cc_in.ap().opt()], outs=[cc_out.ap().opt()]).then_inc(sb.h["CC"], 1)
        sb.c["CC"] += 1
        sb.wait(gp, "CC")
        sb.inc(gp.dma_start(out=op_sb, in_=cc_out.ap()), "DG", 16)
        sb.wait(sy, "DS")
        sb.inc(sy.dma_start(out=x8, in_=x_h[0:8, :]), "DS", 16)
        sb.wait(ve, "DG", "DS")
        sb.inc(ve.scalar_tensor_tensor(x8, op_sb, opmask[:, 0:1], x8, ALU.mult, ALU.add), "V")
        if debug:
            sb.wait(sy, "V")
            sb.inc(sy.dma_start(out=dbg["d_op"], in_=op_sb), "DS", 16)
        sb.wait(sy, "V")
        sb.inc(sy.dma_start(out=y8_h, in_=x8), "DS", 16)


def _prep_l1_ffn_inputs(x_core, ins, core_id):
    cons = mk_consts()
    return dict(
        x=np.ascontiguousarray(x_core),
        w1=ins["L1_ffn_w1"], w3=ins["L1_ffn_w3"], w2=ins["L1_ffn_w2"],
        wr=ins["L1_ffn_wr"],
        opmask=np.full((8, 1), 1.0 if core_id == 0 else 0.0, np.float32),
        c_ident=cons["c_ident"])


_BUILD_CACHE = {}


def build_fused(T=4096, dbg_mid=False, parts=(0, 1, 2, 3)):
    nc = bass.Bass("TRN2", target_bir_lowering=False)

    def inp(name, shape, dt=F32):
        return nc.dram_tensor(name, shape, dt, kind="ExternalInput").ap()

    x0 = inp("L0_attn_x", [T, D])
    out = nc.dram_tensor("out", [T, D], F32, kind="ExternalOutput").ap()
    y8 = nc.dram_tensor("y8", [8, D], F32, kind="ExternalOutput").ap()
    mk = "ExternalOutput" if dbg_mid else "Internal"
    x1s = nc.dram_tensor("x1s", [T, D], F32, kind=mk).ap()
    x2s_r = nc.dram_tensor("x2s_r", [T, D], F32, kind=mk).ap()

    cons = dict(c_ident=inp("c_ident", [128, 128], BF16),
                c_masku8=inp("c_masku8", [128, 128], U8),
                c_lt1=inp("c_lt1", [128, 128], F32))

    with contextlib.ExitStack() as st:
        sb = SB(nc, st, ["T", "V", "S", "DS", "DG", "DT", "CC"])
        if 0 in parts:
            emit_l0_attn(nc, sb, dict(
            x=x0, y=x1s, Wq=inp("L0_attn_Wq", [D, 1024]), Wk=inp("L0_attn_Wk", [D, 1024]),
            Wv=inp("L0_attn_Wv", [D, D]), Wg=inp("L0_attn_Wg", [D, D]),
            Wgk1=inp("L0_attn_Wgk1", [D, 16]), Wgk2b=inp("p_wgk2b", [17, 1024]),
            normw=inp("p_normw", [128, 1]), Wo=inp("L0_attn_Wo", [D, D]), **cons), T)
        if 1 in parts:
            sb.fence(nc)
            emit_l0_ffn(nc, sb, dict(
            x=x1s, y=x2s_r, W_up=inp("L0_ffn_W_up", [8192, D]),
            W_gate=inp("L0_ffn_W_gate", [8192, D]), W_down=inp("L0_ffn_W_down", [D, 8192]),
            su_ug=inp("p_suug", [128, 16]), su_dp=inp("p_sudp", [64, 128]),
            ws_up=inp("p_wsup", [128, 64]), ws_gt=inp("p_wsgt", [128, 64]),
            ws_dp=inp("p_wsdp", [128, 16]), c_ident=cons["c_ident"],
            c_h128=inp("c_h128", [128, 128], BF16), c_h64=inp("c_h64", [64, 64], BF16)), T)
        if 2 in parts:
            sb.fence(nc)
            emit_l1_attn(nc, sb, dict(
            x=x2s_r, y=out, Wq=inp("L1_attn_Wq", [D, D]), Wf=inp("L1_attn_Wf", [D, D]),
            Wi=inp("L1_attn_Wi", [D, D]), Wo=inp("L1_attn_Wo", [D, D]),
            gw=inp("p_gw", [128, 16]), **cons), T)
        if 3 in parts:
            sb.fence(nc)
            emit_l1_ffn(nc, sb, dict(
            x=out, y8=y8, w1=inp("L1_ffn_w1", [D, D]), w3=inp("L1_ffn_w3", [D, D]),
            w2=inp("L1_ffn_w2", [D, D]), wr=inp("L1_ffn_wr", [8, D]),
            opmask=inp("p_opmask", [8, 1]), c_ident=cons["c_ident"]), T)
        nc.sync.wait_ge(sb.h["DS"], sb.c["DS"])
        if sb.c["DG"]:
            nc.sync.wait_ge(sb.h["DG"], sb.c["DG"])
        if sb.c["DT"]:
            nc.sync.wait_ge(sb.h["DT"], sb.c["DT"])
    return nc


def _prep_fused_inputs(x_core, ins, core_id):
    cons = mk_consts()
    h128 = np.asarray(algebra_butterfly_eye(128), dtype=ml_dtypes.bfloat16)
    h64 = np.asarray(algebra_butterfly_eye(64), dtype=ml_dtypes.bfloat16)
    d = {k: ins[k] for k in (
        "L0_attn_Wq", "L0_attn_Wk", "L0_attn_Wv", "L0_attn_Wg", "L0_attn_Wgk1",
        "L0_attn_Wo", "L0_ffn_W_up", "L0_ffn_W_gate", "L0_ffn_W_down",
        "L1_attn_Wq", "L1_attn_Wf", "L1_attn_Wi", "L1_attn_Wo",
        "L1_ffn_w1", "L1_ffn_w3", "L1_ffn_w2", "L1_ffn_wr")}
    d.update(
        L0_attn_x=np.ascontiguousarray(x_core),
        p_wgk2b=np.ascontiguousarray(np.concatenate(
            [ins["L0_attn_Wgk2"], ins["L0_attn_bgk2"][None, :]], axis=0)),
        p_normw=np.ascontiguousarray(ins["L0_attn_norm_w"][:, None]),
        p_suug=np.ascontiguousarray(ins["L0_ffn_SU_ug"].reshape(16, 128).T),
        p_sudp=np.ascontiguousarray(ins["L0_ffn_SU_dp"].reshape(64, 128)),
        p_wsup=np.ascontiguousarray(ins["L0_ffn_Wscale_ug"][:8192].reshape(64, 128).T),
        p_wsgt=np.ascontiguousarray(ins["L0_ffn_Wscale_ug"][8192:].reshape(64, 128).T),
        p_wsdp=np.ascontiguousarray(ins["L0_ffn_Wscale_dp"].reshape(16, 128).T),
        p_gw=np.ascontiguousarray(ins["L1_attn_g_weight"].reshape(16, 128).T),
        p_opmask=np.full((8, 1), 1.0 if core_id == 0 else 0.0, np.float32),
        c_h128=h128, c_h64=h64, **cons)
    return d


def kernel(**inputs):
    x = np.ascontiguousarray(np.asarray(inputs["L0_attn_x"], np.float32))
    B, T, _ = x.shape
    assert B == NCORES
    key = ("fused", T)
    if key not in _BUILD_CACHE:
        _BUILD_CACHE[key] = build_fused(T=T)
    nc = _BUILD_CACHE[key]
    in_maps = [_prep_fused_inputs(x[c], inputs, c) for c in range(NCORES)]
    r = run_bass_kernel_spmd(nc, in_maps, core_ids=list(range(NCORES)))
    out = np.stack([np.asarray(r.results[c]["out"], np.float32) for c in range(NCORES)], axis=0)
    out[0, 0:8, :] = np.asarray(r.results[0]["y8"], np.float32)
    return out


# version 07 (final)
# speedup vs baseline: 1.5318x; 1.1032x over previous
import contextlib
import numpy as np
import ml_dtypes
import concourse.bass as bass
import concourse.mybir as mybir
from concourse.bass_utils import run_bass_kernel_spmd

F32 = mybir.dt.float32
BF16 = mybir.dt.bfloat16
U8 = mybir.dt.uint8
AF = mybir.ActivationFunctionType
ALU = mybir.AluOpType
AX = mybir.AxisListType

NCORES = 8
DBG_FFN = False
D = 2048
EPS = 1e-5


class SB:

    def __init__(self, nc, stack, names):
        self.nc = nc
        self.h = {}
        self.c = {}
        self.last = {}
        for n in names:
            self.h[n] = stack.enter_context(nc.semaphore(n))
            self.c[n] = 0

    def inc(self, inst, name, n=1):
        inst.then_inc(self.h[name], n)
        self.c[name] += n
        return inst

    def fence(self, nc):
        for eng in (nc.sync, nc.gpsimd, nc.vector, nc.scalar, nc.tensor):
            self.wait(eng, *self.c.keys())

    def wait_val(self, eng, name, val):
        key = (id(eng), name)
        if val > 0 and self.last.get(key, -1) < val:
            eng.wait_ge(self.h[name], val)
            self.last[key] = val

    def wait(self, eng, *names):
        for nm in names:
            cnt = self.c[nm]
            key = (id(eng), nm)
            if cnt > 0 and self.last.get(key, -1) < cnt:
                eng.wait_ge(self.h[nm], cnt)
                self.last[key] = cnt


def mk_consts():
    ident = np.eye(128, dtype=ml_dtypes.bfloat16)
    masku8 = np.triu(np.ones((128, 128), np.uint8))
    lt1 = np.triu(np.ones((128, 128), np.float32))
    return {"c_ident": ident, "c_masku8": masku8, "c_lt1": lt1}


def emit_l0_attn(nc, sb, E, T, debug=False):
    OC = 512
    n_oc = T // OC
    NTT = OC // 128
    H = 16
    DK, DV = 64, 128
    KD = 1024
    QSCALE = float(np.log(DK ** (-0.5)))

    x_h, y_h = E["x"], E["y"]
    wq_h, wk_h, wv_h, wg_h = E["Wq"], E["Wk"], E["Wv"], E["Wg"]
    wgk1_h, wgk2b_h, normw_h, wo_h = E["Wgk1"], E["Wgk2b"], E["normw"], E["Wo"]
    ident_h, mask_h, lt1_h = E["c_ident"], E["c_masku8"], E["c_lt1"]
    with contextlib.ExitStack() as st:
        def sbuf(name, shape, dt):
            return st.enter_context(nc.sbuf_tensor("A0" + name, shape, dt)).ap()

        def psum(name, shape, dt=F32):
            return st.enter_context(nc.psum_tensor("A0" + name, shape, dt)).ap()

        identb = sbuf("identb", [128, 128], BF16)
        masku = sbuf("masku", [128, 128], U8)
        lt16 = sbuf("lt16", [128, 128], F32)
        normw = sbuf("normw_s", [128, 1], F32)
        onesb = sbuf("onesb", [128, 1], BF16)
        onesf = sbuf("onesf", [128, 1], F32)
        onesr = sbuf("onesr", [1, 128], F32)
        zerob = sbuf("zerob", [128, 128], BF16)
        cols = sbuf("cols", [128, 8], F32)
        kn05 = sbuf("kn05", [128, 1], F32)
        x_tok = sbuf("x_tok", [128, D], F32)
        sq_f = sbuf("sq_f", [128, D], F32)
        sq_s = sbuf("sq_s", [128, D], BF16)
        xlnT = sbuf("xlnT", [128, 16 * OC], BF16)
        qT = sbuf("qT", [128, 8 * OC], BF16)
        kT = sbuf("kT", [128, 8 * OC], BF16)
        gT = sbuf("gT", [128, 16 * OC], BF16)
        sgp = sbuf("sgp", [128, 2048], BF16)
        v_tok = sbuf("v_tok", [128, NTT * D], BF16)
        t1b = sbuf("t1b", [17, OC], BF16)
        z_tok = sbuf("z_tok", [128, KD], F32)
        sc2 = sbuf("sc2", [128, KD], F32)
        g_tok = sbuf("g_tok", [128, KD], F32)
        at_ = sbuf("at", [128, 8 * 128], F32)
        e_s = sbuf("e_s", [128, 128], F32)
        qtl = sbuf("qtl", [128, 8 * 128], BF16)
        ktl = sbuf("ktl", [128, 8 * 128], BF16)
        khT = sbuf("khT", [128, 8 * 128], BF16)
        kh_tok = sbuf("kh_tok", [128, 8 * 128], BF16)
        pbuf = sbuf("pbuf", [128, 128], BF16)
        stl = sbuf("stl", [128, 8 * 128], BF16)
        s_all = sbuf("s_all", [128, 8 * 128], F32)
        o_all = sbuf("o_all", [128, 2048], F32)
        rstd = sbuf("rstd", [1, 2048], F32)
        rb = sbuf("rb", [128, 512], F32)
        otg = sbuf("otg", [128, NTT * 2048], BF16)
        wbuf = sbuf("wbuf", [128, 2 * 16 * 512], BF16)
        xs = sbuf("xs", [128, 512], F32)
        wgk1b = sbuf("wgk1b", [128, 16 * 16], BF16)
        wgk2b = sbuf("wgk2b", [17, KD], BF16)

        p_projA = psum("p_projA", [128, 512], F32)
        p_projB = psum("p_projB", [128, 512], F32)
        p_proj = p_projA
        p_tr = psum("p_tr", [128, 128], BF16)
        p_tr2 = psum("p_tr2", [128, 128], BF16)
        p_cum = psum("p_cum", [128, 128], F32)
        p_p = psum("p_p", [128, 128], F32)
        p_o = psum("p_o", [128, 128], F32)
        p_ds = psum("p_ds", [64, 128], F32)

        sy, gp, ve, sc, te = nc.sync, nc.gpsimd, nc.vector, nc.scalar, nc.tensor

        sb.inc(sy.dma_start(out=identb, in_=ident_h), "DS", 16)
        sb.inc(sy.dma_start(out=masku, in_=mask_h), "DS", 16)
        sb.inc(sy.dma_start(out=lt16, in_=lt1_h), "DS", 16)
        sb.inc(sy.dma_start(out=normw, in_=normw_h), "DS", 16)
        sb.inc(gp.dma_start(out=wgk1b.rearrange("p (kt m) -> p kt m", m=16),
                            in_=wgk1_h.rearrange("(kt p) m -> p kt m", p=128)), "DG", 16)
        sb.inc(gp.dma_start(out=wgk2b, in_=wgk2b_h), "DG", 16)
        sb.inc(ve.memset(kn05, -0.5), "V")
        sb.inc(ve.memset(onesb, 1.0), "V")
        sb.inc(ve.memset(onesf, 1.0), "V")
        sb.inc(ve.memset(onesr, 1.0), "V")
        sb.inc(ve.memset(zerob, 0.0), "V")
        sb.inc(ve.memset(t1b, 1.0), "V")
        sb.inc(ve.memset(s_all, 0.0), "V")
        sb.wait(ve, "DS")
        sb.inc(ve.tensor_scalar_mul(lt16, lt16, 1.0 / 16.0), "V")
        for _e in (te, sc, gp, sy):
            sb.wait(_e, "DS", "DG")
        PST = {"tr": [0, 0], "wu": [0, 0], "wr": [0, 0], "pe": [0, 0], "pb": 0}

        for oc in range(n_oc):
            t0 = oc * OC
            for tt in range(NTT):
                r0 = t0 + tt * 128
                sb.wait(sy, "S", "V", "T", "DS")
                sb.inc(sy.dma_start(out=x_tok, in_=x_h[r0:r0 + 128, :]), "DS", 16)
                sb.wait(sc, "DS")
                sb.inc(sc.activation(sq_f, x_tok, AF.Square), "S")
                sb.wait(ve, "DS")
                sb.inc(ve.tensor_reduce(cols[:, 1:2], x_tok, AX.X, ALU.add), "V")
                sb.wait(ve, "S")
                sb.inc(ve.tensor_reduce(cols[:, 0:1], sq_f, AX.X, ALU.add), "V")
                sb.wait(ve, "V")
                ve.tensor_scalar_mul(cols[:, 2:3], cols[:, 1:2], 1.0 / D)
                sb.inc(ve.tensor_scalar(cols[:, 3:4], cols[:, 0:1], 1.0 / D, EPS, ALU.mult, ALU.add), "V")
                sb.wait(ve, "V")
                sb.inc(ve.tensor_tensor(cols[:, 4:5], cols[:, 2:3], cols[:, 2:3], ALU.mult), "V")
                sb.wait(ve, "V")
                sb.inc(ve.tensor_tensor(cols[:, 5:6], cols[:, 3:4], cols[:, 4:5], ALU.subtract), "V")
                sb.wait(sc, "V")
                sc.activation(cols[:, 6:7], cols[:, 5:6], AF.Ln)
                sb.inc(sc.activation(cols[:, 7:8], cols[:, 6:7], AF.Exp, scale=kn05[:, 0:1]), "S")
                sb.wait(ve, "S")
                sb.inc(ve.tensor_scalar(sq_s, x_tok, cols[:, 2:3], cols[:, 7:8],
                                        ALU.subtract, ALU.mult), "V")
                if debug and oc == 0 and tt == 0:
                    sb.wait(sy, "V", "S", "T")
                    sb.inc(sy.dma_start(out=dbg["d_cols"], in_=cols), "DS", 16)
                    sb.inc(sy.dma_start(out=dbg["d_xtok"], in_=x_tok), "DS", 16)
                sb.wait(te, "V")
                for j in range(16):
                    pt = p_tr if j % 2 == 0 else p_tr2
                    sb.wait_val(te, "V", PST["tr"][j % 2])
                    sb.inc(te.transpose(pt, sq_s[:, j * 128:(j + 1) * 128], identb), "T")
                    sb.wait(ve, "T")
                    sb.inc(ve.tensor_copy(xlnT[:, j * OC + tt * 128: j * OC + tt * 128 + 128], pt), "V")
                    PST["tr"][j % 2] = sb.c["V"]

            if debug and oc == 0:
                sb.wait(sy, "V", "S", "T")
                sb.inc(sy.dma_start(out=dbg["d_xlnT"], in_=xlnT), "DS", 16)
            def load_w(src_ap, m0, slot):
                sb.wait_val(gp, "T", PST["wu"][slot])
                sb.inc(gp.dma_start(
                    out=wbuf.rearrange("p (s kt m) -> p s kt m", s=2, m=512)[:, slot],
                    in_=src_ap[:, m0:m0 + 512].rearrange("(kt p) m -> p kt m", p=128)), "DG", 16)
                PST["wr"][slot] = sb.c["DG"]

            def run_jobs(jobs):
                load_w(*jobs[0][:2], 0)
                for idx, (wsrc, m0, kind, dst, dcol0) in enumerate(jobs):
                    slot = idx % 2
                    if idx + 1 < len(jobs):
                        load_w(*jobs[idx + 1][:2], (idx + 1) % 2)
                    wb = wbuf.rearrange("p (s kt m) -> p s kt m", s=2, m=512)[:, slot]
                    for mt in range(4):
                        bank = PST["pb"]
                        PST["pb"] ^= 1
                        pp = p_projA if bank == 0 else p_projB
                        sb.wait_val(te, "DG", PST["wr"][slot])
                        sb.wait_val(te, "V", PST["pe"][bank])
                        for kt in range(16):
                            if kind == "fm":
                                te_i = te.matmul(pp[:, 0:512],
                                                 wb[:, kt, mt * 128: mt * 128 + 128],
                                                 xlnT[:, kt * OC: kt * OC + 512],
                                                 start=(kt == 0), stop=(kt == 15))
                            else:
                                te_i = te.matmul(pp[:, 0:512],
                                                 xlnT[:, kt * OC + mt * 128: kt * OC + mt * 128 + 128],
                                                 wb[:, kt, 0:512],
                                                 start=(kt == 0), stop=(kt == 15))
                        sb.inc(te_i, "T")
                        PST["wu"][slot] = sb.c["T"]
                        sb.wait(ve, "T")
                        if kind == "fm":
                            sb.inc(ve.tensor_copy(
                                dst[:, dcol0 + mt * OC: dcol0 + mt * OC + 512], pp[:, 0:512]), "V")
                        else:
                            sb.inc(ve.tensor_copy(
                                dst[:, mt * D + dcol0: mt * D + dcol0 + 512], pp[:, 0:512]), "V")
                        PST["pe"][bank] = sb.c["V"]

            sb.wait(te, "V")
            run_jobs([(wq_h, 0, "fm", qT, 0), (wq_h, 512, "fm", qT, 4 * OC),
                      (wk_h, 0, "fm", kT, 0), (wk_h, 512, "fm", kT, 4 * OC),
                      (wv_h, 0, "tok", v_tok, 0), (wv_h, 512, "tok", v_tok, 512),
                      (wv_h, 1024, "tok", v_tok, 1024), (wv_h, 1536, "tok", v_tok, 1536),
                      (wg_h, 0, "fm", gT, 0), (wg_h, 512, "fm", gT, 4 * OC),
                      (wg_h, 1024, "fm", gT, 8 * OC), (wg_h, 1536, "fm", gT, 12 * OC)])
            sb.wait(te, "DG", "V")
            for kt in range(16):
                te_i = te.matmul(p_projA[0:16, 0:OC], wgk1b[:, kt * 16:(kt + 1) * 16],
                                 xlnT[:, kt * OC: kt * OC + OC],
                                 start=(kt == 0), stop=(kt == 15))
            sb.inc(te_i, "T")
            sb.wait(ve, "T")
            sb.inc(ve.tensor_copy(t1b[0:16, :], p_projA[0:16, 0:OC]), "V")

            if debug and oc == 0:
                sb.wait(sy, "V", "S", "T")
                for nm, buf in (("d_qT", qT), ("d_kT", kT), ("d_v", v_tok), ("d_t1", t1b)):
                    sb.inc(sy.dma_start(out=dbg[nm], in_=buf), "DS", 16)
            for i4 in range(4):
                gsl = gT[:, i4 * 2048:(i4 + 1) * 2048]
                sb.wait(sc, "V")
                sb.inc(sc.activation(sgp, gsl, AF.Sigmoid), "S")
                sb.wait(ve, "S")
                sb.inc(ve.tensor_tensor(gsl, gsl, sgp, ALU.mult), "V")

            if debug and oc == 0:
                sb.wait(sy, "V", "S", "T")
                sb.inc(sy.dma_start(out=dbg["d_gT"], in_=gT), "DS", 16)
            for cc in range(NTT):
                ccol = cc * 128
                for half in range(2):
                    sb.wait(te, "V", "DG")
                    sb.inc(te.matmul(p_proj[:, 0:512], t1b[:, ccol:ccol + 128],
                                     wgk2b[:, half * 512: half * 512 + 512],
                                     start=True, stop=True), "T")
                    sb.wait(ve, "T")
                    sb.inc(ve.tensor_copy(z_tok[:, half * 512: half * 512 + 512], p_proj[:, 0:512]), "V")
                sb.wait(sc, "V")
                sc.activation(sc2, z_tok, AF.Abs)
                sc.activation(sc2, sc2, AF.Exp, scale=-1.0)
                sb.inc(sc.activation(sc2, sc2, AF.Ln, bias=1.0), "S")
                sb.wait(ve, "S")
                ve.tensor_scalar_min(g_tok, z_tok, 0.0)
                sb.inc(ve.tensor_tensor(g_tok, g_tok, sc2, ALU.subtract), "V")
                for j in range(8):
                    sb.wait(te, "V")
                    sb.inc(te.matmul(p_cum, g_tok[:, j * 128:(j + 1) * 128], lt16,
                                     start=True, stop=True), "T")
                    sb.wait(ve, "T")
                    sb.inc(ve.tensor_copy(at_[:, j * 128:(j + 1) * 128], p_cum), "V")
                sb.wait(ve, "V")
                for j in range(8):
                    lastc = at_[:, j * 128 + 127: j * 128 + 128]
                    ve.tensor_scalar(cols[:, 0:1], lastc, -0.5, QSCALE, ALU.mult, ALU.add)
                    sb.inc(ve.tensor_scalar_mul(cols[:, 1:2], lastc, 0.5), "V")
                    sb.wait(sc, "V")
                    sb.inc(sc.activation(e_s, at_[:, j * 128:(j + 1) * 128], AF.Exp,
                                         bias=cols[:, 0:1]), "S")
                    sb.wait(ve, "S")
                    sb.inc(ve.tensor_tensor(qtl[:, j * 128:(j + 1) * 128],
                                            qT[:, j * OC + ccol: j * OC + ccol + 128], e_s, ALU.mult), "V")
                    sb.wait(sc, "V")
                    sb.inc(sc.activation(e_s, at_[:, j * 128:(j + 1) * 128], AF.Exp,
                                         scale=-1.0, bias=cols[:, 1:2]), "S")
                    sb.wait(ve, "S")
                    sb.inc(ve.tensor_tensor(ktl[:, j * 128:(j + 1) * 128],
                                            kT[:, j * OC + ccol: j * OC + ccol + 128], e_s, ALU.mult), "V")
                    sb.wait(sc, "V")
                    sb.inc(sc.activation(cols[:, 2:3], cols[:, 1:2], AF.Exp), "S")
                    sb.inc(sc.activation(cols[:, 3:4], lastc, AF.Exp), "S")
                    sb.wait(ve, "S")
                    sb.inc(ve.tensor_scalar(khT[:, j * 128:(j + 1) * 128],
                                            ktl[:, j * 128:(j + 1) * 128],
                                            cols[:, 2:3], None, ALU.mult), "V")
                    sb.inc(ve.tensor_scalar(stl[:, j * 128:(j + 1) * 128],
                                            s_all[:, j * 128:(j + 1) * 128],
                                            cols[:, 2:3], None, ALU.mult), "V")
                    sb.wait(te, "V")
                    sb.inc(te.transpose(p_tr, khT[:, j * 128:(j + 1) * 128], identb), "T")
                    sb.wait(ve, "T")
                    sb.inc(ve.tensor_copy(kh_tok[:, j * 128:(j + 1) * 128], p_tr), "V")
                    for hh in range(2):
                        h = 2 * j + hh
                        pb = hh * 64
                        sb.wait(te, "V")
                        sb.inc(te.matmul(p_p, ktl[pb:pb + 64, j * 128:(j + 1) * 128],
                                         qtl[pb:pb + 64, j * 128:(j + 1) * 128],
                                         start=True, stop=True), "T")
                        sb.wait(ve, "T")
                        sb.inc(ve.select(pbuf, masku, p_p, zerob), "V")
                        sb.wait(te, "V")
                        te.matmul(p_o, v_tok[:, cc * D + h * 128: cc * D + h * 128 + 128],
                                  pbuf, start=True, stop=False)
                        sb.inc(te.matmul(p_o, stl[pb:pb + 64, j * 128:(j + 1) * 128],
                                         qtl[pb:pb + 64, j * 128:(j + 1) * 128],
                                         start=False, stop=True), "T")
                        sb.wait(ve, "T")
                        sb.inc(ve.tensor_copy(o_all[:, h * 128:(h + 1) * 128], p_o), "V")
                        sb.wait(te, "V")
                        sb.inc(te.matmul(p_ds, kh_tok[:, h * 64: h * 64 + 64],
                                         v_tok[:, cc * D + h * 128: cc * D + h * 128 + 128],
                                         start=True, stop=True), "T")
                        sb.wait(ve, "T")
                        sb.inc(ve.scalar_tensor_tensor(
                            s_all[pb:pb + 64, j * 128:(j + 1) * 128],
                            s_all[pb:pb + 64, j * 128:(j + 1) * 128],
                            cols[pb:pb + 64, 3:4], p_ds, ALU.mult, ALU.add), "V")
                if debug and oc == 0 and cc == 0:
                    sb.wait(sy, "V", "S", "T")
                    for nm, buf in (("d_z", z_tok), ("d_g", g_tok), ("d_at", at_),
                                    ("d_qtl", qtl), ("d_ktl", ktl), ("d_khT", khT),
                                    ("d_oall", o_all), ("d_sall", s_all)):
                        sb.inc(sy.dma_start(out=dbg[nm], in_=buf), "DS", 16)
                if debug and oc == 0 and cc == 3:
                    sb.wait(sy, "V", "S", "T")
                    sb.inc(sy.dma_start(out=dbg["d_oall3"], in_=o_all), "DS", 16)
                sb.wait(sc, "V")
                sb.inc(sc.activation(sgp, o_all, AF.Square), "S")
                for nn in range(4):
                    sb.wait(te, "S", "V")
                    sb.inc(te.matmul(p_projB[0:1, 0:512], onesb, sgp[:, nn * 512:(nn + 1) * 512],
                                     start=True, stop=True), "T")
                    sb.wait(ve, "T")
                    sb.inc(ve.tensor_copy(rstd[:, nn * 512:(nn + 1) * 512], p_projB[0:1, 0:512]), "V")
                sb.wait(ve, "V")
                sb.inc(ve.tensor_scalar(rstd, rstd, 1.0 / 128.0, EPS, ALU.mult, ALU.add), "V")
                sb.wait(sc, "V")
                sc.activation(rstd, rstd, AF.Ln)
                sb.inc(sc.activation(rstd, rstd, AF.Exp, scale=kn05[0:1, 0:1]), "S")
                for nn in range(4):
                    sb.wait(te, "S", "V")
                    sb.inc(te.matmul(p_proj[:, 0:512], onesr, rstd[:, nn * 512:(nn + 1) * 512],
                                     start=True, stop=True), "T")
                    sb.wait(ve, "T")
                    sb.inc(ve.tensor_copy(rb, p_proj[:, 0:512]), "V")
                    o_sl = o_all[:, nn * 512:(nn + 1) * 512]
                    ve.tensor_tensor(o_sl, o_sl, rb, ALU.mult)
                    g_ap = gT.rearrange("p (h t) -> p h t", t=OC)[
                        :, nn * 4:(nn + 1) * 4, ccol:ccol + 128]
                    og_ap = otg.rearrange("p (c h t) -> p c h t", h=16, t=128)[
                        :, cc, nn * 4:(nn + 1) * 4, :]
                    sb.inc(ve.scalar_tensor_tensor(
                        og_ap, o_sl.rearrange("p (h t) -> p h t", t=128),
                        normw[:, 0:1], g_ap, ALU.mult, ALU.mult), "V")

            if debug and oc == 0:
                sb.wait(sy, "V", "S", "T")
                sb.inc(sy.dma_start(out=dbg["d_otg"], in_=otg), "DS", 16)
            sb.wait(te, "V")
            PST["pe"] = [sb.c["V"], sb.c["V"]]
            load_w(wo_h, 0, 0)
            for blk in range(4):
                slot = blk % 2
                if blk + 1 < 4:
                    load_w(wo_h, (blk + 1) * 512, (blk + 1) % 2)
                wb = wbuf.rearrange("p (s kt m) -> p s kt m", s=2, m=512)[:, slot]
                for tt in range(NTT):
                    r0 = t0 + tt * 128
                    bank = PST["pb"]
                    PST["pb"] ^= 1
                    pp = p_projA if bank == 0 else p_projB
                    sb.wait_val(te, "DG", PST["wr"][slot])
                    sb.wait_val(te, "V", PST["pe"][bank])
                    for kt in range(16):
                        te_i = te.matmul(
                            pp[:, 0:512],
                            otg[:, tt * 2048 + kt * 128: tt * 2048 + kt * 128 + 128],
                            wb[:, kt, 0:512],
                            start=(kt == 0), stop=(kt == 15))
                    sb.inc(te_i, "T")
                    PST["wu"][slot] = sb.c["T"]
                    sb.wait(sy, "V", "DS")
                    sb.inc(sy.dma_start(out=xs, in_=x_h[r0:r0 + 128, blk * 512: blk * 512 + 512]), "DS", 16)
                    sb.wait(ve, "T", "DS")
                    sb.inc(ve.tensor_tensor(xs, pp[:, 0:512], xs, ALU.add), "V")
                    PST["pe"][bank] = sb.c["V"]
                    sb.wait(sy, "V")
                    sb.inc(sy.dma_start(out=y_h[r0:r0 + 128, blk * 512: blk * 512 + 512], in_=xs), "DS", 16)


def _prep_l0_attn_inputs(x_core, ins):
    cons = mk_consts()
    return dict(
        x=np.ascontiguousarray(x_core),
        Wq=ins["L0_attn_Wq"], Wk=ins["L0_attn_Wk"], Wv=ins["L0_attn_Wv"],
        Wg=ins["L0_attn_Wg"], Wgk1=ins["L0_attn_Wgk1"],
        Wgk2b=np.ascontiguousarray(np.concatenate(
            [ins["L0_attn_Wgk2"], ins["L0_attn_bgk2"][None, :]], axis=0)),
        normw=np.ascontiguousarray(ins["L0_attn_norm_w"][:, None]),
        Wo=ins["L0_attn_Wo"], **cons)


def emit_l1_attn(nc, sb, E, T, debug=False):
    OC = 512
    n_oc = T // OC
    NTT = OC // 128
    QSCALE = float(np.log(128.0 ** (-0.5)))

    x_h, y_h = E["x"], E["y"]
    wq_h, wf_h, wi_h, wo_h, gw_h = E["Wq"], E["Wf"], E["Wi"], E["Wo"], E["gw"]
    ident_h, mask_h, lt1_h = E["c_ident"], E["c_masku8"], E["c_lt1"]
    with contextlib.ExitStack() as st:
        def sbuf(name, shape, dt):
            return st.enter_context(nc.sbuf_tensor("A1" + name, shape, dt)).ap()

        def psum(name, shape, dt=F32):
            return st.enter_context(nc.psum_tensor("A1" + name, shape, dt)).ap()

        identb = sbuf("identb", [128, 128], BF16)
        masku = sbuf("masku", [128, 128], U8)
        lt1 = sbuf("lt1", [128, 128], F32)
        gw = sbuf("gw_s", [128, 16], F32)
        onesb = sbuf("onesb", [128, 1], BF16)
        onesr = sbuf("onesr", [1, 128], F32)
        zerob = sbuf("zerob", [128, 128], BF16)
        cols = sbuf("cols", [128, 8], F32)
        kn05 = sbuf("kn05", [128, 1], F32)
        x_tok = sbuf("x_tok", [128, D], F32)
        sq_f = sbuf("sq_f", [128, D], BF16)
        sq_s = sbuf("sq_s", [128, D], BF16)
        xlnT = sbuf("xlnT", [128, 16 * OC], BF16)
        qT = sbuf("qT", [128, 16 * OC], BF16)
        sgp = sbuf("sgp", [128, 2048], BF16)
        f_bf = sbuf("f_bf", [128, NTT * D], BF16)
        k_c = sbuf("k_c", [128, D], BF16)
        v_tok = sbuf("v_tok", [128, NTT * D], BF16)
        fz = sbuf("fz", [128, D], F32)
        sc2 = sbuf("sc2", [128, D], F32)
        at_ = sbuf("at", [128, 16 * 128], F32)
        e_s = sbuf("e_s", [128, 128], F32)
        kTc = sbuf("kTc", [128, 16 * 128], BF16)
        qtl = sbuf("qtl", [128, 16 * 128], BF16)
        ktl = sbuf("ktl", [128, 16 * 128], BF16)
        khT = sbuf("khT", [128, 16 * 128], BF16)
        kh_tok = sbuf("kh_tok", [128, 16 * 128], BF16)
        pbuf = sbuf("pbuf", [128, 128], BF16)
        stl = sbuf("stl", [128, 16 * 128], BF16)
        s_all = sbuf("s_all", [128, 16 * 128], F32)
        o_all = sbuf("o_all", [128, 2048], F32)
        rtok = sbuf("rtok", [1, 128], F32)
        rtok4 = sbuf("rtok4", [1, 128], F32)
        rb = sbuf("rb", [128, 128], F32)
        otg = sbuf("otg", [128, NTT * 2048], BF16)
        wbuf = sbuf("wbuf", [128, 2 * 16 * 512], BF16)
        xs = sbuf("xs", [128, 512], F32)

        p_projA = psum("p_projA", [128, 512], F32)
        p_projB = psum("p_projB", [128, 512], F32)
        p_proj = p_projA
        p_tr = psum("p_tr", [128, 128], BF16)
        p_tr2 = psum("p_tr2", [128, 128], BF16)
        p_cum = psum("p_cum", [128, 128], F32)
        p_p = psum("p_p", [128, 128], F32)
        p_o = psum("p_o", [128, 128], F32)
        p_ds = psum("p_ds", [128, 128], F32)

        sy, gp, ve, sc, te = nc.sync, nc.gpsimd, nc.vector, nc.scalar, nc.tensor

        sb.inc(sy.dma_start(out=identb, in_=ident_h), "DS", 16)
        sb.inc(sy.dma_start(out=masku, in_=mask_h), "DS", 16)
        sb.inc(sy.dma_start(out=lt1, in_=lt1_h), "DS", 16)
        sb.inc(sy.dma_start(out=gw, in_=gw_h), "DS", 16)
        sb.inc(ve.memset(kn05, -0.5), "V")
        sb.inc(ve.memset(onesb, 1.0), "V")
        sb.inc(ve.memset(onesr, 1.0), "V")
        sb.inc(ve.memset(zerob, 0.0), "V")
        sb.inc(ve.memset(s_all, 0.0), "V")
        sb.wait(ve, "DS")
        for _e in (te, sc, gp, sy):
            sb.wait(_e, "DS")

        PST = {"tr": [0, 0], "wu": [0, 0], "wr": [0, 0], "pe": [0, 0], "pb": 0}

        def load_w(src_ap, m0, slot):
            sb.wait_val(gp, "T", PST["wu"][slot])
            sb.inc(gp.dma_start(
                out=wbuf.rearrange("p (s kt m) -> p s kt m", s=2, m=512)[:, slot],
                in_=src_ap[:, m0:m0 + 512].rearrange("(kt p) m -> p kt m", p=128)), "DG", 16)
            PST["wr"][slot] = sb.c["DG"]

        for oc in range(n_oc):
            t0 = oc * OC
            for tt in range(NTT):
                r0 = t0 + tt * 128
                sb.wait(sy, "S", "V", "T", "DS")
                sb.inc(sy.dma_start(out=x_tok, in_=x_h[r0:r0 + 128, :]), "DS", 16)
                sb.wait(sc, "DS")
                sb.inc(sc.activation(sq_f, x_tok, AF.Square), "S")
                sb.wait(ve, "DS")
                sb.inc(ve.tensor_reduce(cols[:, 1:2], x_tok, AX.X, ALU.add), "V")
                sb.wait(ve, "S")
                sb.inc(ve.tensor_reduce(cols[:, 0:1], sq_f, AX.X, ALU.add), "V")
                sb.wait(ve, "V")
                ve.tensor_scalar_mul(cols[:, 2:3], cols[:, 1:2], 1.0 / D)
                sb.inc(ve.tensor_scalar(cols[:, 3:4], cols[:, 0:1], 1.0 / D, EPS, ALU.mult, ALU.add), "V")
                sb.wait(ve, "V")
                sb.inc(ve.tensor_tensor(cols[:, 4:5], cols[:, 2:3], cols[:, 2:3], ALU.mult), "V")
                sb.wait(ve, "V")
                sb.inc(ve.tensor_tensor(cols[:, 5:6], cols[:, 3:4], cols[:, 4:5], ALU.subtract), "V")
                sb.wait(sc, "V")
                sc.activation(cols[:, 6:7], cols[:, 5:6], AF.Ln)
                sb.inc(sc.activation(cols[:, 7:8], cols[:, 6:7], AF.Exp, scale=kn05[:, 0:1]), "S")
                sb.wait(ve, "S")
                sb.inc(ve.tensor_scalar(sq_s, x_tok, cols[:, 2:3], cols[:, 7:8],
                                        ALU.subtract, ALU.mult), "V")
                sb.wait(te, "V")
                for j in range(16):
                    pt = p_tr if j % 2 == 0 else p_tr2
                    sb.wait_val(te, "V", PST["tr"][j % 2])
                    sb.inc(te.transpose(pt, sq_s[:, j * 128:(j + 1) * 128], identb), "T")
                    sb.wait(ve, "T")
                    sb.inc(ve.tensor_copy(xlnT[:, j * OC + tt * 128: j * OC + tt * 128 + 128], pt), "V")
                    PST["tr"][j % 2] = sb.c["V"]

            def run_jobs(jobs):
                load_w(*jobs[0][:2], 0)
                for idx, (wsrc, m0, kind, blk) in enumerate(jobs):
                    slot = idx % 2
                    if idx + 1 < len(jobs):
                        load_w(*jobs[idx + 1][:2], (idx + 1) % 2)
                    wb = wbuf.rearrange("p (s kt m) -> p s kt m", s=2, m=512)[:, slot]
                    for mt in range(4):
                        bank = PST["pb"]
                        PST["pb"] ^= 1
                        pp = p_projA if bank == 0 else p_projB
                        sb.wait_val(te, "DG", PST["wr"][slot])
                        sb.wait_val(te, "V", PST["pe"][bank])
                        for kt in range(16):
                            if kind == "q":
                                te_i = te.matmul(pp[:, 0:512],
                                                 wb[:, kt, mt * 128: mt * 128 + 128],
                                                 xlnT[:, kt * OC: kt * OC + 512],
                                                 start=(kt == 0), stop=(kt == 15))
                            else:
                                te_i = te.matmul(pp[:, 0:512],
                                                 xlnT[:, kt * OC + mt * 128: kt * OC + mt * 128 + 128],
                                                 wb[:, kt, 0:512],
                                                 start=(kt == 0), stop=(kt == 15))
                        sb.inc(te_i, "T")
                        PST["wu"][slot] = sb.c["T"]
                        sb.wait(ve, "T")
                        if kind == "q":
                            sb.inc(ve.tensor_copy(
                                qT[:, (blk * 4 + mt) * OC: (blk * 4 + mt) * OC + 512],
                                pp[:, 0:512]), "V")
                        elif kind == "f":
                            f_sl = f_bf[:, mt * D + blk * 512: mt * D + blk * 512 + 512]
                            sb.inc(ve.tensor_copy(f_sl, pp[:, 0:512]), "V")
                        else:
                            sb.inc(ve.tensor_copy(
                                v_tok[:, mt * D + blk * 512: mt * D + blk * 512 + 512],
                                pp[:, 0:512]), "V")
                        PST["pe"][bank] = sb.c["V"]

            sb.wait(te, "V")
            run_jobs([(wq_h, b * 512, "q", b) for b in range(4)]
                     + [(wf_h, b * 512, "f", b) for b in range(4)]
                     + [(wi_h, b * 512, "i", b) for b in range(4)])
            for i4 in range(4):
                qsl = qT[:, i4 * 2048:(i4 + 1) * 2048]
                sb.wait(sc, "V")
                sb.inc(sc.activation(sgp, qsl, AF.Sigmoid), "S")
                sb.wait(ve, "S")
                sb.inc(ve.tensor_tensor(qsl, qsl, sgp, ALU.mult), "V")

            for cc in range(NTT):
                f_c = f_bf[:, cc * D:(cc + 1) * D]
                sb.wait(sc, "V")
                sc.activation(sc2, f_c, AF.Abs)
                sc.activation(sc2, sc2, AF.Exp, scale=-1.0)
                sb.inc(sc.activation(sc2, sc2, AF.Ln, bias=1.0), "S")
                sb.wait(ve, "S")
                ve.tensor_scalar_min(fz, f_c, 0.0)
                sb.inc(ve.tensor_tensor(fz, fz, sc2, ALU.subtract), "V")
                sb.inc(ve.scalar_tensor_tensor(sc2, f_c, 0.0, sc2, ALU.max, ALU.add), "V")
                sb.wait(sc, "V")
                sb.inc(sc.activation(k_c, sc2, AF.Exp, scale=-1.0), "S")
                if debug and oc == 0 and cc == 0:
                    sb.wait(sy, "V", "S", "T")
                    sb.inc(sy.dma_start(out=dbg["d_gl"], in_=fz), "DS", 16)
                for j in range(16):
                    sb.wait(te, "V")
                    sb.inc(te.matmul(p_cum, fz[:, j * 128:(j + 1) * 128], lt1,
                                     start=True, stop=True), "T")
                    sb.wait(ve, "T")
                    sb.inc(ve.tensor_copy(at_[:, j * 128:(j + 1) * 128], p_cum), "V")
                sb.wait(te, "S")
                for j in range(16):
                    pt = p_tr if j % 2 == 0 else p_tr2
                    sb.wait_val(te, "V", PST["tr"][j % 2])
                    sb.inc(te.transpose(pt, k_c[:, j * 128:(j + 1) * 128], identb), "T")
                    sb.wait(ve, "T")
                    sb.inc(ve.tensor_copy(kTc[:, j * 128:(j + 1) * 128], pt), "V")
                    PST["tr"][j % 2] = sb.c["V"]
                sb.wait(ve, "V")
                for j in range(16):
                    h = j
                    lastc = at_[:, j * 128 + 127: j * 128 + 128]
                    ve.tensor_scalar(cols[:, 0:1], lastc, -0.5, QSCALE, ALU.mult, ALU.add)
                    sb.inc(ve.tensor_scalar_mul(cols[:, 1:2], lastc, 0.5), "V")
                    sb.wait(sc, "V")
                    sb.inc(sc.activation(e_s, at_[:, j * 128:(j + 1) * 128], AF.Exp,
                                         bias=cols[:, 0:1]), "S")
                    sb.wait(ve, "S")
                    sb.inc(ve.tensor_tensor(qtl[:, j * 128:(j + 1) * 128],
                                            qT[:, j * OC + cc * 128: j * OC + cc * 128 + 128],
                                            e_s, ALU.mult), "V")
                    sb.wait(sc, "V")
                    sb.inc(sc.activation(e_s, at_[:, j * 128:(j + 1) * 128], AF.Exp,
                                         scale=-1.0, bias=cols[:, 1:2]), "S")
                    sb.wait(ve, "S")
                    sb.inc(ve.tensor_tensor(ktl[:, j * 128:(j + 1) * 128],
                                            kTc[:, j * 128:(j + 1) * 128], e_s, ALU.mult), "V")
                    sb.wait(sc, "V")
                    sb.inc(sc.activation(cols[:, 2:3], cols[:, 1:2], AF.Exp), "S")
                    sb.inc(sc.activation(cols[:, 3:4], lastc, AF.Exp), "S")
                    sb.wait(ve, "S")
                    sb.inc(ve.tensor_scalar(khT[:, j * 128:(j + 1) * 128],
                                            ktl[:, j * 128:(j + 1) * 128],
                                            cols[:, 2:3], None, ALU.mult), "V")
                    sb.inc(ve.tensor_scalar(stl[:, j * 128:(j + 1) * 128],
                                            s_all[:, j * 128:(j + 1) * 128],
                                            cols[:, 2:3], None, ALU.mult), "V")
                    sb.wait(te, "V")
                    sb.inc(te.transpose(p_tr, khT[:, j * 128:(j + 1) * 128], identb), "T")
                    sb.wait(ve, "T")
                    sb.inc(ve.tensor_copy(kh_tok[:, j * 128:(j + 1) * 128], p_tr), "V")
                    sb.wait(te, "V")
                    sb.inc(te.matmul(p_p, ktl[:, j * 128:(j + 1) * 128],
                                     qtl[:, j * 128:(j + 1) * 128],
                                     start=True, stop=True), "T")
                    sb.wait(ve, "T")
                    sb.inc(ve.select(pbuf, masku, p_p, zerob), "V")
                    sb.wait(te, "V")
                    te.matmul(p_o, v_tok[:, cc * D + h * 128: cc * D + h * 128 + 128],
                              pbuf, start=True, stop=False)
                    sb.inc(te.matmul(p_o, stl[:, j * 128:(j + 1) * 128],
                                     qtl[:, j * 128:(j + 1) * 128],
                                     start=False, stop=True), "T")
                    sb.wait(ve, "T")
                    sb.inc(ve.tensor_copy(o_all[:, h * 128:(h + 1) * 128], p_o), "V")
                    sb.wait(te, "V")
                    sb.inc(te.matmul(p_ds, kh_tok[:, j * 128:(j + 1) * 128],
                                     v_tok[:, cc * D + h * 128: cc * D + h * 128 + 128],
                                     start=True, stop=True), "T")
                    sb.wait(ve, "T")
                    sb.inc(ve.scalar_tensor_tensor(
                        s_all[:, j * 128:(j + 1) * 128],
                        s_all[:, j * 128:(j + 1) * 128],
                        cols[:, 3:4], p_ds, ALU.mult, ALU.add), "V")
                if debug and oc == 0 and cc == 0:
                    sb.wait(sy, "V", "S", "T")
                    sb.inc(sy.dma_start(out=dbg["d_at"], in_=at_), "DS", 16)
                    sb.inc(sy.dma_start(out=dbg["d_oall"], in_=o_all), "DS", 16)
                    sb.inc(sy.dma_start(out=dbg["d_sall"], in_=s_all), "DS", 16)
                sb.wait(sc, "V")
                sb.inc(sc.activation(sgp, o_all, AF.Square), "S")
                sb.inc(ve.memset(rtok, 0.0), "V")
                for nn in range(4):
                    sb.wait(te, "S", "V")
                    sb.inc(te.matmul(p_projB[0:1, 0:512], onesb, sgp[:, nn * 512:(nn + 1) * 512],
                                     start=True, stop=True), "T")
                    sb.wait(ve, "T")
                    sb.inc(ve.tensor_reduce(
                        rtok4, p_projB[0:1, 0:512].rearrange("o (h t) -> o t h", h=4), AX.X, ALU.add), "V")
                    sb.wait(ve, "V")
                    sb.inc(ve.tensor_tensor(rtok, rtok, rtok4, ALU.add), "V")
                    sb.wait(ve, "V")
                sb.inc(ve.tensor_scalar(rtok, rtok, 1.0 / D, EPS, ALU.mult, ALU.add), "V")
                sb.wait(sc, "V")
                sc.activation(rtok, rtok, AF.Ln)
                sb.inc(sc.activation(rtok, rtok, AF.Exp, scale=kn05[0:1, 0:1]), "S")
                sb.wait(te, "S", "V")
                sb.inc(te.matmul(p_projA[0:128, 0:128], onesr, rtok, start=True, stop=True), "T")
                sb.wait(ve, "T")
                sb.inc(ve.tensor_copy(rb, p_projA[0:128, 0:128]), "V")
                for h in range(16):
                    sb.inc(ve.scalar_tensor_tensor(
                        otg[:, cc * D + h * 128: cc * D + h * 128 + 128],
                        o_all[:, h * 128:(h + 1) * 128],
                        gw[:, h:h + 1], rb, ALU.mult, ALU.mult), "V")

            if debug and oc == 0:
                sb.wait(sy, "V", "S", "T")
                for nm, buf in (("d_qT", qT), ("d_fb", f_bf), ("d_k", k_tok),
                                ("d_v", v_tok), ("d_otg", otg)):
                    sb.inc(sy.dma_start(out=dbg[nm], in_=buf), "DS", 16)

            sb.wait(te, "V")
            PST["pe"] = [sb.c["V"], sb.c["V"]]
            load_w(wo_h, 0, 0)
            for blk in range(4):
                slot = blk % 2
                if blk + 1 < 4:
                    load_w(wo_h, (blk + 1) * 512, (blk + 1) % 2)
                wb = wbuf.rearrange("p (s kt m) -> p s kt m", s=2, m=512)[:, slot]
                for tt in range(NTT):
                    r0 = t0 + tt * 128
                    bank = PST["pb"]
                    PST["pb"] ^= 1
                    pp = p_projA if bank == 0 else p_projB
                    sb.wait_val(te, "DG", PST["wr"][slot])
                    sb.wait_val(te, "V", PST["pe"][bank])
                    for kt in range(16):
                        te_i = te.matmul(
                            pp[:, 0:512],
                            otg[:, tt * 2048 + kt * 128: tt * 2048 + kt * 128 + 128],
                            wb[:, kt, 0:512],
                            start=(kt == 0), stop=(kt == 15))
                    sb.inc(te_i, "T")
                    PST["wu"][slot] = sb.c["T"]
                    sb.wait(sy, "V", "DS")
                    sb.inc(sy.dma_start(out=xs, in_=x_h[r0:r0 + 128, blk * 512: blk * 512 + 512]), "DS", 16)
                    sb.wait(ve, "T", "DS")
                    sb.inc(ve.tensor_tensor(xs, pp[:, 0:512], xs, ALU.add), "V")
                    PST["pe"][bank] = sb.c["V"]
                    sb.wait(sy, "V")
                    sb.inc(sy.dma_start(out=y_h[r0:r0 + 128, blk * 512: blk * 512 + 512], in_=xs), "DS", 16)


def _prep_l1_attn_inputs(x_core, ins):
    cons = mk_consts()
    return dict(
        x=np.ascontiguousarray(x_core),
        Wq=ins["L1_attn_Wq"], Wf=ins["L1_attn_Wf"], Wi=ins["L1_attn_Wi"],
        Wo=ins["L1_attn_Wo"],
        gw=np.ascontiguousarray(ins["L1_attn_g_weight"].reshape(16, 128).T),
        **cons)


def emit_l0_ffn(nc, sb, E, T, debug=False):
    INTER = 8192

    x_h, y_h = E["x"], E["y"]
    wup_h, wgt_h, wdn_h = E["W_up"], E["W_gate"], E["W_down"]
    suug_h, sudp_h = E["su_ug"], E["su_dp"]
    wsup_h, wsgt_h, wsdp_h = E["ws_up"], E["ws_gt"], E["ws_dp"]
    ident_h, h128_h, h64_h = E["c_ident"], E["c_h128"], E["c_h64"]
    wupc = nc.dram_tensor("wupc", [INTER, D], BF16, kind="Internal").ap()
    wgtc = nc.dram_tensor("wgtc", [INTER, D], BF16, kind="Internal").ap()
    wdnc = nc.dram_tensor("wdnc", [D, INTER], BF16, kind="Internal").ap()
    wupT = nc.dram_tensor("wupT", [D, INTER], BF16,
                          kind="ExternalOutput" if DBG_FFN else "Internal").ap()
    wgtT = nc.dram_tensor("wgtT", [D, INTER], BF16, kind="Internal").ap()
    wh1 = nc.dram_tensor("wh1", [64, 128, D], BF16, kind="Internal").ap()
    dbg_lhs = (nc.dram_tensor("dbg_lhs", [128, 16 * 512], BF16, kind="ExternalOutput").ap()
               if DBG_FFN else None)
    wdns = nc.dram_tensor("wdns", [INTER, D], BF16,
                          kind="ExternalOutput" if DBG_FFN else "Internal").ap()
    x2s = nc.dram_tensor("x2s", [INTER, T], BF16,
                         kind="ExternalOutput" if DBG_FFN else "Internal").ap()
    with contextlib.ExitStack() as st:
        def sbuf(name, shape, dt):
            return st.enter_context(nc.sbuf_tensor("F0" + name, shape, dt)).ap()

        identb = sbuf("identb", [128, 128], BF16)
        h128b = sbuf("h128b", [128, 128], BF16)
        h64b = sbuf("h64b", [64, 64], BF16)
        su_ug = sbuf("su_ug_s", [128, 16], F32)
        su_dp = sbuf("su_dp_s", [64, 128], F32)
        ws_dp = sbuf("ws_dp_s", [128, 16], F32)
        ws_upg = sbuf("ws_upg", [128, 128], F32)
        cols = sbuf("cols", [128, 8], F32)
        kn05 = sbuf("kn05", [128, 1], F32)

        sy, gp, ve, sc, te = nc.sync, nc.gpsimd, nc.vector, nc.scalar, nc.tensor

        sb.inc(sy.dma_start(out=identb, in_=ident_h), "DS", 16)
        sb.inc(sy.dma_start(out=h128b, in_=h128_h), "DS", 16)
        sb.inc(sy.dma_start(out=h64b, in_=h64_h), "DS", 16)
        sb.inc(sy.dma_start(out=su_ug, in_=suug_h), "DS", 16)
        sb.inc(sy.dma_start(out=su_dp, in_=sudp_h), "DS", 16)
        sb.inc(sy.dma_start(out=ws_dp, in_=wsdp_h), "DS", 16)
        sb.inc(sy.dma_start(out=ws_upg[:, 0:64], in_=wsup_h), "DS", 16)
        sb.inc(sy.dma_start(out=ws_upg[:, 64:128], in_=wsgt_h), "DS", 16)
        sb.inc(ve.memset(kn05, -0.5), "V")
        sb.wait(ve, "DS")
        sb.inc(ve.tensor_scalar_mul(su_ug, su_ug, float(2048.0 ** -0.5)), "V")
        sb.inc(ve.tensor_scalar_mul(ws_dp, ws_dp, float(8192.0 ** -0.5)), "V")
        sb.wait(ve, "V")
        for _e in (te, sc, gp, sy):
            sb.wait(_e, "DS")

        with nc.sbuf_tensor("F0cast_s", [128, 2 * INTER], BF16) as cast_g:
            cast_f = cast_g.ap()
            slots = [cast_f[:, 0:INTER], cast_f[:, INTER:2 * INTER]]
            cjobs = []
            for m, src, dstd, nt in (("up", wup_h, wupc, 64), ("gt", wgt_h, wgtc, 64),
                                     ("dn", wdn_h, wdnc, 16)):
                for t_i in range(nt):
                    cjobs.append((m, src, dstd, t_i))
            ld_c = [0, 0]
            sc_c = [0, 0]
            stc = [0, 0]
            for i, (m, src, dstd, t_i) in enumerate(cjobs):
                s = i % 2
                wdt = D if m != "dn" else INTER
                sl = slots[s][:, 0:wdt]
                sb.wait_val(gp, "DS", stc[s])
                sb.inc(gp.dma_start(out=sl, in_=src[t_i * 128:(t_i + 1) * 128, :]), "DG", 16)
                ld_c[s] = sb.c["DG"]
                if m == "up":
                    s_ap = ws_upg[:, t_i: t_i + 1]
                elif m == "gt":
                    s_ap = ws_upg[:, 64 + t_i: 65 + t_i]
                else:
                    s_ap = ws_dp[:, t_i: t_i + 1]
                sb.wait_val(sc, "DG", ld_c[s])
                sb.inc(sc.activation(sl, sl, AF.Copy, scale=s_ap), "S")
                sc_c[s] = sb.c["S"]
                sb.wait_val(sy, "S", sc_c[s])
                sb.inc(sy.dma_start(out=dstd[t_i * 128:(t_i + 1) * 128, :], in_=sl), "DS", 16)
                stc[s] = sb.c["DS"]
            sb.wait(sy, "DG", "DS", "S")
            sb.wait(gp, "DG", "DS", "S")
            tjobs = [(srcd, dstd, ht)
                     for srcd, dstd in ((wupc, wupT), (wgtc, wgtT)) for ht in range(16)]
            tr_c = [0, 0]
            stc = [sb.c["DG"], sb.c["DG"]]
            for i, (srcd, dstd, ht) in enumerate(tjobs):
                s = i % 2
                sb.wait_val(sy, "DG", stc[s])
                sb.inc(sy.dma_start_transpose(
                    slots[s], srcd[:, ht * 128:(ht + 1) * 128]), "DT", 16)
                tr_c[s] = sb.c["DT"]
                sb.wait_val(gp, "DT", tr_c[s])
                sb.inc(gp.dma_start(out=dstd[ht * 128:(ht + 1) * 128, :],
                                    in_=slots[s]), "DG", 16)
                stc[s] = sb.c["DG"]

        sb.fence(nc)
        with (nc.sbuf_tensor("F0f_in", [128, 2 * D], BF16) as f_in_g,
              nc.sbuf_tensor("F0f_h1", [128, 2 * D], BF16) as f_h1_g,
              nc.psum_tensor("F0p_w", [128, 512], F32) as p_w_g,
              nc.psum_tensor("F0p_w2", [128, 512], F32) as p_w2_g,
              nc.psum_tensor("F0p_f", [64, 512], F32) as p_f_g,
              nc.psum_tensor("F0p_f2", [64, 512], F32) as p_f2_g):
            f_in2, f_h12 = f_in_g.ap(), f_h1_g.ap()
            pw = [p_w_g.ap(), p_w2_g.ap()]
            pf = [p_f_g.ap(), p_f2_g.ap()]
            te1_done = [0, 0]
            tr1_c = [0, 0]
            ev1 = [0, 0]
            st1 = [0, 0]

            def tr_in(it):
                s = it % 2
                sb.wait_val(sy, "T", te1_done[s])
                sb.inc(sy.dma_start_transpose(
                    f_in2[:, s * D:(s + 1) * D], wdnc[:, it * 128:(it + 1) * 128]), "DT", 16)
                tr1_c[s] = sb.c["DT"]

            tr_in(0)
            for it in range(64):
                s = it % 2
                if it + 1 < 64:
                    tr_in(it + 1)
                sb.wait_val(te, "DT", tr1_c[s])
                for nn2 in range(4):
                    b = nn2 % 2
                    sb.wait_val(te, "V", ev1[b])
                    sb.inc(te.matmul(pw[b], h128b,
                                     f_in2[:, s * D + nn2 * 512: s * D + nn2 * 512 + 512],
                                     start=True, stop=True), "T")
                    if nn2 == 3:
                        te1_done[s] = sb.c["T"]
                    sb.wait(ve, "T")
                    if nn2 == 0:
                        sb.wait_val(ve, "DS", st1[s])
                    sb.inc(ve.tensor_copy(
                        f_h12[:, s * D + nn2 * 512: s * D + nn2 * 512 + 512], pw[b]), "V")
                    ev1[b] = sb.c["V"]
                sb.wait(sy, "V")
                sb.inc(sy.dma_start(out=wh1[it], in_=f_h12[:, s * D:(s + 1) * D]), "DS", 16)
                st1[s] = sb.c["DS"]
            with (nc.sbuf_tensor("F0f_rd", [64, 2 * 8 * D], BF16) as f_rd_g,
                  nc.sbuf_tensor("F0f_o", [64, 2 * 8 * D], BF16) as f_o_g):
                f_rd2, f_o2 = f_rd_g.ap(), f_o_g.ap()
                te2_done = [0, 0]
                ld2 = [0, 0]
                ev2 = [0, 0]
                st2 = [0, 0]

                def ld_bg(bg):
                    s = bg % 2
                    sb.wait_val(sy, "T", te2_done[s])
                    sb.inc(sy.dma_start(
                        out=f_rd2.rearrange("p (s2 b d) -> p s2 b d", s2=2, b=8)[:, s],
                        in_=wh1[:, bg * 8:(bg + 1) * 8, :]), "DS", 16)
                    ld2[s] = sb.c["DS"]

                sb.wait(sy, "V", "DS")
                ld_bg(0)
                for bg in range(16):
                    s = bg % 2
                    if bg + 1 < 16:
                        ld_bg(bg + 1)
                    sb.wait_val(te, "DS", ld2[s])
                    for bs in range(8):
                        bidx = bg * 8 + bs
                        o0 = s * 8 * D + bs * D
                        for dc in range(4):
                            pb = (bs * 4 + dc) % 2
                            sb.wait_val(te, "V", ev2[pb])
                            sb.inc(te.matmul(pf[pb], h64b,
                                             f_rd2[0:64, o0 + dc * 512: o0 + dc * 512 + 512],
                                             start=True, stop=True), "T")
                            if bs == 7 and dc == 3:
                                te2_done[s] = sb.c["T"]
                            sb.wait(ve, "T")
                            if bs == 0 and dc == 0:
                                sb.wait_val(ve, "DS", st2[s])
                            sb.inc(ve.tensor_scalar(
                                f_o2[0:64, o0 + dc * 512: o0 + dc * 512 + 512],
                                pf[pb], su_dp[:, bidx:bidx + 1], None, ALU.mult), "V")
                            ev2[pb] = sb.c["V"]
                    sb.wait(sy, "V")
                    sb.inc(sy.dma_start(
                        out=wdns.rearrange("(a b) d -> a b d", b=128)[:, bg * 8:(bg + 1) * 8, :],
                        in_=f_o2.rearrange("p (s2 b d) -> p s2 b d", s2=2, b=8)[0:64, s]), "DS", 16)
                    st2[s] = sb.c["DS"]
            if debug:
                sb.wait(sy, "DS")
                sb.inc(sy.dma_start(out=dbg["d_wdns"], in_=wdns[0:128, :]), "DS", 16)

        sb.fence(nc)
        with (nc.sbuf_tensor("F0x1", [128, 16 * T], BF16) as x1_g,
              nc.psum_tensor("F0p_a", [128, 512], F32) as p_a_g,
              nc.psum_tensor("F0p_tr", [128, 128], BF16) as p_tr_g):
            x1, p_a, p_tr = x1_g.ap(), p_a_g.ap(), p_tr_g.ap()
            with (nc.sbuf_tensor("F0x_tok", [128, D], F32) as xtok_g,
                  nc.sbuf_tensor("F0sq_f", [128, D], BF16) as sqf_g,
                  nc.sbuf_tensor("F0sq_s", [128, D], BF16) as sqs_g,
                  nc.sbuf_tensor("F0xsuT", [128, 16 * 512], BF16) as xsuT_g,
                  nc.sbuf_tensor("F0bb", [128, 16 * 256], F32) as bb_g,
                  nc.sbuf_tensor("F0btmp", [128, 8 * 256], F32) as btmp_g):
                x_tok, sq_f, sq_s, xsuT, bb, btmp = (
                    xtok_g.ap(), sqf_g.ap(), sqs_g.ap(), xsuT_g.ap(), bb_g.ap(), btmp_g.ap())
                for oc in range(T // 512):
                    for tt in range(4):
                        r0 = oc * 512 + tt * 128
                        sb.wait(sy, "S", "V", "T", "DS")
                        sb.inc(sy.dma_start(out=x_tok, in_=x_h[r0:r0 + 128, :]), "DS", 16)
                        sb.wait(sc, "DS")
                        sb.inc(sc.activation(sq_f, x_tok, AF.Square), "S")
                        sb.wait(ve, "DS")
                        sb.inc(ve.tensor_reduce(cols[:, 1:2], x_tok, AX.X, ALU.add), "V")
                        sb.wait(ve, "S")
                        sb.inc(ve.tensor_reduce(cols[:, 0:1], sq_f, AX.X, ALU.add), "V")
                        sb.wait(ve, "V")
                        ve.tensor_scalar_mul(cols[:, 2:3], cols[:, 1:2], 1.0 / D)
                        sb.inc(ve.tensor_scalar(cols[:, 3:4], cols[:, 0:1], 1.0 / D, EPS,
                                                ALU.mult, ALU.add), "V")
                        sb.wait(ve, "V")
                        sb.inc(ve.tensor_tensor(cols[:, 4:5], cols[:, 2:3], cols[:, 2:3], ALU.mult), "V")
                        sb.wait(ve, "V")
                        sb.inc(ve.tensor_tensor(cols[:, 5:6], cols[:, 3:4], cols[:, 4:5], ALU.subtract), "V")
                        sb.wait(sc, "V")
                        sc.activation(cols[:, 6:7], cols[:, 5:6], AF.Ln)
                        sb.inc(sc.activation(cols[:, 7:8], cols[:, 6:7], AF.Exp, scale=kn05[:, 0:1]), "S")
                        sb.wait(ve, "S")
                        sb.inc(ve.tensor_scalar(sq_s, x_tok, cols[:, 2:3], cols[:, 7:8],
                                                ALU.subtract, ALU.mult), "V")
                        for j in range(16):
                            sb.wait(te, "V")
                            sb.inc(te.transpose(p_tr, sq_s[:, j * 128:(j + 1) * 128], identb), "T")
                            sb.wait(ve, "T")
                            sb.inc(ve.tensor_scalar(
                                xsuT[:, j * 512 + tt * 128: j * 512 + tt * 128 + 128],
                                p_tr, su_ug[:, j:j + 1], None, ALU.mult), "V")
                    for hh2 in range(2):
                        c0 = hh2 * 256
                        for j in range(16):
                            sb.wait(te, "V")
                            sb.inc(te.matmul(p_a[:, 0:256], h128b,
                                             xsuT[:, j * 512 + c0: j * 512 + c0 + 256],
                                             start=True, stop=True), "T")
                            sb.wait(ve, "T")
                            sb.inc(ve.tensor_copy(bb[:, j * 256:(j + 1) * 256], p_a[:, 0:256]), "V")
                        for s4 in range(4):
                            stp = 1 << s4
                            a_ap = bb.rearrange("p (pr two w) -> p pr two w", two=2, w=stp * 256)[:, :, 0, :]
                            b_ap = bb.rearrange("p (pr two w) -> p pr two w", two=2, w=stp * 256)[:, :, 1, :]
                            t_ap = btmp.rearrange("p (pr w) -> p pr w", w=stp * 256)
                            sb.wait(ve, "V")
                            sb.inc(ve.tensor_tensor(t_ap, a_ap, b_ap, ALU.subtract), "V")
                            sb.wait(ve, "V")
                            if s4 == 3:
                                x1v = x1.rearrange("p (j t) -> p j t", t=T)
                                sb.inc(ve.tensor_tensor(
                                    x1v[:, 0:8, oc * 512 + c0: oc * 512 + c0 + 256], a_ap, b_ap, ALU.add), "V")
                                sb.inc(ve.tensor_copy(
                                    x1v[:, 8:16, oc * 512 + c0: oc * 512 + c0 + 256], t_ap), "V")
                            else:
                                sb.inc(ve.tensor_tensor(a_ap, a_ap, b_ap, ALU.add), "V")
                                sb.wait(ve, "V")
                                sb.inc(ve.tensor_copy(b_ap, t_ap), "V")
                if debug:
                    sb.wait(sy, "V")
                    sb.inc(sy.dma_start(out=dbg["d_x1"], in_=x1[:, 0:16 * 512]), "DS", 16)

            sb.fence(nc)
            with (nc.sbuf_tensor("F0lhs_up", [128, 2 * 16 * 512], BF16) as lup_g,
                  nc.sbuf_tensor("F0lhs_gt", [128, 2 * 16 * 512], BF16) as lgt_g,
                  nc.sbuf_tensor("F0sgb", [128, 2 * 512], BF16) as sgb_g,
                  nc.sbuf_tensor("F0m1b", [128, 2 * 512], F32) as m1b_g,
                  nc.sbuf_tensor("F0x2b", [128, 2 * 512], BF16) as x2b_g,
                  nc.psum_tensor("F0p_up0", [128, 512], F32) as pup0_g,
                  nc.psum_tensor("F0p_up1", [128, 512], F32) as pup1_g,
                  nc.psum_tensor("F0p_gt0", [128, 512], F32) as pgt0_g,
                  nc.psum_tensor("F0p_gt1", [128, 512], F32) as pgt1_g):
                lhs_upb, lhs_gtb, sgb2, m1b2, x2b2 = (lup_g.ap(), lgt_g.ap(), sgb_g.ap(),
                                                      m1b_g.ap(), x2b_g.ap())
                p_ups = [pup0_g.ap(), pup1_g.ap()]
                p_gts = [pgt0_g.ap(), pgt1_g.ap()]
                W2 = {"wu": [0, 0], "wr": [0, 0], "pe": [0, 0], "pb": 0}

                def load_ug(b16, slot):
                    sb.wait_val(gp, "T", W2["wu"][slot])
                    sb.wait(gp, "DT")
                    sb.inc(gp.dma_start(
                        out=lhs_upb.rearrange("p (s kt m) -> p s kt m", s=2, m=512)[:, slot],
                        in_=wupT[:, b16 * 512:(b16 + 1) * 512].rearrange("(kt p) m -> p kt m", p=128)), "DG", 16)
                    sb.inc(gp.dma_start(
                        out=lhs_gtb.rearrange("p (s kt m) -> p s kt m", s=2, m=512)[:, slot],
                        in_=wgtT[:, b16 * 512:(b16 + 1) * 512].rearrange("(kt p) m -> p kt m", p=128)), "DG", 16)
                    W2["wr"][slot] = sb.c["DG"]

                sb.wait(te, "V")
                load_ug(0, 0)
                for b16 in range(16):
                    slot = b16 % 2
                    if b16 + 1 < 16:
                        load_ug(b16 + 1, (b16 + 1) % 2)
                    lu = lhs_upb.rearrange("p (s kt m) -> p s kt m", s=2, m=512)[:, slot]
                    lg2 = lhs_gtb.rearrange("p (s kt m) -> p s kt m", s=2, m=512)[:, slot]
                    for mt in range(4):
                        for tc in range(T // 512):
                            bank = W2["pb"]
                            W2["pb"] ^= 1
                            p_up = p_ups[bank]
                            p_gt = p_gts[bank]
                            sgb = sgb2[:, bank * 512:(bank + 1) * 512]
                            m1b = m1b2[:, bank * 512:(bank + 1) * 512]
                            x2b = x2b2[:, bank * 512:(bank + 1) * 512]
                            sb.wait_val(te, "DG", W2["wr"][slot])
                            sb.wait_val(te, "V", W2["pe"][bank])
                            for kt in range(16):
                                te_i = te.matmul(
                                    p_up,
                                    lu[:, kt, mt * 128: mt * 128 + 128],
                                    x1[:, kt * T + tc * 512: kt * T + tc * 512 + 512],
                                    start=(kt == 0), stop=(kt == 15))
                            sb.inc(te_i, "T")
                            for kt in range(16):
                                te_i = te.matmul(
                                    p_gt,
                                    lg2[:, kt, mt * 128: mt * 128 + 128],
                                    x1[:, kt * T + tc * 512: kt * T + tc * 512 + 512],
                                    start=(kt == 0), stop=(kt == 15))
                            sb.inc(te_i, "T")
                            W2["wu"][slot] = sb.c["T"]
                            sb.wait(sc, "T")
                            sb.inc(sc.activation(sgb, p_gt, AF.Sigmoid), "S")
                            sb.wait(ve, "S")
                            sb.inc(ve.tensor_tensor(m1b, p_gt, sgb, ALU.mult), "V")
                            sb.wait(ve, "V", "DS")
                            sb.inc(ve.tensor_tensor(x2b, m1b, p_up, ALU.mult), "V")
                            W2["pe"][bank] = sb.c["V"]
                            sb.wait(sy, "V")
                            sb.inc(sy.dma_start(
                                out=x2s[b16 * 512 + mt * 128: b16 * 512 + mt * 128 + 128,
                                        tc * 512:(tc + 1) * 512],
                                in_=x2b), "DS", 16)
                            if debug and b16 == 0 and mt == 0 and tc == 0:
                                sb.wait(sy, "V", "T")
                                sb.inc(sy.dma_start(out=dbg["d_x2"], in_=x2b), "DS", 16)
                                sb.inc(sy.dma_start(out=dbg["d_up"], in_=m1b), "DS", 16)

        sb.fence(nc)
        with (nc.sbuf_tensor("F0x2sl", [128, 2 * 64 * 512], BF16) as x2sl_g,
              nc.sbuf_tensor("F0wdnb", [128, 4 * 1024], BF16) as wdnb_g,
              nc.sbuf_tensor("F0xsa", [128, 4 * 1024], F32) as xsa_g,
              contextlib.ExitStack() as st3):
            x2sl2, wdnb, xsa = x2sl_g.ap(), wdnb_g.ap(), xsa_g.ap()
            p_d = [st3.enter_context(nc.psum_tensor(f"F0p_d{i}", [128, 512], F32)).ap()
                   for i in range(8)]
            W3 = {"xu": [0, 0], "xr": [0, 0], "wu": [0, 0, 0, 0],
                  "wr": [("DG", 0)] * 4, "ev": 0, "xsu": 0}

            def load_x2(tg, slot):
                sb.wait_val(sy, "T", W3["xu"][slot])
                sb.wait(sy, "DS")
                sb.inc(sy.dma_start(
                    out=x2sl2.rearrange("p (s kt t) -> p s kt t", s=2, t=512)[:, slot],
                    in_=x2s[:, tg * 512:(tg + 1) * 512].rearrange("(kt p) t -> p kt t", p=128)), "DS", 16)
                W3["xr"][slot] = sb.c["DS"]

            def load_wdn(kt, half, slot):
                q, semn = (gp, "DG") if kt % 2 == 0 else (sy, "DS")
                sb.wait_val(q, "T", W3["wu"][slot])
                sb.inc(q.dma_start(
                    out=wdnb[:, slot * 1024: slot * 1024 + 1024],
                    in_=wdns[kt * 128:(kt + 1) * 128,
                             half * 1024: half * 1024 + 1024]), semn, 16)
                W3["wr"][slot] = (semn, sb.c[semn])

            load_x2(0, 0)
            for tg in range(T // 512):
                xslot = tg % 2
                if tg + 1 < T // 512:
                    load_x2(tg + 1, (tg + 1) % 2)
                x2sl = x2sl2.rearrange("p (s kt t) -> p s kt t", s=2, t=512)[:, xslot]
                sb.wait_val(te, "DS", W3["xr"][xslot])
                for half in range(2):
                    sb.wait_val(sy, "DS", W3["xsu"])
                    sb.inc(sy.dma_start(
                        out=xsa.rearrange("p (sub c) -> p sub c", c=1024),
                        in_=x_h[tg * 512:(tg + 1) * 512, half * 1024: half * 1024 + 1024]
                        .rearrange("(sub p) c -> p sub c", p=128)), "DS", 16)
                    xsa_ready = sb.c["DS"]
                    for kpre in range(3):
                        load_wdn(kpre, half, kpre % 4)
                    for kt in range(64):
                        wslot = kt % 4
                        if kt + 3 < 64:
                            load_wdn(kt + 3, half, (kt + 3) % 4)
                        wsem, wcnt = W3["wr"][wslot]
                        sb.wait_val(te, wsem, wcnt)
                        if kt == 0:
                            sb.wait_val(te, "V", W3["ev"])
                        for sub in range(4):
                            for dc2 in range(2):
                                te_i = te.matmul(
                                    p_d[sub * 2 + dc2],
                                    x2sl[:, kt, sub * 128: sub * 128 + 128],
                                    wdnb[:, wslot * 1024 + dc2 * 512: wslot * 1024 + dc2 * 512 + 512],
                                    start=(kt == 0), stop=(kt == 63))
                        sb.inc(te_i, "T")
                        W3["wu"][wslot] = sb.c["T"]
                    W3["xu"][xslot] = sb.c["T"]
                    sb.wait(ve, "T")
                    sb.wait_val(ve, "DS", xsa_ready)
                    for sub in range(4):
                        for dc2 in range(2):
                            xsl = xsa.rearrange("p (sub c) -> p sub c", c=1024)[
                                :, sub, dc2 * 512: dc2 * 512 + 512]
                            ve_i = ve.tensor_tensor(xsl, p_d[sub * 2 + dc2], xsl, ALU.add)
                    sb.inc(ve_i, "V")
                    W3["ev"] = sb.c["V"]
                    sb.wait(sy, "V")
                    sb.inc(sy.dma_start(
                        out=y_h[tg * 512:(tg + 1) * 512, half * 1024: half * 1024 + 1024]
                        .rearrange("(sub p) c -> p sub c", p=128),
                        in_=xsa.rearrange("p (sub c) -> p sub c", c=1024)), "DS", 16)
                    W3["xsu"] = sb.c["DS"]


def _prep_l0_ffn_inputs(x_core, ins):
    cons = mk_consts()
    h128 = np.asarray(algebra_butterfly_eye(128), dtype=ml_dtypes.bfloat16)
    h64 = np.asarray(algebra_butterfly_eye(64), dtype=ml_dtypes.bfloat16)
    return dict(
        x=np.ascontiguousarray(x_core),
        W_up=ins["L0_ffn_W_up"], W_gate=ins["L0_ffn_W_gate"], W_down=ins["L0_ffn_W_down"],
        su_ug=np.ascontiguousarray(ins["L0_ffn_SU_ug"].reshape(16, 128).T),
        su_dp=np.ascontiguousarray(ins["L0_ffn_SU_dp"].reshape(64, 128)),
        ws_up=np.ascontiguousarray(ins["L0_ffn_Wscale_ug"][:8192].reshape(64, 128).T),
        ws_gt=np.ascontiguousarray(ins["L0_ffn_Wscale_ug"][8192:].reshape(64, 128).T),
        ws_dp=np.ascontiguousarray(ins["L0_ffn_Wscale_dp"].reshape(16, 128).T),
        c_ident=cons["c_ident"], c_h128=h128, c_h64=h64)


def algebra_butterfly_eye(n):
    x = np.eye(n, dtype=np.float32)
    h = 1
    while h < n:
        x = x.reshape(-1, n // (2 * h), 2, h)
        a, b = x[:, :, 0, :].copy(), x[:, :, 1, :].copy()
        x = np.stack([a + b, a - b], axis=2).reshape(-1, n)
        h *= 2
    return x


def emit_l1_ffn(nc, sb, E, T, debug=False):
    OC = 512
    n_oc = T // OC

    x_h, y8_h = E["x"], E["y8"]
    w1_h, w3_h, w2_h, wr_h, msk_h = E["w1"], E["w3"], E["w2"], E["wr"], E["opmask"]
    ident_h = E["c_ident"]
    w1c = nc.dram_tensor("w1c", [D, D], BF16, kind="Internal").ap()
    w3c = nc.dram_tensor("w3c", [D, D], BF16, kind="Internal").ap()
    w2c = nc.dram_tensor("w2c", [D, D], BF16, kind="Internal").ap()
    w1T = nc.dram_tensor("w1T", [D, D], BF16, kind="Internal").ap()
    w3T = nc.dram_tensor("w3T", [D, D], BF16, kind="Internal").ap()
    w2T = nc.dram_tensor("w2T", [D, D], BF16, kind="Internal").ap()
    cc_in = nc.dram_tensor("cc_in", [8, D], F32, kind="Internal")
    cc_out = nc.dram_tensor("cc_out", [8, D], F32, kind="Internal")
    with contextlib.ExitStack() as st:
        def sbuf(name, shape, dt):
            return st.enter_context(nc.sbuf_tensor("F1" + name, shape, dt)).ap()

        def psum(name, shape, dt=F32):
            return st.enter_context(nc.psum_tensor("F1" + name, shape, dt)).ap()

        identb = sbuf("identb", [128, 128], BF16)
        wrT = sbuf("wrT", [128, 16 * 8], BF16)
        opmask = sbuf("opmask_s", [8, 1], F32)
        cols = sbuf("cols", [128, 8], F32)
        kn05 = sbuf("kn05", [128, 1], F32)
        x_tok = sbuf("x_tok", [128, D], F32)
        sq_f = sbuf("sq_f", [128, D], F32)
        sq_s = sbuf("sq_s", [128, D], BF16)
        xlnT = sbuf("xlnT", [128, 16 * OC], BF16)
        h1T = sbuf("h1T", [128, 16 * OC], BF16)
        h2T = sbuf("h2T", [128, 16 * OC], BF16)
        h3b = sbuf("h3b", [128, 4 * D], BF16)
        wbuf = sbuf("wbuf", [128, 2 * 16 * 512], BF16)
        sgb = sbuf("sgb", [128, 2 * 512], BF16)
        lg = sbuf("lg", [128, 8], F32)
        lcols = sbuf("lcols", [128, 8], F32)
        mb = sbuf("mb", [128, 4 * 8], BF16)
        op_sb = sbuf("op_sb", [8, D], F32)
        x8 = sbuf("x8", [8, D], F32)

        p_projA = psum("p_projA", [128, 512], F32)
        p_projB = psum("p_projB", [128, 512], F32)
        p_proj = p_projA
        p_tr = psum("p_tr", [128, 128], BF16)
        p_lg = psum("p_lg", [128, 8], F32)
        p_op = [psum(f"p_op{i}", [8, 512], F32) for i in range(4)]

        sy, gp, ve, sc, te = nc.sync, nc.gpsimd, nc.vector, nc.scalar, nc.tensor

        sb.inc(sy.dma_start(out=identb, in_=ident_h), "DS", 16)
        sb.inc(sy.dma_start(out=opmask, in_=msk_h), "DS", 16)
        sb.inc(ve.memset(kn05, -0.5), "V")
        for _e in (te, sc, gp, ve):
            sb.wait(_e, "DS")

        with nc.sbuf_tensor("F1cast_s", [128, 2 * D], BF16) as cast_g:
            cast_f = cast_g.ap()
            csl = [cast_f[:, 0:D], cast_f[:, D:2 * D]]
            cast_s = csl[0]
            cjobs = [(src, dstc, t_i)
                     for src, dstc in ((w1_h, w1c), (w3_h, w3c), (w2_h, w2c))
                     for t_i in range(16)]
            ldc = [0, 0]
            stc = [0, 0]
            for i, (src, dstc, t_i) in enumerate(cjobs):
                s = i % 2
                sb.wait_val(gp, "DS", stc[s])
                sb.inc(gp.dma_start(out=csl[s],
                                    in_=src[t_i * 128:(t_i + 1) * 128, :]), "DG", 16)
                ldc[s] = sb.c["DG"]
                sb.wait_val(sy, "DG", ldc[s])
                sb.inc(sy.dma_start(out=dstc[t_i * 128:(t_i + 1) * 128, :],
                                    in_=csl[s]), "DS", 16)
                stc[s] = sb.c["DS"]
            sb.wait(sy, "DG", "DS")
            sb.wait(gp, "DG", "DS")
            t2jobs = [(srcc, dstT, ht)
                      for srcc, dstT in ((w1c, w1T), (w3c, w3T), (w2c, w2T))
                      for ht in range(16)]
            trc = [0, 0]
            stc = [sb.c["DG"], sb.c["DG"]]
            for i, (srcc, dstT, ht) in enumerate(t2jobs):
                s = i % 2
                sb.wait_val(sy, "DG", stc[s])
                sb.inc(sy.dma_start_transpose(
                    csl[s], srcc[:, ht * 128:(ht + 1) * 128]), "DT", 16)
                trc[s] = sb.c["DT"]
                sb.wait_val(gp, "DT", trc[s])
                sb.inc(gp.dma_start(out=dstT[ht * 128:(ht + 1) * 128, :],
                                    in_=csl[s]), "DG", 16)
                stc[s] = sb.c["DG"]
            with nc.sbuf_tensor("F1wr8", [8, D], BF16) as wr8_g:
                wr8 = wr8_g.ap()
                sb.wait(gp, "DT")
                sb.inc(gp.dma_start(out=wr8, in_=wr_h), "DG", 16)
                for kt in range(16):
                    sb.wait(te, "DG", "V")
                    sb.inc(te.transpose(p_tr[0:128, 0:8], wr8[:, kt * 128:(kt + 1) * 128],
                                        identb[0:8, 0:8]), "T")
                    sb.wait(ve, "T")
                    sb.inc(ve.tensor_copy(wrT[:, kt * 8:(kt + 1) * 8], p_tr[0:128, 0:8]), "V")

        sb.fence(nc)
        PSF = {"wu": [0, 0], "wr": [0, 0], "pe": [0, 0], "pb": 0}
        for oc in range(n_oc):
            t0 = oc * OC
            for tt in range(4):
                r0 = t0 + tt * 128
                sb.wait(sy, "S", "V", "T", "DS")
                sb.inc(sy.dma_start(out=x_tok, in_=x_h[r0:r0 + 128, :]), "DS", 16)
                sb.wait(sc, "DS")
                sb.inc(sc.activation(sq_f, x_tok, AF.Square), "S")
                sb.wait(ve, "DS")
                sb.inc(ve.tensor_reduce(cols[:, 1:2], x_tok, AX.X, ALU.add), "V")
                sb.wait(ve, "S")
                sb.inc(ve.tensor_reduce(cols[:, 0:1], sq_f, AX.X, ALU.add), "V")
                sb.wait(ve, "V")
                ve.tensor_scalar_mul(cols[:, 2:3], cols[:, 1:2], 1.0 / D)
                sb.inc(ve.tensor_scalar(cols[:, 3:4], cols[:, 0:1], 1.0 / D, EPS, ALU.mult, ALU.add), "V")
                sb.wait(ve, "V")
                sb.inc(ve.tensor_tensor(cols[:, 4:5], cols[:, 2:3], cols[:, 2:3], ALU.mult), "V")
                sb.wait(ve, "V")
                sb.inc(ve.tensor_tensor(cols[:, 5:6], cols[:, 3:4], cols[:, 4:5], ALU.subtract), "V")
                sb.wait(sc, "V")
                sc.activation(cols[:, 6:7], cols[:, 5:6], AF.Ln)
                sb.inc(sc.activation(cols[:, 7:8], cols[:, 6:7], AF.Exp, scale=kn05[:, 0:1]), "S")
                sb.wait(ve, "S")
                sb.inc(ve.tensor_scalar(sq_s, x_tok, cols[:, 2:3], cols[:, 7:8],
                                        ALU.subtract, ALU.mult), "V")
                for j in range(16):
                    sb.wait(te, "V")
                    sb.inc(te.transpose(p_tr, sq_s[:, j * 128:(j + 1) * 128], identb), "T")
                    sb.wait(ve, "T")
                    sb.inc(ve.tensor_copy(xlnT[:, j * OC + tt * 128: j * OC + tt * 128 + 128], p_tr), "V")

            for tt in range(4):
                sb.wait(te, "V")
                for kt in range(16):
                    te_i = te.matmul(p_lg, xlnT[:, kt * OC + tt * 128: kt * OC + tt * 128 + 128],
                                     wrT[:, kt * 8:(kt + 1) * 8],
                                     start=(kt == 0), stop=(kt == 15))
                sb.inc(te_i, "T")
                sb.wait(ve, "T")
                sb.inc(ve.tensor_copy(lg, p_lg), "V")
                sb.wait(ve, "V")
                sb.inc(ve.tensor_reduce(lcols[:, 0:1], lg, AX.X, ALU.max), "V")
                sb.wait(ve, "V")
                sb.inc(ve.tensor_scalar(lcols[:, 2:3], lcols[:, 0:1], 1.0, None, ALU.mult), "V")
                sb.wait(ve, "V")
                sb.inc(ve.tensor_scalar(sq_f[:, 0:8], lg, lcols[:, 0:1], None, ALU.is_ge), "V")
                sb.wait(ve, "V")
                sb.inc(ve.scalar_tensor_tensor(sq_f[:, 8:16], sq_f[:, 0:8], -1.0e30, lg,
                                               ALU.mult, ALU.add), "V")
                sb.wait(ve, "V")
                sb.inc(ve.tensor_reduce(lcols[:, 1:2], sq_f[:, 8:16], AX.X, ALU.max), "V")
                sb.wait(ve, "V")
                sb.inc(ve.tensor_scalar(sq_f[:, 16:24], sq_f[:, 8:16], lcols[:, 1:2], None, ALU.is_ge), "V")
                sb.inc(ve.tensor_tensor(lcols[:, 3:4], lcols[:, 0:1], lcols[:, 1:2], ALU.subtract), "V")
                sb.wait(sc, "V")
                sb.inc(sc.activation(lcols[:, 4:5], lcols[:, 3:4], AF.Sigmoid), "S")
                sb.wait(ve, "S")
                sb.inc(ve.tensor_scalar(lcols[:, 5:6], lcols[:, 4:5], -1.0, 1.0, ALU.mult, ALU.add), "V")
                sb.wait(ve, "V")
                sb.inc(ve.tensor_scalar(sq_f[:, 24:32], sq_f[:, 0:8], lcols[:, 4:5], None, ALU.mult), "V")
                sb.wait(ve, "V")
                sb.inc(ve.scalar_tensor_tensor(mb[:, tt * 8:(tt + 1) * 8], sq_f[:, 16:24],
                                               lcols[:, 5:6], sq_f[:, 24:32], ALU.mult, ALU.add), "V")
                if debug and oc == 0 and tt == 0:
                    sb.wait(sy, "V", "S", "T")
                    sb.inc(sy.dma_start(out=dbg["d_lg"], in_=lg), "DS", 16)
                    sb.inc(sy.dma_start(out=dbg["d_M"], in_=mb[:, 0:8]), "DS", 16)

            def load_wl(wsrc, blk, slot):
                sb.wait_val(gp, "T", PSF["wu"][slot])
                sb.wait(gp, "DT")
                sb.inc(gp.dma_start(
                    out=wbuf.rearrange("p (s kt m) -> p s kt m", s=2, m=512)[:, slot],
                    in_=wsrc[:, blk * 512:(blk + 1) * 512].rearrange("(kt p) m -> p kt m", p=128)), "DG", 16)
                PSF["wr"][slot] = sb.c["DG"]

            def hproj(srcT, dstT_buf, wsrc, sig):
                sb.wait(te, "V")
                load_wl(wsrc, 0, 0)
                for blk in range(4):
                    slot = blk % 2
                    if blk + 1 < 4:
                        load_wl(wsrc, blk + 1, (blk + 1) % 2)
                    wb = wbuf.rearrange("p (s kt m) -> p s kt m", s=2, m=512)[:, slot]
                    for mt in range(4):
                        bank = PSF["pb"]
                        PSF["pb"] ^= 1
                        pp = p_projA if bank == 0 else p_projB
                        sb.wait_val(te, "DG", PSF["wr"][slot])
                        sb.wait_val(te, "V", PSF["pe"][bank])
                        for kt in range(16):
                            if sig:
                                te_i = te.matmul(pp[:, 0:512],
                                                 wb[:, kt, mt * 128: mt * 128 + 128],
                                                 srcT[:, kt * OC: kt * OC + 512],
                                                 start=(kt == 0), stop=(kt == 15))
                            else:
                                te_i = te.matmul(pp[:, 0:512],
                                                 srcT[:, kt * OC + mt * 128: kt * OC + mt * 128 + 128],
                                                 wb[:, kt, 0:512],
                                                 start=(kt == 0), stop=(kt == 15))
                        sb.inc(te_i, "T")
                        PSF["wu"][slot] = sb.c["T"]
                        if sig:
                            sb.wait(sc, "T")
                            sgbk = sgb[:, bank * 512: bank * 512 + 512]
                            sb.inc(sc.activation(sgbk, pp[:, 0:512], AF.Sigmoid), "S")
                            sb.wait(ve, "S")
                            sb.inc(ve.tensor_tensor(
                                dstT_buf[:, (blk * 4 + mt) * OC: (blk * 4 + mt) * OC + 512],
                                pp[:, 0:512], sgbk, ALU.mult), "V")
                        else:
                            sb.wait(ve, "T")
                            sb.inc(ve.tensor_copy(
                                dstT_buf[:, mt * D + blk * 512: mt * D + blk * 512 + 512],
                                pp[:, 0:512]), "V")
                        PSF["pe"][bank] = sb.c["V"]

            hproj(xlnT, h1T, w1T, True)
            hproj(h1T, h2T, w3T, True)
            hproj(h2T, h3b, w2T, False)
            for tt in range(4):
                sb.wait(te, "V")
                for dc in range(4):
                    te_i = te.matmul(p_op[dc], mb[:, tt * 8:(tt + 1) * 8],
                                     h3b[:, tt * D + dc * 512: tt * D + dc * 512 + 512],
                                     start=(oc == 0 and tt == 0),
                                     stop=(oc == n_oc - 1 and tt == 3))
                sb.inc(te_i, "T")
            if debug and oc == 0:
                sb.wait(sy, "V", "S", "T")
                sb.inc(sy.dma_start(out=dbg["d_h1"], in_=h1T[:, 0:512]), "DS", 16)
                sb.inc(sy.dma_start(out=dbg["d_h3"], in_=h3b[:, 0:2048]), "DS", 16)

        sb.wait(ve, "T")
        for dc in range(4):
            sb.inc(ve.tensor_copy(op_sb[:, dc * 512:(dc + 1) * 512], p_op[dc]), "V")
        sb.wait(gp, "V")
        sb.inc(gp.dma_start(out=cc_in.ap(), in_=op_sb), "DG", 16)
        sb.wait(gp, "DG")
        gp.collective_compute("AllReduce", ALU.add, replica_groups=[list(range(8))],
                              ins=[cc_in.ap().opt()], outs=[cc_out.ap().opt()]).then_inc(sb.h["CC"], 1)
        sb.c["CC"] += 1
        sb.wait(gp, "CC")
        sb.inc(gp.dma_start(out=op_sb, in_=cc_out.ap()), "DG", 16)
        sb.wait(sy, "DS")
        sb.inc(sy.dma_start(out=x8, in_=x_h[0:8, :]), "DS", 16)
        sb.wait(ve, "DG", "DS")
        sb.inc(ve.scalar_tensor_tensor(x8, op_sb, opmask[:, 0:1], x8, ALU.mult, ALU.add), "V")
        if debug:
            sb.wait(sy, "V")
            sb.inc(sy.dma_start(out=dbg["d_op"], in_=op_sb), "DS", 16)
        sb.wait(sy, "V")
        sb.inc(sy.dma_start(out=y8_h, in_=x8), "DS", 16)


def _prep_l1_ffn_inputs(x_core, ins, core_id):
    cons = mk_consts()
    return dict(
        x=np.ascontiguousarray(x_core),
        w1=ins["L1_ffn_w1"], w3=ins["L1_ffn_w3"], w2=ins["L1_ffn_w2"],
        wr=ins["L1_ffn_wr"],
        opmask=np.full((8, 1), 1.0 if core_id == 0 else 0.0, np.float32),
        c_ident=cons["c_ident"])


_BUILD_CACHE = {}


def build_fused(T=4096, dbg_mid=False, parts=(0, 1, 2, 3)):
    nc = bass.Bass("TRN2", target_bir_lowering=False)

    def inp(name, shape, dt=F32):
        return nc.dram_tensor(name, shape, dt, kind="ExternalInput").ap()

    x0 = inp("L0_attn_x", [T, D])
    out = nc.dram_tensor("out", [T, D], F32, kind="ExternalOutput").ap()
    y8 = nc.dram_tensor("y8", [8, D], F32, kind="ExternalOutput").ap()
    mk = "ExternalOutput" if dbg_mid else "Internal"
    x1s = nc.dram_tensor("x1s", [T, D], F32, kind=mk).ap()
    x2s_r = nc.dram_tensor("x2s_r", [T, D], F32, kind=mk).ap()

    cons = dict(c_ident=inp("c_ident", [128, 128], BF16),
                c_masku8=inp("c_masku8", [128, 128], U8),
                c_lt1=inp("c_lt1", [128, 128], F32))

    with contextlib.ExitStack() as st:
        sb = SB(nc, st, ["T", "V", "S", "DS", "DG", "DT", "CC"])
        if 0 in parts:
            emit_l0_attn(nc, sb, dict(
            x=x0, y=x1s, Wq=inp("L0_attn_Wq", [D, 1024]), Wk=inp("L0_attn_Wk", [D, 1024]),
            Wv=inp("L0_attn_Wv", [D, D]), Wg=inp("L0_attn_Wg", [D, D]),
            Wgk1=inp("L0_attn_Wgk1", [D, 16]), Wgk2b=inp("p_wgk2b", [17, 1024]),
            normw=inp("p_normw", [128, 1]), Wo=inp("L0_attn_Wo", [D, D]), **cons), T)
        if 1 in parts:
            sb.fence(nc)
            emit_l0_ffn(nc, sb, dict(
            x=x1s, y=x2s_r, W_up=inp("L0_ffn_W_up", [8192, D]),
            W_gate=inp("L0_ffn_W_gate", [8192, D]), W_down=inp("L0_ffn_W_down", [D, 8192]),
            su_ug=inp("p_suug", [128, 16]), su_dp=inp("p_sudp", [64, 128]),
            ws_up=inp("p_wsup", [128, 64]), ws_gt=inp("p_wsgt", [128, 64]),
            ws_dp=inp("p_wsdp", [128, 16]), c_ident=cons["c_ident"],
            c_h128=inp("c_h128", [128, 128], BF16), c_h64=inp("c_h64", [64, 64], BF16)), T)
        if 2 in parts:
            sb.fence(nc)
            emit_l1_attn(nc, sb, dict(
            x=x2s_r, y=out, Wq=inp("L1_attn_Wq", [D, D]), Wf=inp("L1_attn_Wf", [D, D]),
            Wi=inp("L1_attn_Wi", [D, D]), Wo=inp("L1_attn_Wo", [D, D]),
            gw=inp("p_gw", [128, 16]), **cons), T)
        if 3 in parts:
            sb.fence(nc)
            emit_l1_ffn(nc, sb, dict(
            x=out, y8=y8, w1=inp("L1_ffn_w1", [D, D]), w3=inp("L1_ffn_w3", [D, D]),
            w2=inp("L1_ffn_w2", [D, D]), wr=inp("L1_ffn_wr", [8, D]),
            opmask=inp("p_opmask", [8, 1]), c_ident=cons["c_ident"]), T)
        nc.sync.wait_ge(sb.h["DS"], sb.c["DS"])
        if sb.c["DG"]:
            nc.sync.wait_ge(sb.h["DG"], sb.c["DG"])
        if sb.c["DT"]:
            nc.sync.wait_ge(sb.h["DT"], sb.c["DT"])
    return nc


def _prep_fused_inputs(x_core, ins, core_id):
    cons = mk_consts()
    h128 = np.asarray(algebra_butterfly_eye(128), dtype=ml_dtypes.bfloat16)
    h64 = np.asarray(algebra_butterfly_eye(64), dtype=ml_dtypes.bfloat16)
    d = {k: ins[k] for k in (
        "L0_attn_Wq", "L0_attn_Wk", "L0_attn_Wv", "L0_attn_Wg", "L0_attn_Wgk1",
        "L0_attn_Wo", "L0_ffn_W_up", "L0_ffn_W_gate", "L0_ffn_W_down",
        "L1_attn_Wq", "L1_attn_Wf", "L1_attn_Wi", "L1_attn_Wo",
        "L1_ffn_w1", "L1_ffn_w3", "L1_ffn_w2", "L1_ffn_wr")}
    d.update(
        L0_attn_x=np.ascontiguousarray(x_core),
        p_wgk2b=np.ascontiguousarray(np.concatenate(
            [ins["L0_attn_Wgk2"], ins["L0_attn_bgk2"][None, :]], axis=0)),
        p_normw=np.ascontiguousarray(ins["L0_attn_norm_w"][:, None]),
        p_suug=np.ascontiguousarray(ins["L0_ffn_SU_ug"].reshape(16, 128).T),
        p_sudp=np.ascontiguousarray(ins["L0_ffn_SU_dp"].reshape(64, 128)),
        p_wsup=np.ascontiguousarray(ins["L0_ffn_Wscale_ug"][:8192].reshape(64, 128).T),
        p_wsgt=np.ascontiguousarray(ins["L0_ffn_Wscale_ug"][8192:].reshape(64, 128).T),
        p_wsdp=np.ascontiguousarray(ins["L0_ffn_Wscale_dp"].reshape(16, 128).T),
        p_gw=np.ascontiguousarray(ins["L1_attn_g_weight"].reshape(16, 128).T),
        p_opmask=np.full((8, 1), 1.0 if core_id == 0 else 0.0, np.float32),
        c_h128=h128, c_h64=h64, **cons)
    return d


def kernel(**inputs):
    x = np.ascontiguousarray(np.asarray(inputs["L0_attn_x"], np.float32))
    B, T, _ = x.shape
    assert B == NCORES
    key = ("fused", T)
    if key not in _BUILD_CACHE:
        _BUILD_CACHE[key] = build_fused(T=T)
    nc = _BUILD_CACHE[key]
    in_maps = [_prep_fused_inputs(x[c], inputs, c) for c in range(NCORES)]
    r = run_bass_kernel_spmd(nc, in_maps, core_ids=list(range(NCORES)))
    out = np.stack([np.asarray(r.results[c]["out"], np.float32) for c in range(NCORES)], axis=0)
    out[0, 0:8, :] = np.asarray(r.results[0]["y8"], np.float32)
    return out
